# version 23
# baseline (speedup 1.0000x reference)
import math
import numpy as np
import ml_dtypes
import concourse.bass as bass
import concourse.mybir as mybir
from concourse.bass_utils import run_bass_kernel_spmd

F32 = mybir.dt.float32
BF16 = mybir.dt.bfloat16
I32 = mybir.dt.int32
U32 = mybir.dt.uint32
AF = mybir.ActivationFunctionType
OP = mybir.AluOpType
AX = mybir.AxisListType

ENGINES = ("sync", "tensor", "vector", "scalar", "gpsimd")


class Sem:
    __slots__ = ("name", "count")

    def __init__(self, name):
        self.name = name
        self.count = 0


class Plan:

    def __init__(self):
        self.ops = {e: [] for e in ENGINES}
        self.sems = {}
        self._last_wait = {e: {} for e in ENGINES}
        self.targets = {}

    def sem(self, name) -> Sem:
        if name not in self.sems:
            self.sems[name] = Sem(name)
        return self.sems[name]

    def _resolve_waits(self, eng, wait):
        wl = []
        for w in wait:
            s, v = (w if isinstance(w, tuple) else (w, w.count))
            if v <= 0:
                continue
            if v > self._last_wait[eng].get(s.name, -1):
                self._last_wait[eng][s.name] = v
                wl.append((s.name, v))
                self.targets.setdefault(s.name, set()).add(v)
        return wl

    def o(self, eng, fn, wait=(), inc=None, k=1):
        wl = self._resolve_waits(eng, wait)
        postv = None
        if inc is not None:
            inc.count += k
            postv = inc.count
        self.ops[eng].append((fn, wl, (inc.name if inc else None), k, postv))

    def barrier(self, eng, *sems):
        wl = self._resolve_waits(eng, sems)
        self.ops[eng].append((None, wl, None, 0, None))

    def emit(self, nc):
        handles = {s.name: nc.alloc_semaphore(name=s.name) for s in self.sems.values()}
        tsorted = {n: sorted(v) for n, v in self.targets.items()}
        emitted = {n: 0 for n in self.sems}
        with nc.Block() as blk:
            for eng in ENGINES:
                if not self.ops[eng]:
                    continue
                ops = self.ops[eng]

                def body(e, ops=ops):
                    for fn, wl, incname, k, postv in ops:
                        for sname, v in wl:
                            e.wait_ge(handles[sname], v)
                        if fn is None:
                            continue
                        ins = fn(e)
                        if incname is None:
                            continue
                        if k == 16:
                            ins.then_inc(handles[incname], k)
                            continue
                        tl = self.targets.get(incname)
                        if tl is not None and postv in tl:
                            gap = postv - emitted[incname]
                            emitted[incname] = postv
                            ins.then_inc(handles[incname], gap)

                getattr(blk, eng)(body)


class Cfg:
    def __init__(self, S=2048, D=1024, H=16, B=8, HID=4096, E=8):
        self.S, self.D, self.H, self.B, self.HID, self.E = S, D, H, B, HID, E
        self.hd = 64
        self.NT = S // 128
        self.DC = D // 128
        self.HC = HID // 128
        self.N = B * S
        self.A = self.N // 128
        self.scale = math.sqrt(D)
        self.K = 2
        self.CAP = int(1.2 * (self.N // E + 1))
        self.NST = (self.CAP + 127) // 128
        self.SLOT = self.NST * 128
        self.EPS = 1e-9
        self.LNEPS = 1e-5
        self.BIG = float(1 << 22)
        self.SH = S // 2


L1_UPTO = 99


def build_l1(cfg: Cfg):
    S, D, H, NT, DC, SH = cfg.S, cfg.D, cfg.H, cfg.NT, cfg.DC, cfg.SH
    HP = H // 2
    QW = min(512, S)
    QWP = 512
    QT = QW // 128
    nc = bass.Bass("TRN2", target_bir_lowering=False)

    x_in = nc.dram_tensor("x", [S, D], F32, kind="ExternalInput").ap()
    wkq_in = nc.dram_tensor("wkqv_t", [3 * DC * DC * 128, 128], F32, kind="ExternalInput").ap()
    m_in = nc.dram_tensor("m", [H, 1], F32, kind="ExternalInput").ap()
    rwt_in = nc.dram_tensor("rwt", [DC * 128, 8], F32, kind="ExternalInput").ap()
    rb_in = nc.dram_tensor("rb", [1, 8], F32, kind="ExternalInput").ap()
    idn_in = nc.dram_tensor("idn", [128, 128], BF16, kind="ExternalInput").ap()

    x2_out = nc.dram_tensor("x2o", [S, D], BF16, kind="ExternalOutput").ap()
    xf_out = nc.dram_tensor("xfo", [S, D], BF16, kind="ExternalOutput").ap()
    meta_out = nc.dram_tensor("meta", [128, NT * 4], F32, kind="ExternalOutput").ap()

    xb = nc.alloc_sbuf_tensor("xb", [128, NT * D], BF16)
    arena1 = nc.alloc_sbuf_tensor("arena1", [128, 4 * DC * S], mybir.dt.uint8)
    ab = arena1[:, :].bitcast(BF16)
    ct = ab[:, 0:DC * S]
    kTv = ab[:, DC * S:2 * DC * S]
    kTz = ab
    qT = nc.alloc_sbuf_tensor("qT", [128, DC * S], BF16)
    vb = nc.alloc_sbuf_tensor("vb", [128, NT * H * 65 + 128], BF16)
    wv = nc.alloc_sbuf_tensor("wv", [128, DC * 512], BF16)
    wsl = [nc.alloc_sbuf_tensor(f"wsl{i}", [128, DC * 128], BF16) for i in range(2)]
    idn = nc.alloc_sbuf_tensor("idn_s", [128, 128], BF16)
    xsc = [nc.alloc_sbuf_tensor(f"xsc{i}", [128, D], BF16) for i in range(2)]
    dump8 = [nc.alloc_sbuf_tensor(f"dump8{i}", [128, 8], F32) for i in range(2)]
    st_ = nc.alloc_sbuf_tensor("st_", [128, 20 * NT], F32)
    qgrid = nc.alloc_sbuf_tensor("qgrid", [128, S], BF16)
    mgrid = nc.alloc_sbuf_tensor("mgrid", [128, 128], F32)
    mb = nc.alloc_sbuf_tensor("mb", [128, 32], F32)

    mcol = nc.alloc_sbuf_tensor("mcol", [128, 1], BF16)
    mrow = nc.alloc_sbuf_tensor("mrow", [1, 128], BF16)
    onesb = nc.alloc_sbuf_tensor("onesb", [1, 128], BF16)
    nrlT2 = nc.alloc_sbuf_tensor("nrlT2", [128, H * 128], BF16)
    dump = nc.alloc_sbuf_tensor("dump", [128, D], BF16)
    pstrip = nc.alloc_sbuf_tensor("pstrip", [128, 8 * QW], BF16)
    biasall = nc.alloc_sbuf_tensor("biasall", [128, H * NT], F32)
    kvals = nc.alloc_sbuf_tensor("kvals", [128, NT], F32)
    nrl = nc.alloc_sbuf_tensor("nrl", [128, 32], F32)
    avt = nc.alloc_sbuf_tensor("avt", [128, 2 * QW], BF16)
    rcol = nc.alloc_sbuf_tensor("rcol", [128, 4], F32)
    lgt = nc.alloc_sbuf_tensor("lgt", [128, 8 * NT], F32)
    top8 = nc.alloc_sbuf_tensor("top8", [128, 8 * NT], F32)
    tpi = nc.alloc_sbuf_tensor("tpi", [128, 8 * NT], U32)
    tw = nc.alloc_sbuf_tensor("tw", [128, 13 * NT], F32)
    meta = nc.alloc_sbuf_tensor("meta_s", [128, 4 * NT], F32)
    rwt = nc.alloc_sbuf_tensor("rwt_s", [128, DC * 8], BF16)
    rbr = nc.alloc_sbuf_tensor("rbr", [1, 8], BF16)
    ccol = nc.alloc_sbuf_tensor("ccol", [128, 2], F32)

    pS0 = nc.alloc_psum_tensor("pS0", [128, 2 * QWP], F32)
    pS1 = nc.alloc_psum_tensor("pS1", [128, 2 * QWP], F32)
    pvs = [nc.alloc_psum_tensor(f"pv{i}", [128, QWP], F32) for i in range(4)]

    LNB = [("b0", pS0[:, 0:512]), ("b1", pS0[:, 512:1024]),
           ("b2", pS1[:, 0:512]), ("b3", pS1[:, 512:1024]),
           ("b4", pvs[0][:, 0:512]), ("b5", pvs[1][:, 0:512]),
           ("b6", pvs[2][:, 0:512]), ("b7", pvs[3][:, 0:512])]

    p = Plan()
    sxg = [p.sem(f"sxg{i}") for i in range(4)]
    swv = p.sem("swv"); sld = p.sem("sld")
    swkp = [p.sem("swk0"), p.sem("swk1")]
    sxfp = [p.sem("sxf0"), p.sem("sxf1")]
    scst = p.sem("scst")
    pe = p.sem("pe"); ve = p.sem("ve"); se = p.sem("se"); ge = p.sem("ge")
    sso = p.sem("sso")

    evac = {}

    p.o("sync", lambda e: e.dma_start(out=idn[:, :], in_=idn_in), inc=sld, k=16)
    p.o("gpsimd", lambda e: e.memset(mcol[:, :], 0.0), inc=ge)
    gm = ge.count
    scm = p.sem("scm")
    p.o("gpsimd", lambda e: e.dma_start(out=mcol[0:H, :], in_=m_in), wait=[(ge, gm)], inc=scm, k=16)
    p.o("gpsimd", lambda e: e.iota(mgrid[:, :], pattern=[[-1, 128]], base=0,
                                   channel_multiplier=1,
                                   allow_small_or_imprecise_dtypes=True), inc=ge)
    p.o("gpsimd", lambda e: e.iota(qgrid[:, :], pattern=[[1, S]], base=0,
                                   channel_multiplier=0,
                                   allow_small_or_imprecise_dtypes=True), inc=ge)
    p.o("gpsimd", lambda e: e.iota(kvals[:, :], pattern=[[128, NT]], base=0,
                                   channel_multiplier=1,
                                   allow_small_or_imprecise_dtypes=True), inc=ge)
    p.o("gpsimd", lambda e: e.memset(onesb[:, :], 1.0), inc=ge)
    p.o("gpsimd", lambda e: e.memset(ccol[:, 0:1], cfg.LNEPS), inc=ge)
    p.o("gpsimd", lambda e: e.memset(ccol[:, 1:2], 1.0 / cfg.scale), inc=ge)
    p.o("gpsimd", lambda e: e.memset(nrlT2[:, :], 0.0), inc=ge)
    ge_c = ge.count
    GSZ = (NT + 3) // 4
    slab_cnt = [0, 0]

    def slab_load(jt, slot, after):
        base = jt * DC * 128
        slab_cnt[slot] += 1
        p.o("gpsimd", lambda e, base=base, slot=slot: e.dma_start(
            out=wsl[slot][:, :].rearrange("p (c j) -> p c j", j=128),
            in_=wkq_in[base:base + DC * 128, :].rearrange("(c p) j -> p c j", p=128)),
            wait=[after] if after else [], inc=swkp[slot], k=16)
        return (swkp[slot], slab_cnt[slot] * 16)

    w0 = slab_load(0, 0, None)
    w1 = slab_load(1, 1, None)
    sxs = p.sem("sxs")
    stgx = pstrip[:, :].bitcast(F32)
    cast_mark = {}
    for t in (0, 1):
        p.o("sync", lambda e, t=t: e.dma_start(
            out=stgx[:, (t % 2) * D:(t % 2 + 1) * D], in_=x_in[t * 128:(t + 1) * 128, :]),
            inc=sxs, k=16)
    xall = [(s, s.count) for s in sxg if s.count]
    VJ0 = 2 * DC
    swv2 = p.sem("swv2")
    NVC = (D + 511) // 512
    TPH = DC // NVC

    def load_wv_half(jc, sem, after=None, ccs=None):
        for cc in (range(DC) if ccs is None else ccs):
            for jj in range(TPH):
                jt = jc * TPH + jj
                base = ((VJ0 + jt) * DC + cc) * 128
                p.o("gpsimd", lambda e, cc=cc, jj=jj, base=base: e.dma_start(
                    out=wv[:, cc * 512 + jj * 128: cc * 512 + (jj + 1) * 128],
                    in_=wkq_in[base:base + 128, :]),
                    wait=[after] if after else [], inc=sem, k=16)

    p.o("gpsimd", lambda e: e.dma_start(
        out=rwt[:, :].rearrange("p (c e) -> p c e", e=8),
        in_=rwt_in.rearrange("(c p) e -> p c e", p=128)), inc=scst, k=16)
    p.o("gpsimd", lambda e: e.dma_start(out=rbr[:, :], in_=rb_in), inc=scst, k=16)
    p.o("vector", lambda e: e.tensor_single_scalar(mgrid[:, :], mgrid[:, :], 0.0, OP.is_gt),
        wait=[(ge, ge_c)], inc=ve)
    p.o("vector", lambda e: e.tensor_scalar(mgrid[:, :], mgrid[:, :], -1e30, None, OP.mult),
        wait=[(ve, ve.count)], inc=ve)
    p.o("vector", lambda e: e.memset(
        vb[:, 0:NT * H * 65].rearrange("p (t h d) -> p t h d", h=H, d=65)[:, :, :, 64:65], 1.0),
        inc=ve)
    p.o("vector", lambda e: e.memset(vb[:, NT * H * 65:NT * H * 65 + 128], 0.0), inc=ve)
    ve_vones = ve.count
    p.o("tensor", lambda e: e.transpose(pS0[:, 0:512].bitcast(BF16)[0:1, 0:128],
                                        mcol[:, :], idn[:, :]),
        wait=[(sld, 16), (scm, 16)], inc=pe)
    p.o("vector", lambda e: e.tensor_copy(mrow[:, :], pS0[:, 0:512].bitcast(BF16)[0:1, 0:128]),
        wait=[(pe, pe.count)], inc=ve)
    p.o("tensor", lambda e: e.matmul(pS0[:, 0:128], onesb[:, :], mrow[:, :], start=True, stop=True),
        wait=[(ve, ve.count), (ge, ge_c)], inc=pe)
    p.o("vector", lambda e: e.tensor_copy(mb[:, 0:H], pS0[:, 0:H]), wait=[(pe, pe.count)], inc=ve)
    evac["b0"] = (ve, ve.count)
    p.o("vector", lambda e: e.tensor_scalar(nrl[:, 0:H], mb[:, 0:H], 0.0, -1.0, OP.max, OP.mult),
        wait=[(ve, ve.count)], inc=ve)

    SUM, SQ, MEAN, EX2, MSQ, VAR, SD, RSTD, NMR = (i * NT for i in range(9))
    GS = min(8, DC)

    def ln_stats_tile(o, t, sw):
        p.o("vector", lambda e, t=t: e.reduce_sum(
            st_[:, o + SUM + t: o + SUM + t + 1], xb[:, t * D:(t + 1) * D], axis=AX.X),
            wait=[*sw, (ve, ve.count)], inc=ve)
        p.o("scalar", lambda e, t=t: e.activation(
            dump[:, :], xb[:, t * D:(t + 1) * D], AF.Square, bias=0.0, scale=1.0,
            accum_out=st_[:, o + SQ + t: o + SQ + t + 1]),
            wait=[*sw, (se, se.count)], inc=se)

    def ln_chain(o, tlo=0, thi=None):
        thi = NT if thi is None else thi
        p.o("vector", lambda e: e.tensor_scalar(
            st_[:, o + MEAN + tlo:o + MEAN + thi], st_[:, o + SUM + tlo:o + SUM + thi], 1.0 / D, None, OP.mult),
            wait=[(ve, ve.count), (se, se.count)], inc=ve)
        p.o("vector", lambda e: e.tensor_scalar(
            st_[:, o + EX2 + tlo:o + EX2 + thi], st_[:, o + SQ + tlo:o + SQ + thi], 1.0 / D, None, OP.mult), inc=ve)
        p.o("vector", lambda e: e.tensor_tensor(
            st_[:, o + MSQ + tlo:o + MSQ + thi], st_[:, o + MEAN + tlo:o + MEAN + thi],
            st_[:, o + MEAN + tlo:o + MEAN + thi], OP.mult), wait=[(ve, ve.count)], inc=ve)
        p.o("vector", lambda e: e.tensor_tensor(
            st_[:, o + VAR + tlo:o + VAR + thi], st_[:, o + EX2 + tlo:o + EX2 + thi],
            st_[:, o + MSQ + tlo:o + MSQ + thi], OP.subtract), wait=[(ve, ve.count)], inc=ve)
        p.o("vector", lambda e: e.tensor_scalar(
            st_[:, o + VAR + tlo:o + VAR + thi], st_[:, o + VAR + tlo:o + VAR + thi], 0.0, None, OP.max),
            wait=[(ve, ve.count)], inc=ve)
        p.o("scalar", lambda e: e.activation(
            st_[:, o + SD + tlo:o + SD + thi], st_[:, o + VAR + tlo:o + VAR + thi], AF.Sqrt,
            bias=ccol[:, 0:1], scale=1.0), wait=[(ve, ve.count), (ge, ge_c)], inc=se)
        p.o("vector", lambda e: e.reciprocal(
            st_[:, o + RSTD + tlo:o + RSTD + thi], st_[:, o + SD + tlo:o + SD + thi]),
            wait=[(se, se.count)], inc=ve)
        p.o("vector", lambda e: e.scalar_tensor_tensor(
            st_[:, o + NMR + tlo:o + NMR + thi], st_[:, o + MEAN + tlo:o + MEAN + thi], -1.0,
            st_[:, o + RSTD + tlo:o + RSTD + thi], OP.mult, OP.mult), wait=[(ve, ve.count)], inc=ve)
        return ve.count

    pe_xsc = {0: 0, 1: 0}

    def ln_finish(o, dst_ct, ve_stats, out_store=None, extra_wait=(), tiles=None, ci0=0):
        marks = []
        ci = ci0
        for t in (range(NT) if tiles is None else tiles):
            par = t % 2
            buf = xsc[par]
            xw = [(ve, ve_stats), *extra_wait]
            if pe_xsc[par]:
                xw.append((pe, pe_xsc[par]))
            if out_store is not None and sxfp[par].count > 0:
                xw.append((sxfp[par], sxfp[par].count))
            p.o("scalar", lambda e, t=t, buf=buf: e.activation(
                buf[:, :], xb[:, t * D:(t + 1) * D], AF.Identity,
                bias=st_[:, o + NMR + t:o + NMR + t + 1],
                scale=st_[:, o + RSTD + t:o + RSTD + t + 1]),
                wait=xw, inc=se)
            se_t = se.count
            if out_store is not None:
                dram_fn = out_store
                p.o("sync", lambda e, t=t, buf=buf, dram_fn=dram_fn: e.dma_start(
                    out=dram_fn(t), in_=buf[:, :]), wait=[(se, se_t)], inc=sxfp[par], k=16)
            for g0 in range(0, DC, GS):
                bname, bank = LNB[(ci // GS) % 8]
                bankb = bank.bitcast(BF16)
                for j in range(GS):
                    cc = g0 + j
                    w_ = [(se, se_t), (sld, 16), *extra_wait]
                    if j == 0 and bname in evac:
                        w_.append(evac[bname])
                    p.o("tensor", lambda e, bankb=bankb, j=j, buf=buf, cc=cc: e.transpose(
                        bankb[:, j * 128:(j + 1) * 128], buf[:, cc * 128:(cc + 1) * 128],
                        idn[:, :]), wait=w_, inc=pe)
                ctv = dst_ct[:, :].rearrange("p (c s) -> p c s", s=S)[:, g0:g0 + GS,
                                                                     t * 128:(t + 1) * 128]
                bv = bankb[:, 0:GS * 128].rearrange("p (c j) -> p c j", j=128)
                p.o("vector", lambda e, ctv=ctv, bv=bv: e.tensor_copy(ctv, bv),
                    wait=[(pe, pe.count), (ve, ve.count)], inc=ve)
                evac[bname] = (ve, ve.count)
                ci += GS
            pe_xsc[par] = pe.count
            marks.append(ve.count)
        return marks

    def _finish():
        vv = ve.count
        for t in range(NT):
            p.o("sync", lambda e, t=t: e.dma_start(
                out=x2_out[t * 128:(t + 1) * 128, :], in_=xb[:, t * D:(t + 1) * D]),
                wait=[(ve, vv), (se, se.count), (pe, pe.count), *xall], inc=sso, k=16)
        p.barrier("sync", (sso, sso.count))
        p.emit(nc)
        return nc

    if L1_UPTO <= 0:
        return _finish()
    marks1 = []
    for glo in range(0, NT, GSZ):
        ghi = min(glo + GSZ, NT)
        for t in range(glo, ghi):
            p.o("vector", lambda e, t=t: e.tensor_copy(
                xb[:, t * D:(t + 1) * D], stgx[:, (t % 2) * D:(t % 2 + 1) * D]),
                wait=[(sxs, 16 * (t + 1)), (ve, ve.count)], inc=ve)
            cast_mark[t] = ve.count
            if t + 2 < NT:
                p.o("sync", lambda e, t=t: e.dma_start(
                    out=stgx[:, (t % 2) * D:(t % 2 + 1) * D],
                    in_=x_in[(t + 2) * 128:(t + 3) * 128, :]),
                    wait=[(ve, cast_mark[t])], inc=sxs, k=16)
            ln_stats_tile(0, t, [(ve, cast_mark[t])])
        vs_g = ln_chain(0, glo, ghi)
        marks1 += ln_finish(0, ct, vs_g, tiles=range(glo, ghi), ci0=glo * (DC // GS))
    ve_xnT = ve.count
    for hx in range(H):
        p.o("vector", lambda e, hx=hx: e.tensor_scalar(
            nrlT2[0:1, hx * 128:(hx + 1) * 128], onesb[:, :], nrl[0:1, hx:hx + 1],
            None, OP.mult), wait=[(ve, ve.count), (ge, ge_c)], inc=ve)
    for h_ in range(H):
        p.o("vector", lambda e, h_=h_: e.tensor_scalar(
            biasall[:, h_ * NT:(h_ + 1) * NT], kvals[:, :], mb[:, h_:h_ + 1], None, OP.mult),
            wait=[(ve, ve.count), (ge, ge_c)], inc=ve)

    if L1_UPTO <= 1:
        return _finish()

    swk_w = {0: w0, 1: w1}
    QCW = min(512, S)
    QC = (S + QCW - 1) // QCW
    TPQ = QCW // 128
    for jt in range(2 * DC):
        slot = jt % 2
        swk_need = swk_w[slot]
        dst = kTv if jt < DC else qT
        dcc = jt % DC
        scl = 1.0 if jt < DC else 1.0 / cfg.scale
        for qc in range(QC):
            ps = (pS0, pS1)[qc % 2]
            bname = ("b0", "b2")[qc % 2]
            for cc in range(DC):
                w = [swk_need, (ve, marks1[min((qc + 1) * TPQ - 1, NT - 1)])]
                if cc == 0 and bname in evac:
                    w.append(evac[bname])
                p.o("tensor", lambda e, qc=qc, cc=cc, ps=ps, slot=slot: e.matmul(
                    ps[:, 0:QCW], wsl[slot][:, cc * 128:(cc + 1) * 128],
                    ct[:, cc * S + qc * QCW: cc * S + (qc + 1) * QCW],
                    start=(cc == 0), stop=(cc == DC - 1)),
                    wait=w, inc=(pe if cc == DC - 1 else None))
            scl_ap = (1.0 if scl == 1.0 else ccol[:, 1:2])
            p.o("scalar", lambda e, dst=dst, dcc=dcc, qc=qc, ps=ps, scl_ap=scl_ap: e.activation(
                dst[:, dcc * S + qc * QCW: dcc * S + (qc + 1) * QCW], ps[:, 0:QCW], AF.Copy,
                bias=0.0, scale=scl_ap), wait=[(pe, pe.count), (ge, ge_c)], inc=se)
            evac[bname] = (se, se.count)
        if jt + 2 < 2 * DC:
            swk_w[slot] = slab_load(jt + 2, slot, (pe, pe.count))
        if jt == 3:
            load_wv_half(0, swv, ccs=range(0, DC // 2))
        if jt == 9:
            load_wv_half(0, swv, ccs=range(DC // 2, DC))
            swv_all = swv.count
    VCW = min(512, D)
    VC = (D + VCW - 1) // VCW
    HPC = VCW // 64
    for jc in range(VC):
        if jc > 0:
            load_wv_half(jc, swv2, after=(pe, pe.count))
        wvsem = (swv, swv_all) if jc == 0 else (swv2, swv2.count)
        for t in range(NT):
            ps = (pS0, pS1)[(jc * NT + t) % 2]
            bname = ("b0", "b2")[(jc * NT + t) % 2]
            for cc in range(DC):
                w = [wvsem, (ve, ve_xnT)]
                if cc == 0 and bname in evac:
                    w.append(evac[bname])
                p.o("tensor", lambda e, t=t, cc=cc, ps=ps: e.matmul(
                    ps[:, 0:VCW], ct[:, cc * S + t * 128:cc * S + (t + 1) * 128],
                    wv[:, cc * 512:cc * 512 + VCW],
                    start=(cc == 0), stop=(cc == DC - 1)),
                    wait=w, inc=(pe if cc == DC - 1 else None))
            p.o("vector", lambda e, t=t, jc=jc, ps=ps: e.tensor_copy(
                vb[:, t * H * 65 + jc * HPC * 65: t * H * 65 + (jc * HPC + HPC) * 65]
                  .rearrange("p (h d) -> p h d", d=65)[:, :, 0:64],
                ps[:, 0:VCW].rearrange("p (h d) -> p h d", d=64)),
                wait=[(pe, pe.count), (ve, ve_vones)], inc=ve)
            evac[bname] = (ve, ve.count)
    se_kqv = se.count
    ve_kqv = ve.count
    pe_kqv = pe.count

    ve_rp = {}
    ge_rp = {}
    ve_cp = {}
    for hp_ in range(HP):
        c0 = hp_ * S
        s0 = (DC + hp_) * S
        p.o("vector", lambda e, c0=c0, s0=s0: e.tensor_copy(
            kTz[0:64, c0:c0 + S], kTz[0:64, s0:s0 + S]),
            wait=[(se, se_kqv), (pe, pe_kqv), (ve, ve.count)], inc=ve)
        ve_cp[hp_] = ve.count
        ve_rp[hp_] = ve.count
    for hp_ in range(HP):
        c0 = hp_ * S
        s0 = (DC + hp_) * S
        p.o("gpsimd", lambda e, c0=c0: e.memset(kTz[64:128, c0:c0 + S], 0.0),
            wait=[(pe, pe_kqv), (ge, ge.count)], inc=ge)
        p.o("gpsimd", lambda e, s0=s0: e.memset(kTz[0:64, s0:s0 + S], 0.0),
            wait=[(se, se_kqv), (ve, ve_cp[hp_]), (ge, ge.count)], inc=ge)
        ge_rp[hp_] = ge.count
    if L1_UPTO <= 2:
        return _finish()

    p.o("vector", lambda e: e.memset(avt[64:128, :], 0.0), inc=ve)
    ve_avt0 = ve.count
    ge_qg = ge_c

    NW = S // QW
    pend = [None, -1, -1]
    pav_free = {}
    avt_tp = {0: 0, 1: 0}
    last_av = {}
    LN2O = 10 * NT

    ln2_pend = []

    def emit_ln2_tile():
        if not ln2_pend:
            return
        t = ln2_pend.pop(0)
        ln_stats_tile(LN2O, t, [(ve, ve.count)])
        p.o("sync", lambda e, t=t: e.dma_start(
            out=x2_out[t * 128:(t + 1) * 128, :], in_=xb[:, t * D:(t + 1) * D]),
            wait=[(ve, ve.count)], inc=sso, k=16)

    def emit_ln2_chunk(wd):
        ln2_pend.extend(range(wd * QT, (wd + 1) * QT))

    inst = 0
    order = []
    for j_ in range(NW // 2):
        for hp_i in range(HP):
            order.append((NW - 1 - j_, hp_i))
            order.append((j_, hp_i))
    if NW % 2:
        for hp_i in range(HP):
            order.append((NW // 2, hp_i))
    wdone = {}
    for (w, hp) in order:
        wlo, whi = w * QW, (w + 1) * QW
        kts = [kt for kt in range(NT) if kt * 128 < whi]
        nk = len(kts)
        h0, h1 = 2 * hp, 2 * hp + 1
        ipar = inst % 2
        se_exp = {}
        av_pe = {}

        def emit_av(ki, X, hx, first, last, w=w, wlo=wlo, whi=whi, kts=kts, ipar=ipar):
            kp = kts[ki]
            q0p = max(wlo, kp * 128)
            nqp = whi - q0p
            slot = X * 4 + (ki % 4)
            pav = pvs[X * 2 + ipar]
            bank = f"b{4 + X * 2 + ipar}"
            avw = [(se, se_exp[(ki, X)])]
            if first:
                avw.append(pav_free.get((X, ipar)) or evac.get(bank, (ve, 0)))
            p.o("tensor", lambda e, kp=kp, hx=hx, q0p=q0p, nqp=nqp, wlo=wlo,
                pav=pav, slot=slot, first=first, last=last: e.matmul(
                pav[0:128, q0p - wlo: q0p - wlo + nqp],
                vb[:, kp * H * 65 + hx * 65: kp * H * 65 + hx * 65 + 128],
                pstrip[:, slot * QW: slot * QW + nqp],
                start=first, stop=last),
                wait=avw, inc=pe)
            av_pe[(ki, X)] = pe.count
            last_av[(ki % 4, X)] = pe.count

        for ki, kt in enumerate(kts):
            q0 = max(wlo, kt * 128)
            nq = whi - q0
            for X, hx in ((0, h0), (1, h1)):
                par = ki % 2
                pst = (pS0, pS1)[par]
                bank = f"b{par * 2 + X}"
                w1_ = [(se, se_kqv), (ve, ve_kqv), (ve, ve_rp[hp]), (ge, ge_rp[hp])]
                if bank in evac:
                    w1_.append(evac[bank])
                reg = hp if X == 0 else DC + hp
                p.o("tensor", lambda e, hp=hp, kt=kt, q0=q0, nq=nq, pst=pst, X=X, reg=reg: e.matmul(
                    pst[:, X * QWP: X * QWP + nq],
                    kTz[:, reg * S + kt * 128: reg * S + (kt + 1) * 128],
                    qT[:, hp * S + q0: hp * S + q0 + nq],
                    start=True, stop=False), wait=w1_, inc=None)
                p.o("tensor", lambda e, pst=pst, X=X, q0=q0, nq=nq, hx=hx: e.matmul(
                    pst[:, X * QWP: X * QWP + nq],
                    nrlT2[:, hx * 128:(hx + 1) * 128],
                    qgrid[:, q0:q0 + nq],
                    start=False, stop=True), inc=pe)
                pe_mm = pe.count
                ve_strip = 0
                if q0 == kt * 128:
                    p.o("vector", lambda e, pst=pst, X=X: e.tensor_tensor(
                        pst[:, X * QWP: X * QWP + 128], pst[:, X * QWP: X * QWP + 128],
                        mgrid[:, :], OP.add),
                        wait=[(pe, pe_mm), (ve, ve.count)], inc=ve)
                    ve_strip = ve.count
                slot = X * 4 + (ki % 4)
                ew = [(pe, pe_mm)]
                if ve_strip:
                    ew.append((ve, ve_strip))
                if (ki % 4, X) in last_av:
                    ew.append((pe, last_av[(ki % 4, X)]))
                p.o("scalar", lambda e, pst=pst, X=X, nq=nq, kt=kt, hx=hx, slot=slot: e.activation(
                    pstrip[:, slot * QW: slot * QW + nq], pst[:, X * QWP: X * QWP + nq],
                    AF.Exp, bias=biasall[:, hx * NT + kt: hx * NT + kt + 1], scale=1.0),
                    wait=ew, inc=se)
                evac[bank] = (se, se.count)
                se_exp[(ki, X)] = se.count
            if ki >= 2:
                for X, hx in ((0, h0), (1, h1)):
                    emit_av(ki - 2, X, hx, first=(ki - 2 == 0), last=False)
            if ki == 1 and pend[0] is not None:
                emit_ln2_tile()
                fw = pend[1]
                pend[0]()
                pend[0] = None
                wdone[fw] = wdone.get(fw, 0) + 1
                if wdone[fw] == HP:
                    emit_ln2_chunk(fw)
        for kl in (nk - 2, nk - 1):
            for X, hx in ((0, h0), (1, h1)):
                emit_av(kl, X, hx, first=(kl == 0), last=(kl == nk - 1))
        pe_av_end = pe.count

        def mk_norm(w=w, h0=h0, h1=h1, pe_av_end=pe_av_end, ipar=ipar):
            def norm():
                ve_avt = {}
                for X in (0, 1):
                    pav = pvs[X * 2 + ipar]
                    aw = [(pe, pe_av_end), (ve, max(ve_avt0, ve.count))]
                    if avt_tp[X]:
                        aw.append((pe, avt_tp[X]))
                    p.o("vector", lambda e, X=X, pav=pav: e.tensor_copy(
                        avt[0:65, X * QW:(X + 1) * QW], pav[0:65, 0:QW]),
                        wait=aw, inc=ve)
                    ve_avt[X] = ve.count
                for X, hx in ((0, h0), (1, h1)):
                    pav = pvs[X * 2 + ipar]
                    pavb = pav[:, :].bitcast(BF16)
                    bank = f"b{4 + X * 2 + ipar}"
                    for qt in range(QT):
                        tw_ = [(ve, ve_avt[X])]
                        if qt == 0 and bank in evac:
                            tw_.append(evac[bank])
                        p.o("tensor", lambda e, X=X, qt=qt, pavb=pavb: e.transpose(
                            pavb[:, qt * 128:(qt + 1) * 128],
                            avt[:, X * QW + qt * 128: X * QW + (qt + 1) * 128],
                            idn[:, :]), wait=tw_, inc=pe)
                    avt_tp[X] = pe.count
                    pe_tp = pe.count
                    for qt in range(QT):
                        p.o("vector", lambda e, X=X, qt=qt, pavb=pavb: e.reciprocal(
                            rcol[:, X:X + 1], pavb[:, qt * 128 + 64: qt * 128 + 65]),
                            wait=[(pe, pe_tp), (ve, ve.count)], inc=ve)
                        qti = w * QT + qt
                        p.o("vector", lambda e, X=X, hx=hx, qt=qt, qti=qti, pavb=pavb: e.scalar_tensor_tensor(
                            xb[:, qti * D + hx * 64: qti * D + hx * 64 + 64],
                            pavb[:, qt * 128: qt * 128 + 64], rcol[:, X:X + 1],
                            xb[:, qti * D + hx * 64: qti * D + hx * 64 + 64],
                            OP.mult, OP.add), wait=[(ve, ve.count)], inc=ve)
                    pav_free[(X, ipar)] = (ve, ve.count)
                    evac[bank] = (ve, ve.count)
            return norm
        pend[0], pend[1], pend[2] = mk_norm(), w, hp
        inst += 1
    if pend[0] is not None:
        fw = pend[1]
        pend[0]()
        pend[0] = None
        wdone[fw] = wdone.get(fw, 0) + 1
        if wdone[fw] == HP:
            emit_ln2_chunk(fw)
    while ln2_pend:
        emit_ln2_tile()
    ve_attn = ve.count
    pe_attn = pe.count
    se_attn = se.count
    if L1_UPTO <= 3:
        return _finish()

    ln2_marks = ln_finish(LN2O, ct, ln_chain(LN2O),
                          out_store=(lambda t: xf_out[t * 128:(t + 1) * 128, :]),
                          extra_wait=[(pe, pe_attn)])
    ve_xfT = ve.count
    if L1_UPTO <= 4:
        p.barrier("sync", (sso, sso.count), (sxfp[0], sxfp[0].count), (sxfp[1], sxfp[1].count))
        p.emit(nc)
        return nc

    for t in range(NT):
        ps = (pS0, pS1)[t % 2]
        bname = ("b0", "b2")[t % 2]
        for cc in range(DC):
            w = [(ve, ln2_marks[t]), (scst, scst.count)]
            if cc == 0 and bname in evac:
                w.append(evac[bname])
            p.o("tensor", lambda e, t=t, cc=cc, ps=ps: e.matmul(
                ps[:, 0:8], ct[:, cc * S + t * 128: cc * S + (t + 1) * 128],
                rwt[:, cc * 8:(cc + 1) * 8], start=(cc == 0), stop=False),
                wait=w, inc=None)
        p.o("tensor", lambda e, ps=ps: e.matmul(
            ps[:, 0:8], onesb[:, :], rbr[:, :], start=False, stop=True), inc=pe)
        p.o("vector", lambda e, t=t, ps=ps: e.tensor_copy(lgt[:, t * 8:(t + 1) * 8], ps[:, 0:8]),
            wait=[(pe, pe.count)], inc=ve)
        evac[bname] = (ve, ve.count)
    ve_lgt = ve.count
    for t in range(NT):
        p.o("vector", lambda e, t=t: e.max(top8[:, t * 8:(t + 1) * 8], lgt[:, t * 8:(t + 1) * 8]),
            wait=[(ve, ve_lgt)], inc=ve)
        p.o("vector", lambda e, t=t: e.max_index(
            tpi[:, t * 8:(t + 1) * 8], top8[:, t * 8:(t + 1) * 8], lgt[:, t * 8:(t + 1) * 8]),
            wait=[(ve, ve.count)], inc=ve)
    ve_top = ve.count
    v1v = lambda: top8[:, :].rearrange("p (t e) -> p t e", e=8)[:, :, 0:1]
    v2v = lambda: top8[:, :].rearrange("p (t e) -> p t e", e=8)[:, :, 1:2]
    NV = NT
    D21, NV1, SSUM, P1, P2, DEN, RDEN, W1, W2, VM1, VM2, E21 = (i * NV for i in range(12))
    p.o("vector", lambda e: e.tensor_tensor(tw[:, D21:D21 + NV], v2v(), v1v(), OP.subtract),
        wait=[(ve, ve_top)], inc=ve)
    p.o("vector", lambda e: e.tensor_scalar(tw[:, NV1:NV1 + NV], v1v(), -1.0, None, OP.mult),
        inc=ve)
    ve_nv1 = ve.count
    for t in range(NT):
        d8 = dump8[t % 2]
        p.o("scalar", lambda e, t=t, d8=d8: e.activation(
            d8[:, :], lgt[:, t * 8:(t + 1) * 8], AF.Exp,
            bias=tw[:, NV1 + t:NV1 + t + 1], scale=1.0,
            accum_out=tw[:, SSUM + t:SSUM + t + 1]),
            wait=[(ve, ve_nv1), (se, se.count)], inc=se)
    p.o("scalar", lambda e: e.activation(tw[:, E21:E21 + NV], tw[:, D21:D21 + NV],
                                         AF.Exp, bias=0.0, scale=1.0),
        wait=[(ve, ve_nv1)], inc=se)
    se_probs = se.count
    p.o("vector", lambda e: e.reciprocal(tw[:, P1:P1 + NV], tw[:, SSUM:SSUM + NV]),
        wait=[(se, se_probs)], inc=ve)
    p.o("vector", lambda e: e.tensor_tensor(tw[:, P2:P2 + NV], tw[:, E21:E21 + NV],
                                            tw[:, P1:P1 + NV], OP.mult),
        wait=[(ve, ve.count)], inc=ve)
    p.o("vector", lambda e: e.tensor_tensor(tw[:, DEN:DEN + NV], tw[:, P1:P1 + NV],
                                            tw[:, P2:P2 + NV], OP.add),
        wait=[(ve, ve.count)], inc=ve)
    p.o("vector", lambda e: e.tensor_scalar(tw[:, DEN:DEN + NV], tw[:, DEN:DEN + NV],
                                            cfg.EPS, None, OP.add),
        wait=[(ve, ve.count)], inc=ve)
    p.o("vector", lambda e: e.reciprocal(tw[:, RDEN:RDEN + NV], tw[:, DEN:DEN + NV]),
        wait=[(ve, ve.count)], inc=ve)
    p.o("vector", lambda e: e.tensor_tensor(tw[:, W1:W1 + NV], tw[:, P1:P1 + NV],
                                            tw[:, RDEN:RDEN + NV], OP.mult),
        wait=[(ve, ve.count)], inc=ve)
    p.o("vector", lambda e: e.tensor_tensor(tw[:, W2:W2 + NV], tw[:, P2:P2 + NV],
                                            tw[:, RDEN:RDEN + NV], OP.mult),
        wait=[(ve, ve.count)], inc=ve)
    p.o("vector", lambda e: e.tensor_single_scalar(tw[:, VM1:VM1 + NV], tw[:, W1:W1 + NV],
                                                   cfg.EPS, OP.is_gt),
        wait=[(ve, ve.count)], inc=ve)
    p.o("vector", lambda e: e.tensor_single_scalar(tw[:, VM2:VM2 + NV], tw[:, W2:W2 + NV],
                                                   cfg.EPS, OP.is_gt),
        wait=[(ve, ve.count)], inc=ve)
    ve_w = ve.count
    mview = lambda q: meta[:, :].rearrange("p (t q) -> p t q", q=4)[:, :, q:q + 1]
    iview = lambda c: tpi[:, :].rearrange("p (t e) -> p t e", e=8)[:, :, c:c + 1]
    p.o("vector", lambda e: e.tensor_copy(tw[:, DEN:DEN + NV], iview(0)), wait=[(ve, ve_w)], inc=ve)
    p.o("vector", lambda e: e.tensor_scalar(tw[:, DEN:DEN + NV], tw[:, DEN:DEN + NV], -8.0, None, OP.add),
        wait=[(ve, ve.count)], inc=ve)
    p.o("vector", lambda e: e.scalar_tensor_tensor(
        mview(0), tw[:, DEN:DEN + NV], 1.0, tw[:, VM1:VM1 + NV], OP.mult, OP.mult),
        wait=[(ve, ve.count)], inc=ve)
    p.o("vector", lambda e: e.tensor_scalar(mview(0), mview(0), 8.0, None, OP.add),
        wait=[(ve, ve.count)], inc=ve)
    p.o("vector", lambda e: e.tensor_copy(tw[:, SSUM:SSUM + NV], iview(1)), inc=ve)
    p.o("vector", lambda e: e.tensor_scalar(tw[:, SSUM:SSUM + NV], tw[:, SSUM:SSUM + NV], -8.0, None, OP.add),
        wait=[(ve, ve.count)], inc=ve)
    p.o("vector", lambda e: e.scalar_tensor_tensor(
        mview(1), tw[:, SSUM:SSUM + NV], 1.0, tw[:, VM2:VM2 + NV], OP.mult, OP.mult),
        wait=[(ve, ve.count)], inc=ve)
    p.o("vector", lambda e: e.tensor_scalar(mview(1), mview(1), 8.0, None, OP.add),
        wait=[(ve, ve.count)], inc=ve)
    p.o("vector", lambda e: e.tensor_copy(mview(2), tw[:, W1:W1 + NV]), inc=ve)
    p.o("vector", lambda e: e.tensor_copy(mview(3), tw[:, W2:W2 + NV]), inc=ve)
    ve_meta = ve.count
    p.o("sync", lambda e: e.dma_start(out=meta_out, in_=meta[:, :]), wait=[(ve, ve_meta)],
        inc=sso, k=16)

    p.barrier("sync", (sso, sso.count), (sxfp[0], sxfp[0].count), (sxfp[1], sxfp[1].count))
    p.emit(nc)
    return nc


def _tile_wkqv(wk):
    D = wk.shape[0]
    J, DCc = wk.shape[1] // 128, D // 128
    t = wk.reshape(DCc, 128, J, 128).transpose(2, 0, 1, 3).reshape(J * DCc * 128, 128)
    return np.ascontiguousarray(t)


def _np_ln(x, eps=1e-5):
    m = x.mean(-1, keepdims=True)
    v = x.var(-1, keepdims=True)
    return (x - m) / np.sqrt(v + eps)


def build_l2(cfg: Cfg):
    S, D, H, NT, DC, HC = cfg.S, cfg.D, cfg.H, cfg.NT, cfg.DC, cfg.HC
    N, A, CAP, NST, SLOT, HID = cfg.N, cfg.A, cfg.CAP, cfg.NST, cfg.SLOT, cfg.HID
    BIG = cfg.BIG
    assert A <= 128
    nc = bass.Bass("TRN2", target_bir_lowering=False)

    meta_in = nc.dram_tensor("meta_all", [128, 4 * A], F32, kind="ExternalInput").ap()
    xf_in = nc.dram_tensor("xf_all", [N, D], BF16, kind="ExternalInput").ap()
    fc1_in = nc.dram_tensor("fc1t", [DC * 128, HID], F32, kind="ExternalInput").ap()
    fc2_in = nc.dram_tensor("fc2t", [HID, D], F32, kind="ExternalInput").ap()
    b1_in = nc.dram_tensor("b1t", [128, HC], F32, kind="ExternalInput").ap()
    b2_in = nc.dram_tensor("b2r", [1, D], F32, kind="ExternalInput").ap()
    ls_in = nc.dram_tensor("lsr", [1, D], F32, kind="ExternalInput").ap()
    eid_in = nc.dram_tensor("eid", [128, 1], F32, kind="ExternalInput").ap()
    idn_in = nc.dram_tensor("idn", [128, 128], BF16, kind="ExternalInput").ap()
    stg_d = nc.dram_tensor("stg_d", [128 * 6 * A, 1], F32).ap()

    y_out = nc.dram_tensor("y", [SLOT, D], BF16, kind="ExternalOutput").ap()

    ea = nc.alloc_sbuf_tensor("ea", [128, 4 * A], F32)
    eid = nc.alloc_sbuf_tensor("eid_s", [128, 1], F32)
    idn = nc.alloc_sbuf_tensor("idn_s", [128, 128], BF16)
    idnf = nc.alloc_sbuf_tensor("idnf", [128, 128], F32)
    rel = nc.alloc_sbuf_tensor("rel", [128, 128], F32)
    qabs = nc.alloc_sbuf_tensor("qabs", [128, 128], F32)
    cbr = nc.alloc_sbuf_tensor("cbr", [128, 4], F32)
    nidx = nc.alloc_sbuf_tensor("nidx", [128, A], F32)
    scolg = nc.alloc_sbuf_tensor("scolg", [128, NST], F32)
    utri = nc.alloc_sbuf_tensor("utri", [128, 128], BF16)
    ltri = nc.alloc_sbuf_tensor("ltri", [128, 128], BF16)
    lful = nc.alloc_sbuf_tensor("lful", [128, 128], BF16)
    shft = nc.alloc_sbuf_tensor("shft", [128, 128], BF16)
    itk = [nc.alloc_sbuf_tensor(f"itk{k}", [128, A], BF16) for k in range(2)]
    itab = [nc.alloc_sbuf_tensor(f"itab{k}", [128, 128], BF16) for k in range(2)]
    scb = nc.alloc_sbuf_tensor("scb", [128, 2], BF16)
    offs = nc.alloc_sbuf_tensor("offs", [128, 8], F32)
    dcb = nc.alloc_sbuf_tensor("dcb", [128, 4], BF16)
    lpg = [nc.alloc_sbuf_tensor(f"lpg{k}", [128, 128], F32) for k in range(2)]
    posg = [nc.alloc_sbuf_tensor(f"posg{k}", [128, 128], F32) for k in range(2)]
    lpt = [nc.alloc_sbuf_tensor(f"lpt{k}", [128, A], F32) for k in range(2)]
    rhs3 = [nc.alloc_sbuf_tensor(f"rhs3{k}", [128, 3 * A], F32) for k in range(2)]
    mscr = [nc.alloc_sbuf_tensor(f"mscr{i}", [128, 128], F32) for i in range(3)]
    arena = nc.alloc_sbuf_tensor("arena", [128, 32768], mybir.dt.uint8)
    stg = arena[:, 0:6 * A * 4].bitcast(F32)
    hts = arena.bitcast(BF16)
    cnt = nc.alloc_sbuf_tensor("cnt", [128, 4 * NST], F32)
    cwork = nc.alloc_sbuf_tensor("cwork", [128, 10 * NST], F32)
    tokf = nc.alloc_sbuf_tensor("tokf", [128, NST], F32)
    toku = nc.alloc_sbuf_tensor("toku", [128, NST], U32)
    gcolu = nc.alloc_sbuf_tensor("gcolu", [128, NST], U32)
    toks = [nc.alloc_sbuf_tensor(f"toks{i}", [128, D], BF16) for i in range(8)]
    tokst = nc.alloc_sbuf_tensor("tokst", [128, DC * 512], BF16)
    ysb = [nc.alloc_sbuf_tensor(f"ysb{i}", [128, D], BF16) for i in range(2)]
    fc1s = nc.alloc_sbuf_tensor("fc1s", [128, DC * HID], BF16)
    fc2s = nc.alloc_sbuf_tensor("fc2s", [128, HC * D], BF16)
    b1s = nc.alloc_sbuf_tensor("b1s", [128, HC], F32)
    b2r = nc.alloc_sbuf_tensor("b2r_s", [1, D], BF16)
    lsb = nc.alloc_sbuf_tensor("lsb", [128, D], BF16)
    onesb = nc.alloc_sbuf_tensor("onesb", [1, 128], BF16)
    onecol = nc.alloc_sbuf_tensor("onecol", [128, 1], F32)

    ph0 = nc.alloc_psum_tensor("ph0", [128, 512], F32)
    ph1 = nc.alloc_psum_tensor("ph1", [128, 512], F32)
    proute = nc.alloc_psum_tensor("proute", [128, 512], F32)
    ptw = nc.alloc_psum_tensor("ptw", [128, 512], F32)
    ptwb = ptw[:, :].bitcast(BF16)
    py = [nc.alloc_psum_tensor(f"py{i}", [128, D], F32) for i in range(2)]

    p = Plan()
    sld = p.sem("sld"); swt = p.sem("swt")
    sng = p.sem("sng")
    swg = [p.sem(f"swg{i}") for i in range(8)]
    sy = [p.sem("sy0"), p.sem("sy1")]
    sstg = p.sem("sstg")
    pe = p.sem("pe"); ve = p.sem("ve"); se = p.sem("se"); ge = p.sem("ge")

    p.o("sync", lambda e: e.dma_start(out=ea[:, :], in_=meta_in), inc=sld, k=16)
    p.o("sync", lambda e: e.dma_start(out=eid[:, :], in_=eid_in), inc=sld, k=16)
    p.o("sync", lambda e: e.dma_start(out=idn[:, :], in_=idn_in), inc=sld, k=16)
    p.o("sync", lambda e: e.dma_start(out=b1s[:, :], in_=b1_in), inc=sld, k=16)
    sld_all = sld.count
    swt2 = p.sem("swt2")
    sls = p.sem("sls")
    p.o("gpsimd", lambda e: e.dma_start(out=toks[0][0:1, 0:D], in_=ls_in), inc=sls, k=16)
    p.o("gpsimd", lambda e: e.dma_start(out=b2r[:, :], in_=b2_in), inc=swt2, k=16)
    for cc in range(DC):
        p.o("gpsimd", lambda e, cc=cc: e.dma_start(
            out=fc1s[:, cc * HID:(cc + 1) * HID], in_=fc1_in[cc * 128:(cc + 1) * 128, :]),
            inc=swt, k=16)
    for hc in range(HC):
        p.o("gpsimd", lambda e, hc=hc: e.dma_start(
            out=fc2s[:, hc * D:(hc + 1) * D], in_=fc2_in[hc * 128:(hc + 1) * 128, :]),
            inc=swt2, k=16)
    swt_all = swt.count
    swt2_all = swt2.count

    p.o("gpsimd", lambda e: e.iota(rel[:, :], pattern=[[-1, 128]], base=0,
                                   channel_multiplier=1, allow_small_or_imprecise_dtypes=True), inc=ge)
    ge_rel = ge.count
    p.o("gpsimd", lambda e: e.iota(qabs[:, :], pattern=[[1, 128]], base=0,
                                   channel_multiplier=0, allow_small_or_imprecise_dtypes=True), inc=ge)
    p.o("gpsimd", lambda e: e.iota(nidx[:, :], pattern=[[128, A]], base=0,
                                   channel_multiplier=1, allow_small_or_imprecise_dtypes=True), inc=ge)
    p.o("gpsimd", lambda e: e.iota(scolg[:, :], pattern=[[128, NST]], base=0,
                                   channel_multiplier=1, allow_small_or_imprecise_dtypes=True), inc=ge)
    p.o("gpsimd", lambda e: e.memset(onesb[:, :], 1.0), inc=ge)
    p.o("gpsimd", lambda e: e.memset(onecol[:, :], 1.0), inc=ge)
    p.o("gpsimd", lambda e: e.memset(tokf[:, :], 0.0), inc=ge)
    ge_c = ge.count
    p.o("vector", lambda e: e.tensor_single_scalar(utri[:, :], rel[:, :], 0.0, OP.is_le),
        wait=[(ge, ge_rel)], inc=ve)
    p.o("vector", lambda e: e.tensor_single_scalar(ltri[:, :], rel[:, :], 0.0, OP.is_lt), inc=ve)
    p.o("vector", lambda e: e.tensor_single_scalar(shft[:, :], rel[:, :], -1.0, OP.is_equal), inc=ve)
    p.o("vector", lambda e: e.tensor_single_scalar(idnf[:, :], rel[:, :], 0.0, OP.is_equal), inc=ve)
    p.o("vector", lambda e: e.memset(lful[:, :], 1.0), inc=ve)
    ve_c = ve.count

    eview = lambda q: ea[:, :].rearrange("p (a q) -> p a q", q=4)[:, :, q:q + 1].squeeze(2)
    for k in range(2):
        p.o("vector", lambda e, k=k: e.tensor_scalar(
            itk[k][:, :], eview(k), eid[:, 0:1], None, OP.is_equal),
            wait=[(sld, sld_all), (ve, ve_c)], inc=ve)
    ve_it = ve.count
    p.o("vector", lambda e: e.memset(scb[:, :], 0.0), inc=ve)
    ve_scb0 = ve.count
    t2ps = {0: ph0, 1: ph1}
    for k in range(2):
        p.o("tensor", lambda e, k=k: e.matmul(
            t2ps[k][0:A, 0:128], itk[k][:, :], utri[:, :], start=True, stop=True),
            wait=[(ve, ve_it)], inc=pe)
    pe_t2 = pe.count
    for k in range(2):
        p.o("vector", lambda e, k=k: e.tensor_copy(scb[0:A, k:k + 1], t2ps[k][0:A, 127:128]),
            wait=[(pe, pe_t2), (ve, ve_scb0)], inc=ve)
    ve_scb = ve.count
    p.o("tensor", lambda e: e.matmul(proute[:, 0:2], ltri[:, :], scb[:, :], start=True, stop=True),
        wait=[(ve, ve_scb)], inc=pe)
    p.o("vector", lambda e: e.tensor_copy(offs[:, 0:2], proute[:, 0:2]), wait=[(pe, pe.count)], inc=ve)
    p.o("tensor", lambda e: e.matmul(proute[:, 2:4], lful[:, :], scb[:, :], start=True, stop=True),
        wait=[(ve, ve.count)], inc=pe)
    p.o("vector", lambda e: e.tensor_copy(offs[:, 2:4], proute[:, 2:4]), wait=[(pe, pe.count)], inc=ve)
    p.o("tensor", lambda e: e.matmul(proute[:, 4:6], shft[:, :], scb[:, :], start=True, stop=True),
        wait=[(ve, ve.count)], inc=pe)
    p.o("vector", lambda e: e.tensor_copy(dcb[:, 1:2], proute[:, 4:5]), wait=[(pe, pe.count)], inc=ve)
    p.o("vector", lambda e: e.tensor_copy(dcb[:, 3:4], proute[:, 5:6]), wait=[(ve, ve.count)], inc=ve)
    p.o("vector", lambda e: e.memset(dcb[:, 0:1], 1.0), inc=ve)
    p.o("vector", lambda e: e.memset(dcb[:, 2:3], 1.0), inc=ve)
    p.o("vector", lambda e: e.tensor_tensor(offs[:, 4:5], offs[:, 1:2], offs[:, 2:3], OP.add),
        wait=[(ve, ve.count)], inc=ve)
    ve_offs = ve.count

    for k in range(2):
        cbcol = (offs[:, 0:1], offs[:, 4:5])[k]
        p.o("vector", lambda e, k=k, cbcol=cbcol: e.tensor_scalar(
            posg[k][0:A, :], t2ps[k][0:A, 0:128], cbcol[0:A, :], -1.0, OP.add, OP.add),
            wait=[(ve, ve_offs)], inc=ve)
        p.o("vector", lambda e, k=k: e.tensor_scalar(
            lpg[k][0:A, :], t2ps[k][0:A, 0:128], -1.0, None, OP.add),
            wait=[(ve, ve.count)], inc=ve)
    ve_pos = ve.count
    for k in range(2):
        p.o("tensor", lambda e, k=k: e.transpose(ptwb[0:A, k * 128:(k + 1) * 128], itk[k][:, :], idn[:, :]),
            wait=[(ve, ve_pos), (pe, pe.count)], inc=pe)
        p.o("vector", lambda e, k=k: e.tensor_copy(itab[k][0:A, :], ptwb[0:A, k * 128:(k + 1) * 128]),
            wait=[(pe, pe.count)], inc=ve)
        p.o("vector", lambda e, k=k: e.scalar_tensor_tensor(
            lpg[k][0:A, :], itab[k][0:A, :], 1e6, lpg[k][0:A, :], OP.mult, OP.add),
            wait=[(ve, ve.count)], inc=ve)
        p.o("vector", lambda e, k=k: e.tensor_scalar(
            lpg[k][0:A, :], lpg[k][0:A, :], -1e6, None, OP.add),
            wait=[(ve, ve.count)], inc=ve)
        ve_lpm = ve.count
        p.o("tensor", lambda e, k=k: e.transpose(
            proute[:, 384:384 + A], lpg[k][0:A, :], idnf[0:A, 0:A]),
            wait=[(ve, ve_lpm), (pe, pe.count)], inc=pe)
        p.o("vector", lambda e, k=k: e.tensor_copy(lpt[k][:, 0:A], proute[:, 384:384 + A]),
            wait=[(pe, pe.count)], inc=ve)
        p.o("tensor", lambda e, k=k: e.transpose(
            proute[:, 384:384 + A], posg[k][0:A, :], idnf[0:A, 0:A]),
            wait=[(ve, ve.count), (pe, pe.count)], inc=pe)
        p.o("vector", lambda e, k=k: e.tensor_copy(
            rhs3[k][:, :].rearrange("p (a q) -> p a q", q=3)[:, :, 1:2].squeeze(2),
            proute[:, 384:384 + A]),
            wait=[(pe, pe.count)], inc=ve)
    ve_tr = ve.count
    for k in range(2):
        p.o("vector", lambda e, k=k: e.tensor_copy(
            rhs3[k][:, :].rearrange("p (a q) -> p a q", q=3)[:, :, 0:1].squeeze(2), nidx[:, :]),
            wait=[(ve, ve_tr)], inc=ve)
        p.o("vector", lambda e, k=k: e.memset(
            rhs3[k][:, :].rearrange("p (a q) -> p a q", q=3)[:, :, 2:3].squeeze(2), 1.0), inc=ve)
    ve_rhs = ve.count

    C1, O1, C2, O2 = 0, NST, 2 * NST, 3 * NST
    CNTC, OFFSEL, T1T, GG = 4 * NST, 5 * NST, 6 * NST, 7 * NST
    pe_cnt0 = pe.count
    mscr6 = mscr + [lpg[0], lpg[1], posg[0]]
    cnt_pe = {}
    for r in range(NST):
        for k in range(2):
            idx = 2 * r + k
            cbcol = (offs[:, 0:1], offs[:, 4:5])[k]
            ms = mscr6[idx % 6]
            cw_ = [(ve, ve.count)]
            if idx - 6 in cnt_pe:
                cw_.append((pe, cnt_pe[idx - 6]))
            elif idx < 6:
                cw_.append((pe, pe_cnt0))
            p.o("vector", lambda e, r=r, k=k, idx=idx, cbcol=cbcol: e.tensor_scalar(
                cbr[:, idx % 4:idx % 4 + 1], cbcol, float(-128 * r), None, OP.add),
                wait=cw_, inc=ve)
            p.o("vector", lambda e, idx=idx, ms=ms: e.tensor_scalar(
                ms[:, :].bitcast(BF16)[:, 0:128], qabs[:, 0:128],
                cbr[:, idx % 4:idx % 4 + 1], None, OP.is_ge),
                wait=[(ve, ve.count)], inc=ve)
            p.o("tensor", lambda e, r=r, k=k, ms=ms: e.matmul(
                ph0[:, 4 * (r % 8) + 2 * k: 4 * (r % 8) + 2 * k + 2],
                ms[:, :].bitcast(BF16)[0:A, 0:128], dcb[0:A, 2 * k:2 * k + 2],
                start=True, stop=True),
                wait=[(ve, ve.count)], inc=pe)
            cnt_pe[idx] = pe.count
        if r % 8 == 7 or r == NST - 1:
            lo = (r // 8) * 8
            p.o("vector", lambda e, r=r, lo=lo: e.tensor_copy(
                cnt[:, 4 * lo:4 * (r + 1)], ph0[:, 0:4 * (r + 1 - lo)]),
                wait=[(pe, pe.count)], inc=ve)
    ve_cnt = ve.count
    cview = lambda q: cnt[:, :].rearrange("p (r q) -> p r q", q=4)[:, :, q:q + 1].squeeze(2)
    p.o("vector", lambda e: e.tensor_tensor(cwork[:, CNTC:CNTC + NST], cview(0), cview(2), OP.add),
        wait=[(ve, ve_cnt)], inc=ve)
    TMPA = 8 * NST
    p.o("vector", lambda e: e.tensor_scalar(cwork[:, TMPA:TMPA + NST], cview(3), offs[:, 2:3],
                                            None, OP.add), inc=ve)
    p.o("vector", lambda e: e.tensor_tensor(cwork[:, TMPA:TMPA + NST], cwork[:, TMPA:TMPA + NST],
                                            cview(1), OP.subtract),
        wait=[(ve, ve.count)], inc=ve)
    p.o("vector", lambda e: e.tensor_scalar(cwork[:, T1T:T1T + NST], scolg[:, :], offs[:, 2:3],
                                            None, OP.is_ge), inc=ve)
    p.o("vector", lambda e: e.tensor_tensor(cwork[:, TMPA:TMPA + NST], cwork[:, TMPA:TMPA + NST],
                                            cwork[:, T1T:T1T + NST], OP.mult),
        wait=[(ve, ve.count)], inc=ve)
    p.o("vector", lambda e: e.tensor_tensor(cwork[:, OFFSEL:OFFSEL + NST], cview(1),
                                            cwork[:, TMPA:TMPA + NST], OP.add),
        wait=[(ve, ve.count)], inc=ve)
    p.o("vector", lambda e: e.tensor_tensor(cwork[:, GG:GG + NST], scolg[:, :],
                                            cwork[:, OFFSEL:OFFSEL + NST], OP.subtract),
        wait=[(ve, ve.count)], inc=ve)
    p.o("vector", lambda e: e.tensor_scalar(cwork[:, GG:GG + NST], cwork[:, GG:GG + NST],
                                            float(6 * A), None, OP.mult),
        wait=[(ve, ve.count)], inc=ve)
    p.o("vector", lambda e: e.tensor_scalar(cwork[:, T1T:T1T + NST], cwork[:, CNTC:CNTC + NST],
                                            3.0, -3.0, OP.mult, OP.add),
        wait=[(ve, ve.count)], inc=ve)
    p.o("vector", lambda e: e.tensor_tensor(cwork[:, GG:GG + NST], cwork[:, GG:GG + NST],
                                            cwork[:, T1T:T1T + NST], OP.add),
        wait=[(ve, ve.count)], inc=ve)
    p.o("vector", lambda e: e.tensor_copy(gcolu[:, :], cwork[:, GG:GG + NST]),
        wait=[(ve, ve.count)], inc=ve)
    ve_g = ve.count
    pe_cnt_end = pe.count

    nch = 2 * A
    ev_m = (ve, ve_rhs)
    pe_m = {}
    for batch in range(0, nch, 128):
        bn = min(128, nch - batch)
        for i in range(bn):
            ch = batch + i
            k, a = ch // A, ch % A
            ms = mscr6[ch % 6]
            bw = [(ve, ve_rhs)]
            if ch - 6 in pe_m:
                bw.append((pe, pe_m[ch - 6]))
            else:
                bw.append((pe, pe_cnt_end))
            p.o("vector", lambda e, k=k, a=a, ms=ms: e.tensor_scalar(
                ms[:, :], qabs[:, 0:128], lpt[k][:, a:a + 1], None, OP.is_equal),
                wait=bw, inc=ve)
            p.o("tensor", lambda e, k=k, a=a, i=i, ms=ms: e.matmul(
                proute[:, 3 * i:3 * i + 3], ms[:, :], rhs3[k][:, 3 * a:3 * a + 3],
                start=True, stop=True),
                wait=[(ve, ve.count), ev_m], inc=pe)
            pe_m[ch] = pe.count
        p.o("vector", lambda e, batch=batch, bn=bn: e.tensor_copy(
            stg[:, 3 * batch:3 * (batch + bn)], proute[:, 0:3 * bn]),
            wait=[(pe, pe.count)], inc=ve)
        ev_m = (ve, ve.count)
    ve_stg = ve.count
    p.o("scalar", lambda e: e.dma_start(
        out=stg_d.rearrange("(p f) o -> p (f o)", p=128), in_=stg[:, :]),
        wait=[(ve, ve_stg)], inc=sstg, k=16)

    for r in range(NST):
        p.o("gpsimd", lambda e, r=r: e.indirect_dma_start(
            out=tokf[:, r:r + 1], out_offset=None,
            in_=stg_d,
            in_offset=bass.IndirectOffsetOnAxis(gcolu[:, r:r + 1], axis=0),
            bounds_check=128 * 6 * A - 1, oob_is_err=False),
            wait=[(ve, ve_g), (sstg, 16), (ge, ge_c)], inc=sng, k=16)
    ve_tok_r = {}
    for r in range(NST):
        p.o("vector", lambda e, r=r: e.tensor_copy(toku[:, r:r + 1], tokf[:, r:r + 1]),
            wait=[(sng, 16 * (r + 1))], inc=ve)
        ve_tok_r[r] = ve.count

    nlsb = (D + 511) // 512
    for i in range(nlsb):
        c0, cn = i * 512, min(512, D - i * 512)
        p.o("tensor", lambda e, c0=c0, cn=cn: e.matmul(
            py[0][:, c0:c0 + cn], onesb[:, :], toks[0][0:1, c0:c0 + cn], start=True, stop=True),
            wait=[(sls, 16), (ge, ge_c)], inc=pe)
    p.o("vector", lambda e: e.tensor_copy(lsb[:, :], py[0][:, 0:D]), wait=[(pe, pe.count)], inc=ve)
    ve_lsb = ve.count
    pe_lsb = pe.count

    NTC = NST * 128 // 512
    assert NST * 128 % 512 == 0 or NTC * 512 < NST * 128
    if NST * 128 % 512 != 0:
        NTC += 1
    pe_prev_s1 = (pe, 0)
    pe_prev_s2 = (pe, 0)
    se_gel = {0: 0, 1: 0}
    ve_epi = {}
    ve_py = {0: (ve, 0), 1: (ve, 0)}
    ve_ptw = (ve, ve.count)
    sw_g = {}
    for tc in range(NTC):
        tiles = [r for r in range(tc * 4, min(tc * 4 + 4, NST))]
        for i, r in enumerate(tiles):
            slot = (tc % 2) * 4 + i
            w = [(ve, ve_tok_r[r]), (ge, ge_c), (pe, pe_lsb)]
            if slot in ve_epi:
                w.append(ve_epi[slot])
            p.o("gpsimd", lambda e, r=r, slot=slot: e.indirect_dma_start(
                out=toks[slot][:, :], out_offset=None,
                in_=xf_in,
                in_offset=bass.IndirectOffsetOnAxis(toku[:, r:r + 1], axis=0),
                bounds_check=N - 1, oob_is_err=False),
                wait=w, inc=swg[slot], k=16)
            sw_g[slot] = (swg[slot], swg[slot].count)
            for cc in range(DC):
                pw = [sw_g[slot], pe_prev_s1]
                if cc == 0:
                    pw.append(ve_ptw)
                p.o("tensor", lambda e, cc=cc, slot=slot: e.transpose(
                    ptwb[:, cc * 128:(cc + 1) * 128], toks[slot][:, cc * 128:(cc + 1) * 128],
                    idn[:, :]), wait=pw, inc=pe)
            tout = tokst[:, :].rearrange("p (c w) -> p c w", w=512)[:, 0:DC,
                                                                   i * 128:(i + 1) * 128]
            tin = ptwb[:, 0:DC * 128].rearrange("p (c j) -> p c j", j=128)
            p.o("vector", lambda e, tout=tout, tin=tin: e.tensor_copy(tout, tin),
                wait=[(pe, pe.count), (ve, ve.count)], inc=ve)
            ve_ptw = (ve, ve.count)
        ve_tT = ve.count
        ncols = len(tiles) * 128
        se_s1 = {}
        for j in range(HC):
            php = (ph0, ph1)[j % 2]
            for cc in range(DC):
                w2 = [(ve, ve_tT), (swt, swt_all)]
                if cc == 0:
                    if se_gel[j % 2]:
                        w2.append((se, se_gel[j % 2]))
                    w2.append(pe_prev_s2)
                    if tc == 0 and j == 0:
                        w2.append((sstg, 16))
                p.o("tensor", lambda e, j=j, cc=cc, php=php, ncols=ncols: e.matmul(
                    php[:, 0:ncols], fc1s[:, cc * HID + j * 128: cc * HID + (j + 1) * 128],
                    tokst[:, cc * 512: cc * 512 + ncols],
                    start=(cc == 0), stop=(cc == DC - 1)),
                    wait=w2, inc=(pe if cc == DC - 1 else None))
            p.o("scalar", lambda e, j=j, php=php, ncols=ncols: e.activation(
                hts[:, j * 512: j * 512 + ncols], php[:, 0:ncols], AF.Gelu,
                bias=b1s[:, j:j + 1], scale=1.0),
                wait=[(pe, pe.count), (sld, sld_all), pe_prev_s2], inc=se)
            se_s1[j] = se.count
            se_gel[j % 2] = se.count
        pe_prev_s1 = (pe, pe.count)
        for i, r in enumerate(tiles):
            pyp = py[i % 2]
            for ci in range(nlsb):
                c0, cn = ci * 512, min(512, D - ci * 512)
                for j in range(HC):
                    w3 = [(se, se_s1[j])]
                    if j == 0:
                        w3.append(ve_py[i % 2])
                        w3.append((swt2, swt2_all))
                    p.o("tensor", lambda e, j=j, c0=c0, cn=cn, i=i, pyp=pyp: e.matmul(
                        pyp[:, c0:c0 + cn], hts[:, j * 512 + i * 128: j * 512 + (i + 1) * 128],
                        fc2s[:, j * D + c0: j * D + c0 + cn],
                        start=(j == 0), stop=False),
                        wait=w3, inc=None)
                p.o("tensor", lambda e, c0=c0, cn=cn, pyp=pyp: e.matmul(
                    pyp[:, c0:c0 + cn], onesb[:, :], b2r[:, c0:c0 + cn],
                    start=False, stop=True), inc=(pe if ci == nlsb - 1 else None))
            pe_fc2 = pe.count
            yb = ysb[i % 2]
            slot = (tc % 2) * 4 + i
            p.o("vector", lambda e, pyp=pyp, yb=yb: e.tensor_tensor(
                yb[:, :], pyp[:, 0:D], lsb[:, :], OP.mult),
                wait=[(pe, pe_fc2), (ve, ve_lsb), (sy[i % 2], sy[i % 2].count)], inc=ve)
            p.o("vector", lambda e, yb=yb, slot=slot: e.tensor_tensor(
                yb[:, :], yb[:, :], toks[slot][:, :], OP.add),
                wait=[(ve, ve.count)], inc=ve)
            ve_y = ve.count
            ve_epi[slot] = (ve, ve_y)
            ve_py[i % 2] = (ve, ve_y)
            p.o("sync", lambda e, r=r, yb=yb: e.dma_start(
                out=y_out[r * 128:(r + 1) * 128, :], in_=yb[:, :]),
                wait=[(ve, ve_y)], inc=sy[i % 2], k=16)
        pe_prev_s2 = (pe, pe.count)

    p.barrier("sync", (sy[0], sy[0].count), (sy[1], sy[1].count))
    p.emit(nc)
    return nc


def build_l3(cfg: Cfg):
    S, D, NT, DC = cfg.S, cfg.D, cfg.NT, cfg.DC
    N, A, CAP, NST, SLOT = cfg.N, cfg.A, cfg.CAP, cfg.NST, cfg.SLOT
    E, BIG = cfg.E, cfg.BIG
    ESLOT = E * SLOT
    nc = bass.Bass("TRN2", target_bir_lowering=False)

    x2_in = nc.dram_tensor("x2", [S, D], BF16, kind="ExternalInput").ap()
    meta_in = nc.dram_tensor("meta_all", [128, 4 * A], F32, kind="ExternalInput").ap()
    y_in = nc.dram_tensor("y_all", [ESLOT, D], BF16, kind="ExternalInput").ap()
    sel_in = nc.dram_tensor("sel", [128, NT], F32, kind="ExternalInput").ap()
    idn_in = nc.dram_tensor("idn", [128, 128], BF16, kind="ExternalInput").ap()

    out_o = nc.dram_tensor("out", [S, D], F32, kind="ExternalOutput").ap()

    ea = nc.alloc_sbuf_tensor("ea", [128, 4 * A], F32)
    selb = nc.alloc_sbuf_tensor("selb", [128, NT], F32)
    idn = nc.alloc_sbuf_tensor("idn_s", [128, 128], BF16)
    idnf = nc.alloc_sbuf_tensor("idnf", [128, 128], F32)
    rel = nc.alloc_sbuf_tensor("rel", [128, 128], F32)
    utri = nc.alloc_sbuf_tensor("utri", [128, 128], BF16)
    ltri = nc.alloc_sbuf_tensor("ltri", [128, 128], BF16)
    lful = nc.alloc_sbuf_tensor("lful", [128, 128], BF16)
    xb = nc.alloc_sbuf_tensor("xb", [128, NT * D], BF16)
    itg = nc.alloc_sbuf_tensor("itg", [128, A], BF16)
    itab = nc.alloc_sbuf_tensor("itab", [128, 128], BF16)
    scb = nc.alloc_sbuf_tensor("scb", [128, 16], BF16)
    offsl = nc.alloc_sbuf_tensor("offsl", [128, 32], F32)
    posw = nc.alloc_sbuf_tensor("posw", [128, 128], F32)
    kpw = nc.alloc_sbuf_tensor("kpw", [128, 128], F32)
    slotg = [nc.alloc_sbuf_tensor(f"slotg{k}", [128, 128], F32) for k in range(2)]
    slotT = nc.alloc_sbuf_tensor("slotT", [128, 2 * NT], F32)
    slotU = nc.alloc_sbuf_tensor("slotU", [128, 2 * NT], U32)
    keptT = nc.alloc_sbuf_tensor("keptT", [128, 2 * NT], F32)
    wT = nc.alloc_sbuf_tensor("wT", [128, 2 * NT], F32)
    gb = [nc.alloc_sbuf_tensor(f"gb{i}", [128, D], BF16) for i in range(8)]
    tsc = [nc.alloc_sbuf_tensor(f"tsc{i}", [128, D], F32) for i in range(2)]
    osb = [nc.alloc_sbuf_tensor(f"osb{i}", [128, D], F32) for i in range(2)]

    itg16 = nc.alloc_sbuf_tensor("itg16", [128, 16 * A], BF16)
    t2s = nc.alloc_sbuf_tensor("t2s", [128, 16 * 128], F32)
    itab16 = nc.alloc_sbuf_tensor("itab16", [128, 16 * 128], BF16)

    ph0 = nc.alloc_psum_tensor("ph0", [128, 128], F32)
    ph1 = nc.alloc_psum_tensor("ph1", [128, 128], F32)
    ptA = nc.alloc_psum_tensor("ptA", [128, 128], BF16)
    ptB = nc.alloc_psum_tensor("ptB", [128, 128], BF16)

    p = Plan()
    sld = p.sem("sld"); sx = p.sem("sx")
    sg = [p.sem(f"sg{i}") for i in range(8)]
    so = [p.sem("so0"), p.sem("so1")]
    pe = p.sem("pe"); ve = p.sem("ve"); se = p.sem("se"); ge = p.sem("ge")

    p.o("sync", lambda e: e.dma_start(out=ea[:, :], in_=meta_in), inc=sld, k=16)
    p.o("sync", lambda e: e.dma_start(out=selb[:, :], in_=sel_in), inc=sld, k=16)
    p.o("sync", lambda e: e.dma_start(out=idn[:, :], in_=idn_in), inc=sld, k=16)
    sld_all = sld.count
    for t in range(NT):
        p.o("sync", lambda e, t=t: e.dma_start(
            out=xb[:, t * D:(t + 1) * D], in_=x2_in[t * 128:(t + 1) * 128, :]), inc=sx, k=16)
    sx_all = sx.count
    p.o("gpsimd", lambda e: e.iota(rel[:, :], pattern=[[-1, 128]], base=0,
                                   channel_multiplier=1, allow_small_or_imprecise_dtypes=True), inc=ge)
    ge_c = ge.count
    p.o("vector", lambda e: e.tensor_single_scalar(utri[:, :], rel[:, :], 0.0, OP.is_le),
        wait=[(ge, ge_c)], inc=ve)
    p.o("vector", lambda e: e.tensor_single_scalar(ltri[:, :], rel[:, :], 0.0, OP.is_lt), inc=ve)
    p.o("vector", lambda e: e.tensor_single_scalar(idnf[:, :], rel[:, :], 0.0, OP.is_equal), inc=ve)
    p.o("vector", lambda e: e.memset(lful[:, :], 1.0), inc=ve)
    p.o("vector", lambda e: e.memset(scb[:, :], 0.0), inc=ve)
    p.o("vector", lambda e: e.memset(slotg[0][:, :], BIG), inc=ve)
    p.o("vector", lambda e: e.memset(slotg[1][:, :], BIG), inc=ve)
    ve_c = ve.count

    eview = lambda q: ea[:, :].rearrange("p (a q) -> p a q", q=4)[:, :, q:q + 1].squeeze(2)

    ve_cmp = {}
    for i in range(2 * E):
        k, ex = divmod(i, E)
        p.o("vector", lambda e, k=k, ex=ex, i=i: e.tensor_single_scalar(
            itg16[:, i * A:(i + 1) * A], eview(k), float(ex), OP.is_equal),
            wait=[(sld, sld_all), (ve, ve_c)], inc=ve)
        ve_cmp[i] = ve.count
    ve_ev = {}
    for i in range(2 * E):
        php = (ph0, ph1)[i % 2]
        ptX = (ptA, ptB)[i % 2]
        w = [(ve, ve_cmp[i])]
        if i - 2 in ve_ev:
            w.append((ve, ve_ev[i - 2]))
        p.o("tensor", lambda e, i=i, php=php: e.matmul(
            php[0:A, :], itg16[:, i * A:(i + 1) * A], utri[:, :], start=True, stop=True),
            wait=w, inc=pe)
        p.o("tensor", lambda e, i=i, ptX=ptX: e.transpose(
            ptX[0:A, :], itg16[:, i * A:(i + 1) * A], idn[:, :]), inc=pe)
        pe_i = pe.count
        p.o("vector", lambda e, i=i, php=php: e.tensor_copy(
            t2s[0:A, i * 128:(i + 1) * 128], php[0:A, :]), wait=[(pe, pe_i)], inc=ve)
        p.o("vector", lambda e, i=i, ptX=ptX: e.tensor_copy(
            itab16[0:A, i * 128:(i + 1) * 128], ptX[0:A, :]), inc=ve)
        ve_ev[i] = ve.count
    ve_t2 = ve.count
    p.o("vector", lambda e: e.tensor_copy(
        scb[:, 0:16],
        t2s[:, :].rearrange("p (i c) -> p i c", c=128)[:, :, 127:128].squeeze(2)),
        wait=[(ve, ve_t2)], inc=ve)
    ve_scb = ve.count
    p.o("tensor", lambda e: e.matmul(ph1[:, 0:16], ltri[:, :], scb[:, :], start=True, stop=True),
        wait=[(ve, ve_scb)], inc=pe)
    p.o("vector", lambda e: e.tensor_copy(offsl[:, 0:16], ph1[:, 0:16]), wait=[(pe, pe.count)], inc=ve)
    p.o("tensor", lambda e: e.matmul(ph1[:, 0:16], lful[:, :], scb[:, :], start=True, stop=True),
        wait=[(ve, ve.count)], inc=pe)
    p.o("vector", lambda e: e.tensor_copy(offsl[:, 16:32], ph1[:, 0:16]), wait=[(pe, pe.count)], inc=ve)
    ve_off = ve.count

    for k in range(2):
        for ex in range(E):
            i = k * E + ex
            t2v = t2s[:, i * 128:(i + 1) * 128]
            itv = itab16[:, i * 128:(i + 1) * 128]
            if k == 0:
                p.o("vector", lambda e, ex=ex, t2v=t2v: e.tensor_scalar(
                    posw[0:A, :], t2v[0:A, :], offsl[0:A, ex:ex + 1], -1.0, OP.add, OP.add),
                    wait=[(ve, ve_off)], inc=ve)
            else:
                p.o("vector", lambda e, ex=ex, t2v=t2v: e.tensor_scalar(
                    posw[0:A, :], t2v[0:A, :], offsl[0:A, E + ex:E + ex + 1], -1.0, OP.add, OP.add),
                    wait=[(ve, ve_off)], inc=ve)
                p.o("vector", lambda e, ex=ex: e.tensor_scalar(
                    posw[0:A, :], posw[0:A, :], offsl[0:A, 16 + ex:16 + ex + 1], None, OP.add),
                    inc=ve)
            p.o("vector", lambda e: e.tensor_single_scalar(kpw[0:A, :], posw[0:A, :],
                                                           float(CAP), OP.is_lt), inc=ve)
            p.o("vector", lambda e, itv=itv: e.tensor_tensor(kpw[0:A, :], kpw[0:A, :],
                                                             itv[0:A, :], OP.mult), inc=ve)
            p.o("vector", lambda e, ex=ex: e.tensor_scalar(
                posw[0:A, :], posw[0:A, :], float(ex * SLOT - BIG), None, OP.add), inc=ve)
            p.o("vector", lambda e: e.tensor_tensor(posw[0:A, :], posw[0:A, :], kpw[0:A, :],
                                                    OP.mult), inc=ve)
            p.o("vector", lambda e, k=k: e.tensor_tensor(slotg[k][0:A, :], slotg[k][0:A, :],
                                                         posw[0:A, :], OP.add), inc=ve)
    ve_slot = ve.count

    for k in range(2):
        p.o("tensor", lambda e, k=k: e.matmul(ph0[:, 0:NT], slotg[k][0:A, :], selb[0:A, :],
                                              start=True, stop=True),
            wait=[(ve, ve.count), (pe, pe.count)], inc=pe)
        p.o("vector", lambda e, k=k: e.tensor_copy(slotT[:, k * NT:(k + 1) * NT], ph0[:, 0:NT]),
            wait=[(pe, pe.count)], inc=ve)
        p.o("tensor", lambda e, k=k: e.transpose(ph1[0:A, :], eview(2 + k), idnf[:, :]),
            wait=[(ve, ve.count), (sld, sld_all)], inc=pe)
        pe_w = pe.count
        p.o("vector", lambda e: e.tensor_copy(posw[0:A, :], ph1[0:A, 0:128]),
            wait=[(pe, pe_w)], inc=ve)
        p.o("tensor", lambda e, k=k: e.matmul(ph0[:, 0:NT], posw[0:A, :], selb[0:A, :],
                                              start=True, stop=True),
            wait=[(ve, ve.count)], inc=pe)
        p.o("vector", lambda e, k=k: e.tensor_copy(wT[:, k * NT:(k + 1) * NT], ph0[:, 0:NT]),
            wait=[(pe, pe.count)], inc=ve)
    p.o("vector", lambda e: e.tensor_single_scalar(keptT[:, :], slotT[:, :], float(ESLOT), OP.is_lt),
        wait=[(ve, ve.count)], inc=ve)
    p.o("vector", lambda e: e.tensor_tensor(wT[:, :], wT[:, :], keptT[:, :], OP.mult),
        wait=[(ve, ve.count)], inc=ve)
    p.o("vector", lambda e: e.tensor_copy(slotU[:, :], slotT[:, :]), inc=ve)
    ve_wt = ve.count

    for bi in range(8):
        p.o("gpsimd", lambda e, bi=bi: e.memset(gb[bi][:, :], 0.0), inc=ge)
    ge_gb0 = ge.count
    ev_g = {i: None for i in range(8)}
    cons = {i: (ve, 0) for i in range(8)}
    tprev = {0: (ve, 0), 1: (ve, 0)}
    for t in range(NT):
        b1i = (t % 4) * 2
        b2i = b1i + 1
        for k, bi in ((0, b1i), (1, b2i)):
            w = [(ve, ve_wt), (ge, ge_gb0), cons[bi]]
            p.o("gpsimd", lambda e, t=t, k=k, bi=bi: e.indirect_dma_start(
                out=gb[bi][:, :], out_offset=None,
                in_=y_in,
                in_offset=bass.IndirectOffsetOnAxis(slotU[:, k * NT + t:k * NT + t + 1], axis=0),
                bounds_check=ESLOT - 1, oob_is_err=False),
                wait=w, inc=sg[bi], k=16)
            ev_g[bi] = (sg[bi], sg[bi].count)
        par = t % 2
        tb = tsc[par]
        ob = osb[par]
        p.o("scalar", lambda e, t=t, tb=tb, b1i=b1i: e.activation(
            tb[:, :], gb[b1i][:, :], AF.Copy, bias=0.0, scale=wT[:, t:t + 1]),
            wait=[ev_g[b1i], (ve, ve_wt), tprev[par], (se, se.count)], inc=se)
        se_t1 = se.count
        cons[b1i] = (se, se_t1)
        p.o("vector", lambda e, t=t, tb=tb, b2i=b2i: e.scalar_tensor_tensor(
            tb[:, :], gb[b2i][:, :], wT[:, NT + t:NT + t + 1], tb[:, :], OP.mult, OP.add),
            wait=[ev_g[b2i], (se, se_t1), (so[par], so[par].count), (ve, ve.count)], inc=ve)
        cons[b2i] = (ve, ve.count)
        p.o("vector", lambda e, t=t, tb=tb, ob=ob: e.tensor_tensor(
            ob[:, :], tb[:, :], xb[:, t * D:(t + 1) * D], OP.add),
            wait=[(sx, sx_all), (ve, ve.count)], inc=ve)
        ve_o = ve.count
        tprev[par] = (ve, ve_o)
        p.o("sync", lambda e, t=t, ob=ob: e.dma_start(
            out=out_o[t * 128:(t + 1) * 128, :], in_=ob[:, :]),
            wait=[(ve, ve_o)], inc=so[par], k=16)

    p.barrier("sync", (so[0], so[0].count), (so[1], so[1].count))
    p.emit(nc)
    return nc


_IDN = np.eye(128, dtype=np.float32).astype(ml_dtypes.bfloat16)


def kernel(L0_attn_x, L0_attn_W_kqv, L0_attn_m, L0_ffn_router_w, L0_ffn_router_b,
           L0_ffn_fc1_w, L0_ffn_fc1_b, L0_ffn_fc2_w, L0_ffn_fc2_b, L0_ffn_layer_scale):
    cfg = Cfg(S=2048, D=1024, H=16, B=8, HID=4096, E=8)
    B, NT, HC = cfg.B, cfg.NT, cfg.HC
    cores = list(range(8))

    x = np.ascontiguousarray(np.asarray(L0_attn_x, dtype=np.float32))
    wk = np.asarray(L0_attn_W_kqv, dtype=np.float32)
    m = np.ascontiguousarray(np.asarray(L0_attn_m, dtype=np.float32).reshape(cfg.H, 1))
    rw = np.asarray(L0_ffn_router_w, dtype=np.float32)
    rb = np.asarray(L0_ffn_router_b, dtype=np.float32)
    fc1 = np.asarray(L0_ffn_fc1_w, dtype=np.float32)
    fc1b = np.asarray(L0_ffn_fc1_b, dtype=np.float32)
    fc2 = np.asarray(L0_ffn_fc2_w, dtype=np.float32)
    fc2b = np.asarray(L0_ffn_fc2_b, dtype=np.float32)
    ls = np.asarray(L0_ffn_layer_scale, dtype=np.float32)

    wkt = _tile_wkqv(wk)
    rwt = np.ascontiguousarray(rw.T)
    rbr = np.ascontiguousarray(rb[None, :])

    nc1 = build_l1(cfg)
    in1 = [{"x": x[c], "wkqv_t": wkt, "m": m, "rwt": rwt, "rb": rbr, "idn": _IDN}
           for c in cores]
    r1 = run_bass_kernel_spmd(nc1, in1, core_ids=cores)
    metas = [r1.results[c]["meta"] for c in cores]
    xfs = [r1.results[c]["xfo"] for c in cores]
    x2s = [r1.results[c]["x2o"] for c in cores]

    meta_all = np.ascontiguousarray(np.concatenate(metas, axis=1))
    xf_all = np.ascontiguousarray(np.concatenate(xfs, axis=0))

    nc2 = build_l2(cfg)
    in2 = [{
        "meta_all": meta_all, "xf_all": xf_all,
        "fc1t": np.ascontiguousarray(fc1[c].T),
        "fc2t": np.ascontiguousarray(fc2[c].T),
        "b1t": np.ascontiguousarray(fc1b[c].reshape(HC, 128).T),
        "b2r": np.ascontiguousarray(fc2b[c][None, :]),
        "lsr": np.ascontiguousarray(ls[c][None, :]),
        "eid": np.full((128, 1), float(c), np.float32),
        "idn": _IDN,
    } for c in cores]
    r2 = run_bass_kernel_spmd(nc2, in2, core_ids=cores)
    y_all = np.ascontiguousarray(np.concatenate([r2.results[c]["y"] for c in cores], axis=0))

    nc3 = build_l3(cfg)
    in3 = []
    for c in cores:
        sel = np.zeros((128, NT), np.float32)
        for r in range(NT):
            sel[c * NT + r, r] = 1.0
        in3.append({"x2": x2s[c], "meta_all": meta_all, "y_all": y_all,
                    "sel": sel, "idn": _IDN})
    r3 = run_bass_kernel_spmd(nc3, in3, core_ids=cores)
    out = np.stack([r3.results[c]["out"] for c in cores], axis=0)
    return out


# version 24
# speedup vs baseline: 1.0029x; 1.0029x over previous
import math
import numpy as np
import ml_dtypes
import concourse.bass as bass
import concourse.mybir as mybir
from concourse.bass_utils import run_bass_kernel_spmd

F32 = mybir.dt.float32
BF16 = mybir.dt.bfloat16
I32 = mybir.dt.int32
U32 = mybir.dt.uint32
AF = mybir.ActivationFunctionType
OP = mybir.AluOpType
AX = mybir.AxisListType

ENGINES = ("sync", "tensor", "vector", "scalar", "gpsimd")


class Sem:
    __slots__ = ("name", "count")

    def __init__(self, name):
        self.name = name
        self.count = 0


class Plan:

    def __init__(self):
        self.ops = {e: [] for e in ENGINES}
        self.sems = {}
        self._last_wait = {e: {} for e in ENGINES}
        self.targets = {}

    def sem(self, name) -> Sem:
        if name not in self.sems:
            self.sems[name] = Sem(name)
        return self.sems[name]

    def _resolve_waits(self, eng, wait):
        wl = []
        for w in wait:
            s, v = (w if isinstance(w, tuple) else (w, w.count))
            if v <= 0:
                continue
            if v > self._last_wait[eng].get(s.name, -1):
                self._last_wait[eng][s.name] = v
                wl.append((s.name, v))
                self.targets.setdefault(s.name, set()).add(v)
        return wl

    def o(self, eng, fn, wait=(), inc=None, k=1):
        wl = self._resolve_waits(eng, wait)
        postv = None
        if inc is not None:
            inc.count += k
            postv = inc.count
        self.ops[eng].append((fn, wl, (inc.name if inc else None), k, postv))

    def barrier(self, eng, *sems):
        wl = self._resolve_waits(eng, sems)
        self.ops[eng].append((None, wl, None, 0, None))

    def emit(self, nc):
        handles = {s.name: nc.alloc_semaphore(name=s.name) for s in self.sems.values()}
        tsorted = {n: sorted(v) for n, v in self.targets.items()}
        emitted = {n: 0 for n in self.sems}
        with nc.Block() as blk:
            for eng in ENGINES:
                if not self.ops[eng]:
                    continue
                ops = self.ops[eng]

                def body(e, ops=ops):
                    for fn, wl, incname, k, postv in ops:
                        for sname, v in wl:
                            e.wait_ge(handles[sname], v)
                        if fn is None:
                            continue
                        ins = fn(e)
                        if incname is None:
                            continue
                        if k == 16:
                            ins.then_inc(handles[incname], k)
                            continue
                        tl = self.targets.get(incname)
                        if tl is not None and postv in tl:
                            gap = postv - emitted[incname]
                            emitted[incname] = postv
                            ins.then_inc(handles[incname], gap)

                getattr(blk, eng)(body)


class Cfg:
    def __init__(self, S=2048, D=1024, H=16, B=8, HID=4096, E=8):
        self.S, self.D, self.H, self.B, self.HID, self.E = S, D, H, B, HID, E
        self.hd = 64
        self.NT = S // 128
        self.DC = D // 128
        self.HC = HID // 128
        self.N = B * S
        self.A = self.N // 128
        self.scale = math.sqrt(D)
        self.K = 2
        self.CAP = int(1.2 * (self.N // E + 1))
        self.NST = (self.CAP + 127) // 128
        self.SLOT = self.NST * 128
        self.EPS = 1e-9
        self.LNEPS = 1e-5
        self.BIG = float(1 << 22)
        self.SH = S // 2


L1_UPTO = 99


def build_l1(cfg: Cfg):
    S, D, H, NT, DC, SH = cfg.S, cfg.D, cfg.H, cfg.NT, cfg.DC, cfg.SH
    HP = H // 2
    QW = min(512, S)
    QWP = 512
    QT = QW // 128
    nc = bass.Bass("TRN2", target_bir_lowering=False)

    x_in = nc.dram_tensor("x", [S, D], F32, kind="ExternalInput").ap()
    wkq_in = nc.dram_tensor("wkqv_t", [3 * DC * DC * 128, 128], F32, kind="ExternalInput").ap()
    m_in = nc.dram_tensor("m", [H, 1], F32, kind="ExternalInput").ap()
    rwt_in = nc.dram_tensor("rwt", [DC * 128, 8], F32, kind="ExternalInput").ap()
    rb_in = nc.dram_tensor("rb", [1, 8], F32, kind="ExternalInput").ap()
    idn_in = nc.dram_tensor("idn", [128, 128], BF16, kind="ExternalInput").ap()

    x2_out = nc.dram_tensor("x2o", [S, D], BF16, kind="ExternalOutput").ap()
    xf_out = nc.dram_tensor("xfo", [S, D], BF16, kind="ExternalOutput").ap()
    meta_out = nc.dram_tensor("meta", [128, NT * 4], F32, kind="ExternalOutput").ap()

    xb = nc.alloc_sbuf_tensor("xb", [128, NT * D], BF16)
    arena1 = nc.alloc_sbuf_tensor("arena1", [128, 4 * DC * S], mybir.dt.uint8)
    ab = arena1[:, :].bitcast(BF16)
    ct = ab[:, 0:DC * S]
    kTv = ab[:, DC * S:2 * DC * S]
    kTz = ab
    qT = nc.alloc_sbuf_tensor("qT", [128, DC * S], BF16)
    vb = nc.alloc_sbuf_tensor("vb", [128, NT * H * 65 + 128], BF16)
    wv = nc.alloc_sbuf_tensor("wv", [128, DC * 512], BF16)
    wsl = [nc.alloc_sbuf_tensor(f"wsl{i}", [128, DC * 128], BF16) for i in range(2)]
    idn = nc.alloc_sbuf_tensor("idn_s", [128, 128], BF16)
    xsc = [nc.alloc_sbuf_tensor(f"xsc{i}", [128, D], BF16) for i in range(2)]
    dump8 = [nc.alloc_sbuf_tensor(f"dump8{i}", [128, 8], F32) for i in range(2)]
    st_ = nc.alloc_sbuf_tensor("st_", [128, 20 * NT], F32)
    qgrid = nc.alloc_sbuf_tensor("qgrid", [128, S], BF16)
    mgrid = nc.alloc_sbuf_tensor("mgrid", [128, 128], F32)
    mb = nc.alloc_sbuf_tensor("mb", [128, 32], F32)

    mcol = nc.alloc_sbuf_tensor("mcol", [128, 1], BF16)
    mrow = nc.alloc_sbuf_tensor("mrow", [1, 128], BF16)
    onesb = nc.alloc_sbuf_tensor("onesb", [1, 128], BF16)
    nrlT2 = nc.alloc_sbuf_tensor("nrlT2", [128, H * 128], BF16)
    dump = nc.alloc_sbuf_tensor("dump", [128, D], BF16)
    pstrip = nc.alloc_sbuf_tensor("pstrip", [128, 8 * QW], BF16)
    biasall = nc.alloc_sbuf_tensor("biasall", [128, H * NT], F32)
    kvals = nc.alloc_sbuf_tensor("kvals", [128, NT], F32)
    nrl = nc.alloc_sbuf_tensor("nrl", [128, 32], F32)
    avt = nc.alloc_sbuf_tensor("avt", [128, 2 * QW], BF16)
    rcol = nc.alloc_sbuf_tensor("rcol", [128, 4], F32)
    lgt = nc.alloc_sbuf_tensor("lgt", [128, 8 * NT], F32)
    top8 = nc.alloc_sbuf_tensor("top8", [128, 8 * NT], F32)
    tpi = nc.alloc_sbuf_tensor("tpi", [128, 8 * NT], U32)
    tw = nc.alloc_sbuf_tensor("tw", [128, 13 * NT], F32)
    meta = nc.alloc_sbuf_tensor("meta_s", [128, 4 * NT], F32)
    rwt = nc.alloc_sbuf_tensor("rwt_s", [128, DC * 8], BF16)
    rbr = nc.alloc_sbuf_tensor("rbr", [1, 8], BF16)
    ccol = nc.alloc_sbuf_tensor("ccol", [128, 2], F32)

    pS0 = nc.alloc_psum_tensor("pS0", [128, 2 * QWP], F32)
    pS1 = nc.alloc_psum_tensor("pS1", [128, 2 * QWP], F32)
    pvs = [nc.alloc_psum_tensor(f"pv{i}", [128, QWP], F32) for i in range(4)]

    LNB = [("b0", pS0[:, 0:512]), ("b1", pS0[:, 512:1024]),
           ("b2", pS1[:, 0:512]), ("b3", pS1[:, 512:1024]),
           ("b4", pvs[0][:, 0:512]), ("b5", pvs[1][:, 0:512]),
           ("b6", pvs[2][:, 0:512]), ("b7", pvs[3][:, 0:512])]

    p = Plan()
    sxg = [p.sem(f"sxg{i}") for i in range(4)]
    swv = p.sem("swv"); sld = p.sem("sld")
    swkp = [p.sem("swk0"), p.sem("swk1")]
    sxfp = [p.sem("sxf0"), p.sem("sxf1")]
    scst = p.sem("scst")
    pe = p.sem("pe"); ve = p.sem("ve"); se = p.sem("se"); ge = p.sem("ge")
    sso = p.sem("sso")

    evac = {}

    p.o("sync", lambda e: e.dma_start(out=idn[:, :], in_=idn_in), inc=sld, k=16)
    p.o("gpsimd", lambda e: e.memset(mcol[:, :], 0.0), inc=ge)
    gm = ge.count
    scm = p.sem("scm")
    p.o("gpsimd", lambda e: e.dma_start(out=mcol[0:H, :], in_=m_in), wait=[(ge, gm)], inc=scm, k=16)
    p.o("gpsimd", lambda e: e.iota(mgrid[:, :], pattern=[[-1, 128]], base=0,
                                   channel_multiplier=1,
                                   allow_small_or_imprecise_dtypes=True), inc=ge)
    p.o("gpsimd", lambda e: e.iota(qgrid[:, :], pattern=[[1, S]], base=0,
                                   channel_multiplier=0,
                                   allow_small_or_imprecise_dtypes=True), inc=ge)
    p.o("gpsimd", lambda e: e.iota(kvals[:, :], pattern=[[128, NT]], base=0,
                                   channel_multiplier=1,
                                   allow_small_or_imprecise_dtypes=True), inc=ge)
    p.o("gpsimd", lambda e: e.memset(onesb[:, :], 1.0), inc=ge)
    p.o("gpsimd", lambda e: e.memset(ccol[:, 0:1], cfg.LNEPS), inc=ge)
    p.o("gpsimd", lambda e: e.memset(ccol[:, 1:2], 1.0 / cfg.scale), inc=ge)
    p.o("gpsimd", lambda e: e.memset(nrlT2[:, :], 0.0), inc=ge)
    ge_c = ge.count
    GSZ = (NT + 3) // 4
    slab_cnt = [0, 0]

    def slab_load(jt, slot, after):
        base = jt * DC * 128
        slab_cnt[slot] += 1
        p.o("gpsimd", lambda e, base=base, slot=slot: e.dma_start(
            out=wsl[slot][:, :].rearrange("p (c j) -> p c j", j=128),
            in_=wkq_in[base:base + DC * 128, :].rearrange("(c p) j -> p c j", p=128)),
            wait=[after] if after else [], inc=swkp[slot], k=16)
        return (swkp[slot], slab_cnt[slot] * 16)

    w0 = w1 = None
    for t in range(NT):
        p.o("gpsimd", lambda e, t=t: e.dma_start(
            out=xb[:, t * D:(t + 1) * D], in_=x_in[t * 128:(t + 1) * 128, :]),
            inc=sxg[t // GSZ], k=16)
        if t == GSZ - 1:
            w0 = slab_load(0, 0, None)
            w1 = slab_load(1, 1, None)
    xall = [(s, s.count) for s in sxg if s.count]
    VJ0 = 2 * DC
    swv2 = p.sem("swv2")
    NVC = (D + 511) // 512
    TPH = DC // NVC

    def load_wv_half(jc, sem, after=None, ccs=None):
        for cc in (range(DC) if ccs is None else ccs):
            for jj in range(TPH):
                jt = jc * TPH + jj
                base = ((VJ0 + jt) * DC + cc) * 128
                p.o("gpsimd", lambda e, cc=cc, jj=jj, base=base: e.dma_start(
                    out=wv[:, cc * 512 + jj * 128: cc * 512 + (jj + 1) * 128],
                    in_=wkq_in[base:base + 128, :]),
                    wait=[after] if after else [], inc=sem, k=16)

    p.o("gpsimd", lambda e: e.dma_start(
        out=rwt[:, :].rearrange("p (c e) -> p c e", e=8),
        in_=rwt_in.rearrange("(c p) e -> p c e", p=128)), inc=scst, k=16)
    p.o("gpsimd", lambda e: e.dma_start(out=rbr[:, :], in_=rb_in), inc=scst, k=16)
    p.o("vector", lambda e: e.tensor_single_scalar(mgrid[:, :], mgrid[:, :], 0.0, OP.is_gt),
        wait=[(ge, ge_c)], inc=ve)
    p.o("vector", lambda e: e.tensor_scalar(mgrid[:, :], mgrid[:, :], -1e30, None, OP.mult),
        wait=[(ve, ve.count)], inc=ve)
    p.o("vector", lambda e: e.memset(
        vb[:, 0:NT * H * 65].rearrange("p (t h d) -> p t h d", h=H, d=65)[:, :, :, 64:65], 1.0),
        inc=ve)
    p.o("vector", lambda e: e.memset(vb[:, NT * H * 65:NT * H * 65 + 128], 0.0), inc=ve)
    ve_vones = ve.count
    p.o("tensor", lambda e: e.transpose(pS0[:, 0:512].bitcast(BF16)[0:1, 0:128],
                                        mcol[:, :], idn[:, :]),
        wait=[(sld, 16), (scm, 16)], inc=pe)
    p.o("vector", lambda e: e.tensor_copy(mrow[:, :], pS0[:, 0:512].bitcast(BF16)[0:1, 0:128]),
        wait=[(pe, pe.count)], inc=ve)
    p.o("tensor", lambda e: e.matmul(pS0[:, 0:128], onesb[:, :], mrow[:, :], start=True, stop=True),
        wait=[(ve, ve.count), (ge, ge_c)], inc=pe)
    p.o("vector", lambda e: e.tensor_copy(mb[:, 0:H], pS0[:, 0:H]), wait=[(pe, pe.count)], inc=ve)
    evac["b0"] = (ve, ve.count)
    p.o("vector", lambda e: e.tensor_scalar(nrl[:, 0:H], mb[:, 0:H], 0.0, -1.0, OP.max, OP.mult),
        wait=[(ve, ve.count)], inc=ve)

    SUM, SQ, MEAN, EX2, MSQ, VAR, SD, RSTD, NMR = (i * NT for i in range(9))
    GS = min(8, DC)

    def ln_stats_tile(o, t, sw):
        p.o("vector", lambda e, t=t: e.reduce_sum(
            st_[:, o + SUM + t: o + SUM + t + 1], xb[:, t * D:(t + 1) * D], axis=AX.X),
            wait=[*sw, (ve, ve.count)], inc=ve)
        p.o("scalar", lambda e, t=t: e.activation(
            dump[:, :], xb[:, t * D:(t + 1) * D], AF.Square, bias=0.0, scale=1.0,
            accum_out=st_[:, o + SQ + t: o + SQ + t + 1]),
            wait=[*sw, (se, se.count)], inc=se)

    def ln_chain(o, tlo=0, thi=None):
        thi = NT if thi is None else thi
        p.o("vector", lambda e: e.tensor_scalar(
            st_[:, o + MEAN + tlo:o + MEAN + thi], st_[:, o + SUM + tlo:o + SUM + thi], 1.0 / D, None, OP.mult),
            wait=[(ve, ve.count), (se, se.count)], inc=ve)
        p.o("vector", lambda e: e.tensor_scalar(
            st_[:, o + EX2 + tlo:o + EX2 + thi], st_[:, o + SQ + tlo:o + SQ + thi], 1.0 / D, None, OP.mult), inc=ve)
        p.o("vector", lambda e: e.tensor_tensor(
            st_[:, o + MSQ + tlo:o + MSQ + thi], st_[:, o + MEAN + tlo:o + MEAN + thi],
            st_[:, o + MEAN + tlo:o + MEAN + thi], OP.mult), wait=[(ve, ve.count)], inc=ve)
        p.o("vector", lambda e: e.tensor_tensor(
            st_[:, o + VAR + tlo:o + VAR + thi], st_[:, o + EX2 + tlo:o + EX2 + thi],
            st_[:, o + MSQ + tlo:o + MSQ + thi], OP.subtract), wait=[(ve, ve.count)], inc=ve)
        p.o("vector", lambda e: e.tensor_scalar(
            st_[:, o + VAR + tlo:o + VAR + thi], st_[:, o + VAR + tlo:o + VAR + thi], 0.0, None, OP.max),
            wait=[(ve, ve.count)], inc=ve)
        p.o("scalar", lambda e: e.activation(
            st_[:, o + SD + tlo:o + SD + thi], st_[:, o + VAR + tlo:o + VAR + thi], AF.Sqrt,
            bias=ccol[:, 0:1], scale=1.0), wait=[(ve, ve.count), (ge, ge_c)], inc=se)
        p.o("vector", lambda e: e.reciprocal(
            st_[:, o + RSTD + tlo:o + RSTD + thi], st_[:, o + SD + tlo:o + SD + thi]),
            wait=[(se, se.count)], inc=ve)
        p.o("vector", lambda e: e.scalar_tensor_tensor(
            st_[:, o + NMR + tlo:o + NMR + thi], st_[:, o + MEAN + tlo:o + MEAN + thi], -1.0,
            st_[:, o + RSTD + tlo:o + RSTD + thi], OP.mult, OP.mult), wait=[(ve, ve.count)], inc=ve)
        return ve.count

    pe_xsc = {0: 0, 1: 0}

    def ln_finish(o, dst_ct, ve_stats, out_store=None, extra_wait=(), tiles=None, ci0=0):
        marks = []
        ci = ci0
        for t in (range(NT) if tiles is None else tiles):
            par = t % 2
            buf = xsc[par]
            xw = [(ve, ve_stats), *extra_wait]
            if pe_xsc[par]:
                xw.append((pe, pe_xsc[par]))
            if out_store is not None and sxfp[par].count > 0:
                xw.append((sxfp[par], sxfp[par].count))
            p.o("scalar", lambda e, t=t, buf=buf: e.activation(
                buf[:, :], xb[:, t * D:(t + 1) * D], AF.Identity,
                bias=st_[:, o + NMR + t:o + NMR + t + 1],
                scale=st_[:, o + RSTD + t:o + RSTD + t + 1]),
                wait=xw, inc=se)
            se_t = se.count
            if out_store is not None:
                dram_fn = out_store
                p.o("sync", lambda e, t=t, buf=buf, dram_fn=dram_fn: e.dma_start(
                    out=dram_fn(t), in_=buf[:, :]), wait=[(se, se_t)], inc=sxfp[par], k=16)
            for g0 in range(0, DC, GS):
                bname, bank = LNB[(ci // GS) % 8]
                bankb = bank.bitcast(BF16)
                for j in range(GS):
                    cc = g0 + j
                    w_ = [(se, se_t), (sld, 16), *extra_wait]
                    if j == 0 and bname in evac:
                        w_.append(evac[bname])
                    p.o("tensor", lambda e, bankb=bankb, j=j, buf=buf, cc=cc: e.transpose(
                        bankb[:, j * 128:(j + 1) * 128], buf[:, cc * 128:(cc + 1) * 128],
                        idn[:, :]), wait=w_, inc=pe)
                ctv = dst_ct[:, :].rearrange("p (c s) -> p c s", s=S)[:, g0:g0 + GS,
                                                                     t * 128:(t + 1) * 128]
                bv = bankb[:, 0:GS * 128].rearrange("p (c j) -> p c j", j=128)
                p.o("vector", lambda e, ctv=ctv, bv=bv: e.tensor_copy(ctv, bv),
                    wait=[(pe, pe.count), (ve, ve.count)], inc=ve)
                evac[bname] = (ve, ve.count)
                ci += GS
            pe_xsc[par] = pe.count
            marks.append(ve.count)
        return marks

    def _finish():
        vv = ve.count
        for t in range(NT):
            p.o("sync", lambda e, t=t: e.dma_start(
                out=x2_out[t * 128:(t + 1) * 128, :], in_=xb[:, t * D:(t + 1) * D]),
                wait=[(ve, vv), (se, se.count), (pe, pe.count), *xall], inc=sso, k=16)
        p.barrier("sync", (sso, sso.count))
        p.emit(nc)
        return nc

    if L1_UPTO <= 0:
        return _finish()
    marks1 = []
    for glo in range(0, NT, GSZ):
        ghi = min(glo + GSZ, NT)
        for t in range(glo, ghi):
            ln_stats_tile(0, t, [(sxg[t // GSZ], sxg[t // GSZ].count)])
        vs_g = ln_chain(0, glo, ghi)
        marks1 += ln_finish(0, ct, vs_g, tiles=range(glo, ghi), ci0=glo * (DC // GS))
    ve_xnT = ve.count
    for hx in range(H):
        p.o("vector", lambda e, hx=hx: e.tensor_scalar(
            nrlT2[0:1, hx * 128:(hx + 1) * 128], onesb[:, :], nrl[0:1, hx:hx + 1],
            None, OP.mult), wait=[(ve, ve.count), (ge, ge_c)], inc=ve)
    for h_ in range(H):
        p.o("vector", lambda e, h_=h_: e.tensor_scalar(
            biasall[:, h_ * NT:(h_ + 1) * NT], kvals[:, :], mb[:, h_:h_ + 1], None, OP.mult),
            wait=[(ve, ve.count), (ge, ge_c)], inc=ve)

    if L1_UPTO <= 1:
        return _finish()

    swk_w = {0: w0, 1: w1}
    QCW = min(512, S)
    QC = (S + QCW - 1) // QCW
    TPQ = QCW // 128
    for jt in range(2 * DC):
        slot = jt % 2
        swk_need = swk_w[slot]
        dst = kTv if jt < DC else qT
        dcc = jt % DC
        scl = 1.0 if jt < DC else 1.0 / cfg.scale
        for qc in range(QC):
            ps = (pS0, pS1)[qc % 2]
            bname = ("b0", "b2")[qc % 2]
            for cc in range(DC):
                w = [swk_need, (ve, marks1[min((qc + 1) * TPQ - 1, NT - 1)])]
                if cc == 0 and bname in evac:
                    w.append(evac[bname])
                p.o("tensor", lambda e, qc=qc, cc=cc, ps=ps, slot=slot: e.matmul(
                    ps[:, 0:QCW], wsl[slot][:, cc * 128:(cc + 1) * 128],
                    ct[:, cc * S + qc * QCW: cc * S + (qc + 1) * QCW],
                    start=(cc == 0), stop=(cc == DC - 1)),
                    wait=w, inc=(pe if cc == DC - 1 else None))
            scl_ap = (1.0 if scl == 1.0 else ccol[:, 1:2])
            p.o("scalar", lambda e, dst=dst, dcc=dcc, qc=qc, ps=ps, scl_ap=scl_ap: e.activation(
                dst[:, dcc * S + qc * QCW: dcc * S + (qc + 1) * QCW], ps[:, 0:QCW], AF.Copy,
                bias=0.0, scale=scl_ap), wait=[(pe, pe.count), (ge, ge_c)], inc=se)
            evac[bname] = (se, se.count)
        if jt + 2 < 2 * DC:
            swk_w[slot] = slab_load(jt + 2, slot, (pe, pe.count))
        if jt == 3:
            load_wv_half(0, swv, ccs=range(0, DC // 2))
        if jt == 9:
            load_wv_half(0, swv, ccs=range(DC // 2, DC))
            swv_all = swv.count
    VCW = min(512, D)
    VC = (D + VCW - 1) // VCW
    HPC = VCW // 64
    for jc in range(VC):
        if jc > 0:
            load_wv_half(jc, swv2, after=(pe, pe.count))
        wvsem = (swv, swv_all) if jc == 0 else (swv2, swv2.count)
        for t in range(NT):
            ps = (pS0, pS1)[(jc * NT + t) % 2]
            bname = ("b0", "b2")[(jc * NT + t) % 2]
            for cc in range(DC):
                w = [wvsem, (ve, ve_xnT)]
                if cc == 0 and bname in evac:
                    w.append(evac[bname])
                p.o("tensor", lambda e, t=t, cc=cc, ps=ps: e.matmul(
                    ps[:, 0:VCW], ct[:, cc * S + t * 128:cc * S + (t + 1) * 128],
                    wv[:, cc * 512:cc * 512 + VCW],
                    start=(cc == 0), stop=(cc == DC - 1)),
                    wait=w, inc=(pe if cc == DC - 1 else None))
            p.o("vector", lambda e, t=t, jc=jc, ps=ps: e.tensor_copy(
                vb[:, t * H * 65 + jc * HPC * 65: t * H * 65 + (jc * HPC + HPC) * 65]
                  .rearrange("p (h d) -> p h d", d=65)[:, :, 0:64],
                ps[:, 0:VCW].rearrange("p (h d) -> p h d", d=64)),
                wait=[(pe, pe.count), (ve, ve_vones)], inc=ve)
            evac[bname] = (ve, ve.count)
    se_kqv = se.count
    ve_kqv = ve.count
    pe_kqv = pe.count

    ve_rp = {}
    ge_rp = {}
    ve_cp = {}
    for hp_ in range(HP):
        c0 = hp_ * S
        s0 = (DC + hp_) * S
        p.o("vector", lambda e, c0=c0, s0=s0: e.tensor_copy(
            kTz[0:64, c0:c0 + S], kTz[0:64, s0:s0 + S]),
            wait=[(se, se_kqv), (pe, pe_kqv), (ve, ve.count)], inc=ve)
        ve_cp[hp_] = ve.count
        ve_rp[hp_] = ve.count
    for hp_ in range(HP):
        c0 = hp_ * S
        s0 = (DC + hp_) * S
        p.o("gpsimd", lambda e, c0=c0: e.memset(kTz[64:128, c0:c0 + S], 0.0),
            wait=[(pe, pe_kqv), (ge, ge.count)], inc=ge)
        p.o("gpsimd", lambda e, s0=s0: e.memset(kTz[0:64, s0:s0 + S], 0.0),
            wait=[(se, se_kqv), (ve, ve_cp[hp_]), (ge, ge.count)], inc=ge)
        ge_rp[hp_] = ge.count
    if L1_UPTO <= 2:
        return _finish()

    p.o("vector", lambda e: e.memset(avt[64:128, :], 0.0), inc=ve)
    ve_avt0 = ve.count
    ge_qg = ge_c

    NW = S // QW
    pend = [None, -1, -1]
    pav_free = {}
    avt_tp = {0: 0, 1: 0}
    last_av = {}
    LN2O = 10 * NT

    ln2_pend = []

    def emit_ln2_tile():
        if not ln2_pend:
            return
        t = ln2_pend.pop(0)
        ln_stats_tile(LN2O, t, [(ve, ve.count)])
        p.o("sync", lambda e, t=t: e.dma_start(
            out=x2_out[t * 128:(t + 1) * 128, :], in_=xb[:, t * D:(t + 1) * D]),
            wait=[(ve, ve.count)], inc=sso, k=16)

    def emit_ln2_chunk(wd):
        ln2_pend.extend(range(wd * QT, (wd + 1) * QT))

    inst = 0
    order = []
    for j_ in range(NW // 2):
        for hp_i in range(HP):
            order.append((NW - 1 - j_, hp_i))
            order.append((j_, hp_i))
    if NW % 2:
        for hp_i in range(HP):
            order.append((NW // 2, hp_i))
    wdone = {}
    for (w, hp) in order:
        wlo, whi = w * QW, (w + 1) * QW
        kts = [kt for kt in range(NT) if kt * 128 < whi]
        nk = len(kts)
        h0, h1 = 2 * hp, 2 * hp + 1
        ipar = inst % 2
        se_exp = {}
        av_pe = {}

        def emit_av(ki, X, hx, first, last, w=w, wlo=wlo, whi=whi, kts=kts, ipar=ipar):
            kp = kts[ki]
            q0p = max(wlo, kp * 128)
            nqp = whi - q0p
            slot = X * 4 + (ki % 4)
            pav = pvs[X * 2 + ipar]
            bank = f"b{4 + X * 2 + ipar}"
            avw = [(se, se_exp[(ki, X)])]
            if first:
                avw.append(pav_free.get((X, ipar)) or evac.get(bank, (ve, 0)))
            p.o("tensor", lambda e, kp=kp, hx=hx, q0p=q0p, nqp=nqp, wlo=wlo,
                pav=pav, slot=slot, first=first, last=last: e.matmul(
                pav[0:128, q0p - wlo: q0p - wlo + nqp],
                vb[:, kp * H * 65 + hx * 65: kp * H * 65 + hx * 65 + 128],
                pstrip[:, slot * QW: slot * QW + nqp],
                start=first, stop=last),
                wait=avw, inc=pe)
            av_pe[(ki, X)] = pe.count
            last_av[(ki % 4, X)] = pe.count

        for ki, kt in enumerate(kts):
            q0 = max(wlo, kt * 128)
            nq = whi - q0
            for X, hx in ((0, h0), (1, h1)):
                par = ki % 2
                pst = (pS0, pS1)[par]
                bank = f"b{par * 2 + X}"
                w1_ = [(se, se_kqv), (ve, ve_kqv), (ve, ve_rp[hp]), (ge, ge_rp[hp])]
                if bank in evac:
                    w1_.append(evac[bank])
                reg = hp if X == 0 else DC + hp
                p.o("tensor", lambda e, hp=hp, kt=kt, q0=q0, nq=nq, pst=pst, X=X, reg=reg: e.matmul(
                    pst[:, X * QWP: X * QWP + nq],
                    kTz[:, reg * S + kt * 128: reg * S + (kt + 1) * 128],
                    qT[:, hp * S + q0: hp * S + q0 + nq],
                    start=True, stop=False), wait=w1_, inc=None)
                p.o("tensor", lambda e, pst=pst, X=X, q0=q0, nq=nq, hx=hx: e.matmul(
                    pst[:, X * QWP: X * QWP + nq],
                    nrlT2[:, hx * 128:(hx + 1) * 128],
                    qgrid[:, q0:q0 + nq],
                    start=False, stop=True), inc=pe)
                pe_mm = pe.count
                ve_strip = 0
                if q0 == kt * 128:
                    p.o("vector", lambda e, pst=pst, X=X: e.tensor_tensor(
                        pst[:, X * QWP: X * QWP + 128], pst[:, X * QWP: X * QWP + 128],
                        mgrid[:, :], OP.add),
                        wait=[(pe, pe_mm), (ve, ve.count)], inc=ve)
                    ve_strip = ve.count
                slot = X * 4 + (ki % 4)
                ew = [(pe, pe_mm)]
                if ve_strip:
                    ew.append((ve, ve_strip))
                if (ki % 4, X) in last_av:
                    ew.append((pe, last_av[(ki % 4, X)]))
                p.o("scalar", lambda e, pst=pst, X=X, nq=nq, kt=kt, hx=hx, slot=slot: e.activation(
                    pstrip[:, slot * QW: slot * QW + nq], pst[:, X * QWP: X * QWP + nq],
                    AF.Exp, bias=biasall[:, hx * NT + kt: hx * NT + kt + 1], scale=1.0),
                    wait=ew, inc=se)
                evac[bank] = (se, se.count)
                se_exp[(ki, X)] = se.count
            if ki >= 2:
                for X, hx in ((0, h0), (1, h1)):
                    emit_av(ki - 2, X, hx, first=(ki - 2 == 0), last=False)
            if ki == 1 and pend[0] is not None:
                emit_ln2_tile()
                fw = pend[1]
                pend[0]()
                pend[0] = None
                wdone[fw] = wdone.get(fw, 0) + 1
                if wdone[fw] == HP:
                    emit_ln2_chunk(fw)
        for kl in (nk - 2, nk - 1):
            for X, hx in ((0, h0), (1, h1)):
                emit_av(kl, X, hx, first=(kl == 0), last=(kl == nk - 1))
        pe_av_end = pe.count

        def mk_norm(w=w, h0=h0, h1=h1, pe_av_end=pe_av_end, ipar=ipar):
            def norm():
                ve_avt = {}
                for X in (0, 1):
                    pav = pvs[X * 2 + ipar]
                    aw = [(pe, pe_av_end), (ve, max(ve_avt0, ve.count))]
                    if avt_tp[X]:
                        aw.append((pe, avt_tp[X]))
                    p.o("vector", lambda e, X=X, pav=pav: e.tensor_copy(
                        avt[0:65, X * QW:(X + 1) * QW], pav[0:65, 0:QW]),
                        wait=aw, inc=ve)
                    ve_avt[X] = ve.count
                for X, hx in ((0, h0), (1, h1)):
                    pav = pvs[X * 2 + ipar]
                    pavb = pav[:, :].bitcast(BF16)
                    bank = f"b{4 + X * 2 + ipar}"
                    for qt in range(QT):
                        tw_ = [(ve, ve_avt[X])]
                        if qt == 0 and bank in evac:
                            tw_.append(evac[bank])
                        p.o("tensor", lambda e, X=X, qt=qt, pavb=pavb: e.transpose(
                            pavb[:, qt * 128:(qt + 1) * 128],
                            avt[:, X * QW + qt * 128: X * QW + (qt + 1) * 128],
                            idn[:, :]), wait=tw_, inc=pe)
                    avt_tp[X] = pe.count
                    pe_tp = pe.count
                    for qt in range(QT):
                        p.o("vector", lambda e, X=X, qt=qt, pavb=pavb: e.reciprocal(
                            rcol[:, X:X + 1], pavb[:, qt * 128 + 64: qt * 128 + 65]),
                            wait=[(pe, pe_tp), (ve, ve.count)], inc=ve)
                        qti = w * QT + qt
                        p.o("vector", lambda e, X=X, hx=hx, qt=qt, qti=qti, pavb=pavb: e.scalar_tensor_tensor(
                            xb[:, qti * D + hx * 64: qti * D + hx * 64 + 64],
                            pavb[:, qt * 128: qt * 128 + 64], rcol[:, X:X + 1],
                            xb[:, qti * D + hx * 64: qti * D + hx * 64 + 64],
                            OP.mult, OP.add), wait=[(ve, ve.count)], inc=ve)
                    pav_free[(X, ipar)] = (ve, ve.count)
                    evac[bank] = (ve, ve.count)
            return norm
        pend[0], pend[1], pend[2] = mk_norm(), w, hp
        inst += 1
    if pend[0] is not None:
        fw = pend[1]
        pend[0]()
        pend[0] = None
        wdone[fw] = wdone.get(fw, 0) + 1
        if wdone[fw] == HP:
            emit_ln2_chunk(fw)
    while ln2_pend:
        emit_ln2_tile()
    ve_attn = ve.count
    pe_attn = pe.count
    se_attn = se.count
    if L1_UPTO <= 3:
        return _finish()

    ln2_marks = ln_finish(LN2O, ct, ln_chain(LN2O),
                          out_store=(lambda t: xf_out[t * 128:(t + 1) * 128, :]),
                          extra_wait=[(pe, pe_attn)])
    ve_xfT = ve.count
    if L1_UPTO <= 4:
        p.barrier("sync", (sso, sso.count), (sxfp[0], sxfp[0].count), (sxfp[1], sxfp[1].count))
        p.emit(nc)
        return nc

    for t in range(NT):
        ps = (pS0, pS1)[t % 2]
        bname = ("b0", "b2")[t % 2]
        for cc in range(DC):
            w = [(ve, ln2_marks[t]), (scst, scst.count)]
            if cc == 0 and bname in evac:
                w.append(evac[bname])
            p.o("tensor", lambda e, t=t, cc=cc, ps=ps: e.matmul(
                ps[:, 0:8], ct[:, cc * S + t * 128: cc * S + (t + 1) * 128],
                rwt[:, cc * 8:(cc + 1) * 8], start=(cc == 0), stop=False),
                wait=w, inc=None)
        p.o("tensor", lambda e, ps=ps: e.matmul(
            ps[:, 0:8], onesb[:, :], rbr[:, :], start=False, stop=True), inc=pe)
        p.o("vector", lambda e, t=t, ps=ps: e.tensor_copy(lgt[:, t * 8:(t + 1) * 8], ps[:, 0:8]),
            wait=[(pe, pe.count)], inc=ve)
        evac[bname] = (ve, ve.count)
    ve_lgt = ve.count
    for t in range(NT):
        p.o("vector", lambda e, t=t: e.max(top8[:, t * 8:(t + 1) * 8], lgt[:, t * 8:(t + 1) * 8]),
            wait=[(ve, ve_lgt)], inc=ve)
        p.o("vector", lambda e, t=t: e.max_index(
            tpi[:, t * 8:(t + 1) * 8], top8[:, t * 8:(t + 1) * 8], lgt[:, t * 8:(t + 1) * 8]),
            wait=[(ve, ve.count)], inc=ve)
    ve_top = ve.count
    v1v = lambda: top8[:, :].rearrange("p (t e) -> p t e", e=8)[:, :, 0:1]
    v2v = lambda: top8[:, :].rearrange("p (t e) -> p t e", e=8)[:, :, 1:2]
    NV = NT
    D21, NV1, SSUM, P1, P2, DEN, RDEN, W1, W2, VM1, VM2, E21 = (i * NV for i in range(12))
    p.o("vector", lambda e: e.tensor_tensor(tw[:, D21:D21 + NV], v2v(), v1v(), OP.subtract),
        wait=[(ve, ve_top)], inc=ve)
    p.o("vector", lambda e: e.tensor_scalar(tw[:, NV1:NV1 + NV], v1v(), -1.0, None, OP.mult),
        inc=ve)
    ve_nv1 = ve.count
    for t in range(NT):
        d8 = dump8[t % 2]
        p.o("scalar", lambda e, t=t, d8=d8: e.activation(
            d8[:, :], lgt[:, t * 8:(t + 1) * 8], AF.Exp,
            bias=tw[:, NV1 + t:NV1 + t + 1], scale=1.0,
            accum_out=tw[:, SSUM + t:SSUM + t + 1]),
            wait=[(ve, ve_nv1), (se, se.count)], inc=se)
    p.o("scalar", lambda e: e.activation(tw[:, E21:E21 + NV], tw[:, D21:D21 + NV],
                                         AF.Exp, bias=0.0, scale=1.0),
        wait=[(ve, ve_nv1)], inc=se)
    se_probs = se.count
    p.o("vector", lambda e: e.reciprocal(tw[:, P1:P1 + NV], tw[:, SSUM:SSUM + NV]),
        wait=[(se, se_probs)], inc=ve)
    p.o("vector", lambda e: e.tensor_tensor(tw[:, P2:P2 + NV], tw[:, E21:E21 + NV],
                                            tw[:, P1:P1 + NV], OP.mult),
        wait=[(ve, ve.count)], inc=ve)
    p.o("vector", lambda e: e.tensor_tensor(tw[:, DEN:DEN + NV], tw[:, P1:P1 + NV],
                                            tw[:, P2:P2 + NV], OP.add),
        wait=[(ve, ve.count)], inc=ve)
    p.o("vector", lambda e: e.tensor_scalar(tw[:, DEN:DEN + NV], tw[:, DEN:DEN + NV],
                                            cfg.EPS, None, OP.add),
        wait=[(ve, ve.count)], inc=ve)
    p.o("vector", lambda e: e.reciprocal(tw[:, RDEN:RDEN + NV], tw[:, DEN:DEN + NV]),
        wait=[(ve, ve.count)], inc=ve)
    p.o("vector", lambda e: e.tensor_tensor(tw[:, W1:W1 + NV], tw[:, P1:P1 + NV],
                                            tw[:, RDEN:RDEN + NV], OP.mult),
        wait=[(ve, ve.count)], inc=ve)
    p.o("vector", lambda e: e.tensor_tensor(tw[:, W2:W2 + NV], tw[:, P2:P2 + NV],
                                            tw[:, RDEN:RDEN + NV], OP.mult),
        wait=[(ve, ve.count)], inc=ve)
    p.o("vector", lambda e: e.tensor_single_scalar(tw[:, VM1:VM1 + NV], tw[:, W1:W1 + NV],
                                                   cfg.EPS, OP.is_gt),
        wait=[(ve, ve.count)], inc=ve)
    p.o("vector", lambda e: e.tensor_single_scalar(tw[:, VM2:VM2 + NV], tw[:, W2:W2 + NV],
                                                   cfg.EPS, OP.is_gt),
        wait=[(ve, ve.count)], inc=ve)
    ve_w = ve.count
    mview = lambda q: meta[:, :].rearrange("p (t q) -> p t q", q=4)[:, :, q:q + 1]
    iview = lambda c: tpi[:, :].rearrange("p (t e) -> p t e", e=8)[:, :, c:c + 1]
    p.o("vector", lambda e: e.tensor_copy(tw[:, DEN:DEN + NV], iview(0)), wait=[(ve, ve_w)], inc=ve)
    p.o("vector", lambda e: e.tensor_scalar(tw[:, DEN:DEN + NV], tw[:, DEN:DEN + NV], -8.0, None, OP.add),
        wait=[(ve, ve.count)], inc=ve)
    p.o("vector", lambda e: e.scalar_tensor_tensor(
        mview(0), tw[:, DEN:DEN + NV], 1.0, tw[:, VM1:VM1 + NV], OP.mult, OP.mult),
        wait=[(ve, ve.count)], inc=ve)
    p.o("vector", lambda e: e.tensor_scalar(mview(0), mview(0), 8.0, None, OP.add),
        wait=[(ve, ve.count)], inc=ve)
    p.o("vector", lambda e: e.tensor_copy(tw[:, SSUM:SSUM + NV], iview(1)), inc=ve)
    p.o("vector", lambda e: e.tensor_scalar(tw[:, SSUM:SSUM + NV], tw[:, SSUM:SSUM + NV], -8.0, None, OP.add),
        wait=[(ve, ve.count)], inc=ve)
    p.o("vector", lambda e: e.scalar_tensor_tensor(
        mview(1), tw[:, SSUM:SSUM + NV], 1.0, tw[:, VM2:VM2 + NV], OP.mult, OP.mult),
        wait=[(ve, ve.count)], inc=ve)
    p.o("vector", lambda e: e.tensor_scalar(mview(1), mview(1), 8.0, None, OP.add),
        wait=[(ve, ve.count)], inc=ve)
    p.o("vector", lambda e: e.tensor_copy(mview(2), tw[:, W1:W1 + NV]), inc=ve)
    p.o("vector", lambda e: e.tensor_copy(mview(3), tw[:, W2:W2 + NV]), inc=ve)
    ve_meta = ve.count
    p.o("sync", lambda e: e.dma_start(out=meta_out, in_=meta[:, :]), wait=[(ve, ve_meta)],
        inc=sso, k=16)

    p.barrier("sync", (sso, sso.count), (sxfp[0], sxfp[0].count), (sxfp[1], sxfp[1].count))
    p.emit(nc)
    return nc


def _tile_wkqv(wk):
    D = wk.shape[0]
    J, DCc = wk.shape[1] // 128, D // 128
    t = wk.reshape(DCc, 128, J, 128).transpose(2, 0, 1, 3).reshape(J * DCc * 128, 128)
    return np.ascontiguousarray(t)


def _np_ln(x, eps=1e-5):
    m = x.mean(-1, keepdims=True)
    v = x.var(-1, keepdims=True)
    return (x - m) / np.sqrt(v + eps)


def build_l2(cfg: Cfg):
    S, D, H, NT, DC, HC = cfg.S, cfg.D, cfg.H, cfg.NT, cfg.DC, cfg.HC
    N, A, CAP, NST, SLOT, HID = cfg.N, cfg.A, cfg.CAP, cfg.NST, cfg.SLOT, cfg.HID
    BIG = cfg.BIG
    assert A <= 128
    nc = bass.Bass("TRN2", target_bir_lowering=False)

    meta_in = nc.dram_tensor("meta_all", [128, 4 * A], F32, kind="ExternalInput").ap()
    xf_in = nc.dram_tensor("xf_all", [N, D], BF16, kind="ExternalInput").ap()
    fc1_in = nc.dram_tensor("fc1t", [DC * 128, HID], F32, kind="ExternalInput").ap()
    fc2_in = nc.dram_tensor("fc2t", [HID, D], F32, kind="ExternalInput").ap()
    b1_in = nc.dram_tensor("b1t", [128, HC], F32, kind="ExternalInput").ap()
    b2_in = nc.dram_tensor("b2r", [1, D], F32, kind="ExternalInput").ap()
    ls_in = nc.dram_tensor("lsr", [1, D], F32, kind="ExternalInput").ap()
    eid_in = nc.dram_tensor("eid", [128, 1], F32, kind="ExternalInput").ap()
    idn_in = nc.dram_tensor("idn", [128, 128], BF16, kind="ExternalInput").ap()
    stg_d = nc.dram_tensor("stg_d", [128 * 6 * A, 1], F32).ap()

    y_out = nc.dram_tensor("y", [SLOT, D], BF16, kind="ExternalOutput").ap()

    ea = nc.alloc_sbuf_tensor("ea", [128, 4 * A], F32)
    eid = nc.alloc_sbuf_tensor("eid_s", [128, 1], F32)
    idn = nc.alloc_sbuf_tensor("idn_s", [128, 128], BF16)
    idnf = nc.alloc_sbuf_tensor("idnf", [128, 128], F32)
    rel = nc.alloc_sbuf_tensor("rel", [128, 128], F32)
    qabs = nc.alloc_sbuf_tensor("qabs", [128, 128], F32)
    cbr = nc.alloc_sbuf_tensor("cbr", [128, 4], F32)
    nidx = nc.alloc_sbuf_tensor("nidx", [128, A], F32)
    scolg = nc.alloc_sbuf_tensor("scolg", [128, NST], F32)
    utri = nc.alloc_sbuf_tensor("utri", [128, 128], BF16)
    ltri = nc.alloc_sbuf_tensor("ltri", [128, 128], BF16)
    lful = nc.alloc_sbuf_tensor("lful", [128, 128], BF16)
    shft = nc.alloc_sbuf_tensor("shft", [128, 128], BF16)
    itk = [nc.alloc_sbuf_tensor(f"itk{k}", [128, A], BF16) for k in range(2)]
    itab = [nc.alloc_sbuf_tensor(f"itab{k}", [128, 128], BF16) for k in range(2)]
    scb = nc.alloc_sbuf_tensor("scb", [128, 2], BF16)
    offs = nc.alloc_sbuf_tensor("offs", [128, 8], F32)
    dcb = nc.alloc_sbuf_tensor("dcb", [128, 4], BF16)
    lpg = [nc.alloc_sbuf_tensor(f"lpg{k}", [128, 128], F32) for k in range(2)]
    posg = [nc.alloc_sbuf_tensor(f"posg{k}", [128, 128], F32) for k in range(2)]
    lpt = [nc.alloc_sbuf_tensor(f"lpt{k}", [128, A], F32) for k in range(2)]
    rhs3 = [nc.alloc_sbuf_tensor(f"rhs3{k}", [128, 3 * A], F32) for k in range(2)]
    mscr = [nc.alloc_sbuf_tensor(f"mscr{i}", [128, 128], F32) for i in range(3)]
    arena = nc.alloc_sbuf_tensor("arena", [128, 32768], mybir.dt.uint8)
    stg = arena[:, 0:6 * A * 4].bitcast(F32)
    hts = arena.bitcast(BF16)
    cnt = nc.alloc_sbuf_tensor("cnt", [128, 4 * NST], F32)
    cwork = nc.alloc_sbuf_tensor("cwork", [128, 10 * NST], F32)
    tokf = nc.alloc_sbuf_tensor("tokf", [128, NST], F32)
    toku = nc.alloc_sbuf_tensor("toku", [128, NST], U32)
    gcolu = nc.alloc_sbuf_tensor("gcolu", [128, NST], U32)
    toks = [nc.alloc_sbuf_tensor(f"toks{i}", [128, D], BF16) for i in range(8)]
    tokst = nc.alloc_sbuf_tensor("tokst", [128, DC * 512], BF16)
    ysb = [nc.alloc_sbuf_tensor(f"ysb{i}", [128, D], BF16) for i in range(2)]
    fc1s = nc.alloc_sbuf_tensor("fc1s", [128, DC * HID], BF16)
    fc2s = nc.alloc_sbuf_tensor("fc2s", [128, HC * D], BF16)
    b1s = nc.alloc_sbuf_tensor("b1s", [128, HC], F32)
    b2r = nc.alloc_sbuf_tensor("b2r_s", [1, D], BF16)
    lsb = nc.alloc_sbuf_tensor("lsb", [128, D], BF16)
    onesb = nc.alloc_sbuf_tensor("onesb", [1, 128], BF16)
    onecol = nc.alloc_sbuf_tensor("onecol", [128, 1], F32)

    ph0 = nc.alloc_psum_tensor("ph0", [128, 512], F32)
    ph1 = nc.alloc_psum_tensor("ph1", [128, 512], F32)
    proute = nc.alloc_psum_tensor("proute", [128, 512], F32)
    ptw = nc.alloc_psum_tensor("ptw", [128, 512], F32)
    ptwb = ptw[:, :].bitcast(BF16)
    py = [nc.alloc_psum_tensor(f"py{i}", [128, D], F32) for i in range(2)]

    p = Plan()
    sld = p.sem("sld"); swt = p.sem("swt")
    sng = p.sem("sng")
    swg = [p.sem(f"swg{i}") for i in range(8)]
    sy = [p.sem("sy0"), p.sem("sy1")]
    sstg = p.sem("sstg")
    pe = p.sem("pe"); ve = p.sem("ve"); se = p.sem("se"); ge = p.sem("ge")

    p.o("sync", lambda e: e.dma_start(out=ea[:, :], in_=meta_in), inc=sld, k=16)
    p.o("sync", lambda e: e.dma_start(out=eid[:, :], in_=eid_in), inc=sld, k=16)
    p.o("sync", lambda e: e.dma_start(out=idn[:, :], in_=idn_in), inc=sld, k=16)
    p.o("sync", lambda e: e.dma_start(out=b1s[:, :], in_=b1_in), inc=sld, k=16)
    sld_all = sld.count
    swt2 = p.sem("swt2")
    sls = p.sem("sls")
    p.o("gpsimd", lambda e: e.dma_start(out=toks[0][0:1, 0:D], in_=ls_in), inc=sls, k=16)
    p.o("gpsimd", lambda e: e.dma_start(out=b2r[:, :], in_=b2_in), inc=swt2, k=16)
    for cc in range(DC):
        p.o("gpsimd", lambda e, cc=cc: e.dma_start(
            out=fc1s[:, cc * HID:(cc + 1) * HID], in_=fc1_in[cc * 128:(cc + 1) * 128, :]),
            inc=swt, k=16)
    for hc in range(HC):
        p.o("gpsimd", lambda e, hc=hc: e.dma_start(
            out=fc2s[:, hc * D:(hc + 1) * D], in_=fc2_in[hc * 128:(hc + 1) * 128, :]),
            inc=swt2, k=16)
    swt_all = swt.count
    swt2_all = swt2.count

    p.o("gpsimd", lambda e: e.iota(rel[:, :], pattern=[[-1, 128]], base=0,
                                   channel_multiplier=1, allow_small_or_imprecise_dtypes=True), inc=ge)
    ge_rel = ge.count
    p.o("gpsimd", lambda e: e.iota(qabs[:, :], pattern=[[1, 128]], base=0,
                                   channel_multiplier=0, allow_small_or_imprecise_dtypes=True), inc=ge)
    p.o("gpsimd", lambda e: e.iota(nidx[:, :], pattern=[[128, A]], base=0,
                                   channel_multiplier=1, allow_small_or_imprecise_dtypes=True), inc=ge)
    p.o("gpsimd", lambda e: e.iota(scolg[:, :], pattern=[[128, NST]], base=0,
                                   channel_multiplier=1, allow_small_or_imprecise_dtypes=True), inc=ge)
    p.o("gpsimd", lambda e: e.memset(onesb[:, :], 1.0), inc=ge)
    p.o("gpsimd", lambda e: e.memset(onecol[:, :], 1.0), inc=ge)
    p.o("gpsimd", lambda e: e.memset(tokf[:, :], 0.0), inc=ge)
    ge_c = ge.count
    p.o("vector", lambda e: e.tensor_single_scalar(utri[:, :], rel[:, :], 0.0, OP.is_le),
        wait=[(ge, ge_rel)], inc=ve)
    p.o("vector", lambda e: e.tensor_single_scalar(ltri[:, :], rel[:, :], 0.0, OP.is_lt), inc=ve)
    p.o("vector", lambda e: e.tensor_single_scalar(shft[:, :], rel[:, :], -1.0, OP.is_equal), inc=ve)
    p.o("vector", lambda e: e.tensor_single_scalar(idnf[:, :], rel[:, :], 0.0, OP.is_equal), inc=ve)
    p.o("vector", lambda e: e.memset(lful[:, :], 1.0), inc=ve)
    ve_c = ve.count

    eview = lambda q: ea[:, :].rearrange("p (a q) -> p a q", q=4)[:, :, q:q + 1].squeeze(2)
    for k in range(2):
        p.o("vector", lambda e, k=k: e.tensor_scalar(
            itk[k][:, :], eview(k), eid[:, 0:1], None, OP.is_equal),
            wait=[(sld, sld_all), (ve, ve_c)], inc=ve)
    ve_it = ve.count
    p.o("vector", lambda e: e.memset(scb[:, :], 0.0), inc=ve)
    ve_scb0 = ve.count
    t2ps = {0: ph0, 1: ph1}
    for k in range(2):
        p.o("tensor", lambda e, k=k: e.matmul(
            t2ps[k][0:A, 0:128], itk[k][:, :], utri[:, :], start=True, stop=True),
            wait=[(ve, ve_it)], inc=pe)
    pe_t2 = pe.count
    for k in range(2):
        p.o("vector", lambda e, k=k: e.tensor_copy(scb[0:A, k:k + 1], t2ps[k][0:A, 127:128]),
            wait=[(pe, pe_t2), (ve, ve_scb0)], inc=ve)
    ve_scb = ve.count
    p.o("tensor", lambda e: e.matmul(proute[:, 0:2], ltri[:, :], scb[:, :], start=True, stop=True),
        wait=[(ve, ve_scb)], inc=pe)
    p.o("vector", lambda e: e.tensor_copy(offs[:, 0:2], proute[:, 0:2]), wait=[(pe, pe.count)], inc=ve)
    p.o("tensor", lambda e: e.matmul(proute[:, 2:4], lful[:, :], scb[:, :], start=True, stop=True),
        wait=[(ve, ve.count)], inc=pe)
    p.o("vector", lambda e: e.tensor_copy(offs[:, 2:4], proute[:, 2:4]), wait=[(pe, pe.count)], inc=ve)
    p.o("tensor", lambda e: e.matmul(proute[:, 4:6], shft[:, :], scb[:, :], start=True, stop=True),
        wait=[(ve, ve.count)], inc=pe)
    p.o("vector", lambda e: e.tensor_copy(dcb[:, 1:2], proute[:, 4:5]), wait=[(pe, pe.count)], inc=ve)
    p.o("vector", lambda e: e.tensor_copy(dcb[:, 3:4], proute[:, 5:6]), wait=[(ve, ve.count)], inc=ve)
    p.o("vector", lambda e: e.memset(dcb[:, 0:1], 1.0), inc=ve)
    p.o("vector", lambda e: e.memset(dcb[:, 2:3], 1.0), inc=ve)
    p.o("vector", lambda e: e.tensor_tensor(offs[:, 4:5], offs[:, 1:2], offs[:, 2:3], OP.add),
        wait=[(ve, ve.count)], inc=ve)
    ve_offs = ve.count

    for k in range(2):
        cbcol = (offs[:, 0:1], offs[:, 4:5])[k]
        p.o("vector", lambda e, k=k, cbcol=cbcol: e.tensor_scalar(
            posg[k][0:A, :], t2ps[k][0:A, 0:128], cbcol[0:A, :], -1.0, OP.add, OP.add),
            wait=[(ve, ve_offs)], inc=ve)
        p.o("vector", lambda e, k=k: e.tensor_scalar(
            lpg[k][0:A, :], t2ps[k][0:A, 0:128], -1.0, None, OP.add),
            wait=[(ve, ve.count)], inc=ve)
    ve_pos = ve.count
    for k in range(2):
        p.o("tensor", lambda e, k=k: e.transpose(ptwb[0:A, k * 128:(k + 1) * 128], itk[k][:, :], idn[:, :]),
            wait=[(ve, ve_pos), (pe, pe.count)], inc=pe)
        p.o("vector", lambda e, k=k: e.tensor_copy(itab[k][0:A, :], ptwb[0:A, k * 128:(k + 1) * 128]),
            wait=[(pe, pe.count)], inc=ve)
        p.o("vector", lambda e, k=k: e.scalar_tensor_tensor(
            lpg[k][0:A, :], itab[k][0:A, :], 1e6, lpg[k][0:A, :], OP.mult, OP.add),
            wait=[(ve, ve.count)], inc=ve)
        p.o("vector", lambda e, k=k: e.tensor_scalar(
            lpg[k][0:A, :], lpg[k][0:A, :], -1e6, None, OP.add),
            wait=[(ve, ve.count)], inc=ve)
        ve_lpm = ve.count
        p.o("tensor", lambda e, k=k: e.transpose(
            proute[:, 384:384 + A], lpg[k][0:A, :], idnf[0:A, 0:A]),
            wait=[(ve, ve_lpm), (pe, pe.count)], inc=pe)
        p.o("vector", lambda e, k=k: e.tensor_copy(lpt[k][:, 0:A], proute[:, 384:384 + A]),
            wait=[(pe, pe.count)], inc=ve)
        p.o("tensor", lambda e, k=k: e.transpose(
            proute[:, 384:384 + A], posg[k][0:A, :], idnf[0:A, 0:A]),
            wait=[(ve, ve.count), (pe, pe.count)], inc=pe)
        p.o("vector", lambda e, k=k: e.tensor_copy(
            rhs3[k][:, :].rearrange("p (a q) -> p a q", q=3)[:, :, 1:2].squeeze(2),
            proute[:, 384:384 + A]),
            wait=[(pe, pe.count)], inc=ve)
    ve_tr = ve.count
    for k in range(2):
        p.o("vector", lambda e, k=k: e.tensor_copy(
            rhs3[k][:, :].rearrange("p (a q) -> p a q", q=3)[:, :, 0:1].squeeze(2), nidx[:, :]),
            wait=[(ve, ve_tr), (ge, ge_c)], inc=ve)
        p.o("vector", lambda e, k=k: e.memset(
            rhs3[k][:, :].rearrange("p (a q) -> p a q", q=3)[:, :, 2:3].squeeze(2), 1.0), inc=ve)
    ve_rhs = ve.count

    C1, O1, C2, O2 = 0, NST, 2 * NST, 3 * NST
    CNTC, OFFSEL, T1T, GG = 4 * NST, 5 * NST, 6 * NST, 7 * NST
    pe_cnt0 = pe.count
    mscr6 = mscr + [lpg[0], lpg[1], posg[0]]
    cnt_pe = {}
    for r in range(NST):
        for k in range(2):
            idx = 2 * r + k
            cbcol = (offs[:, 0:1], offs[:, 4:5])[k]
            ms = mscr6[idx % 6]
            cw_ = [(ve, ve.count)]
            if idx - 6 in cnt_pe:
                cw_.append((pe, cnt_pe[idx - 6]))
            elif idx < 6:
                cw_.append((pe, pe_cnt0))
            p.o("vector", lambda e, r=r, k=k, idx=idx, cbcol=cbcol: e.tensor_scalar(
                cbr[:, idx % 4:idx % 4 + 1], cbcol, float(-128 * r), None, OP.add),
                wait=cw_, inc=ve)
            p.o("vector", lambda e, idx=idx, ms=ms: e.tensor_scalar(
                ms[:, :].bitcast(BF16)[:, 0:128], qabs[:, 0:128],
                cbr[:, idx % 4:idx % 4 + 1], None, OP.is_ge),
                wait=[(ve, ve.count)], inc=ve)
            p.o("tensor", lambda e, r=r, k=k, ms=ms: e.matmul(
                ph0[:, 4 * (r % 8) + 2 * k: 4 * (r % 8) + 2 * k + 2],
                ms[:, :].bitcast(BF16)[0:A, 0:128], dcb[0:A, 2 * k:2 * k + 2],
                start=True, stop=True),
                wait=[(ve, ve.count)], inc=pe)
            cnt_pe[idx] = pe.count
        if r % 8 == 7 or r == NST - 1:
            lo = (r // 8) * 8
            p.o("vector", lambda e, r=r, lo=lo: e.tensor_copy(
                cnt[:, 4 * lo:4 * (r + 1)], ph0[:, 0:4 * (r + 1 - lo)]),
                wait=[(pe, pe.count)], inc=ve)
    ve_cnt = ve.count
    cview = lambda q: cnt[:, :].rearrange("p (r q) -> p r q", q=4)[:, :, q:q + 1].squeeze(2)
    p.o("vector", lambda e: e.tensor_tensor(cwork[:, CNTC:CNTC + NST], cview(0), cview(2), OP.add),
        wait=[(ve, ve_cnt)], inc=ve)
    TMPA = 8 * NST
    p.o("vector", lambda e: e.tensor_scalar(cwork[:, TMPA:TMPA + NST], cview(3), offs[:, 2:3],
                                            None, OP.add), inc=ve)
    p.o("vector", lambda e: e.tensor_tensor(cwork[:, TMPA:TMPA + NST], cwork[:, TMPA:TMPA + NST],
                                            cview(1), OP.subtract),
        wait=[(ve, ve.count)], inc=ve)
    p.o("vector", lambda e: e.tensor_scalar(cwork[:, T1T:T1T + NST], scolg[:, :], offs[:, 2:3],
                                            None, OP.is_ge), inc=ve)
    p.o("vector", lambda e: e.tensor_tensor(cwork[:, TMPA:TMPA + NST], cwork[:, TMPA:TMPA + NST],
                                            cwork[:, T1T:T1T + NST], OP.mult),
        wait=[(ve, ve.count)], inc=ve)
    p.o("vector", lambda e: e.tensor_tensor(cwork[:, OFFSEL:OFFSEL + NST], cview(1),
                                            cwork[:, TMPA:TMPA + NST], OP.add),
        wait=[(ve, ve.count)], inc=ve)
    p.o("vector", lambda e: e.tensor_tensor(cwork[:, GG:GG + NST], scolg[:, :],
                                            cwork[:, OFFSEL:OFFSEL + NST], OP.subtract),
        wait=[(ve, ve.count)], inc=ve)
    p.o("vector", lambda e: e.tensor_scalar(cwork[:, GG:GG + NST], cwork[:, GG:GG + NST],
                                            float(6 * A), None, OP.mult),
        wait=[(ve, ve.count)], inc=ve)
    p.o("vector", lambda e: e.tensor_scalar(cwork[:, T1T:T1T + NST], cwork[:, CNTC:CNTC + NST],
                                            3.0, -3.0, OP.mult, OP.add),
        wait=[(ve, ve.count)], inc=ve)
    p.o("vector", lambda e: e.tensor_tensor(cwork[:, GG:GG + NST], cwork[:, GG:GG + NST],
                                            cwork[:, T1T:T1T + NST], OP.add),
        wait=[(ve, ve.count)], inc=ve)
    p.o("vector", lambda e: e.tensor_copy(gcolu[:, :], cwork[:, GG:GG + NST]),
        wait=[(ve, ve.count)], inc=ve)
    ve_g = ve.count
    pe_cnt_end = pe.count

    nch = 2 * A
    ev_m = (ve, ve_rhs)
    pe_m = {}
    for batch in range(0, nch, 128):
        bn = min(128, nch - batch)
        for i in range(bn):
            ch = batch + i
            k, a = ch // A, ch % A
            ms = mscr6[ch % 6]
            bw = [(ve, ve_rhs)]
            if ch - 6 in pe_m:
                bw.append((pe, pe_m[ch - 6]))
            else:
                bw.append((pe, pe_cnt_end))
            p.o("vector", lambda e, k=k, a=a, ms=ms: e.tensor_scalar(
                ms[:, :], qabs[:, 0:128], lpt[k][:, a:a + 1], None, OP.is_equal),
                wait=bw, inc=ve)
            p.o("tensor", lambda e, k=k, a=a, i=i, ms=ms: e.matmul(
                proute[:, 3 * i:3 * i + 3], ms[:, :], rhs3[k][:, 3 * a:3 * a + 3],
                start=True, stop=True),
                wait=[(ve, ve.count), ev_m], inc=pe)
            pe_m[ch] = pe.count
        p.o("vector", lambda e, batch=batch, bn=bn: e.tensor_copy(
            stg[:, 3 * batch:3 * (batch + bn)], proute[:, 0:3 * bn]),
            wait=[(pe, pe.count)], inc=ve)
        ev_m = (ve, ve.count)
    ve_stg = ve.count
    p.o("scalar", lambda e: e.dma_start(
        out=stg_d.rearrange("(p f) o -> p (f o)", p=128), in_=stg[:, :]),
        wait=[(ve, ve_stg)], inc=sstg, k=16)

    for r in range(NST):
        p.o("gpsimd", lambda e, r=r: e.indirect_dma_start(
            out=tokf[:, r:r + 1], out_offset=None,
            in_=stg_d,
            in_offset=bass.IndirectOffsetOnAxis(gcolu[:, r:r + 1], axis=0),
            bounds_check=128 * 6 * A - 1, oob_is_err=False),
            wait=[(ve, ve_g), (sstg, 16), (ge, ge_c)], inc=sng, k=16)
    ve_tok_r = {}
    for r in range(NST):
        p.o("vector", lambda e, r=r: e.tensor_copy(toku[:, r:r + 1], tokf[:, r:r + 1]),
            wait=[(sng, 16 * (r + 1))], inc=ve)
        ve_tok_r[r] = ve.count

    nlsb = (D + 511) // 512
    for i in range(nlsb):
        c0, cn = i * 512, min(512, D - i * 512)
        p.o("tensor", lambda e, c0=c0, cn=cn: e.matmul(
            py[0][:, c0:c0 + cn], onesb[:, :], toks[0][0:1, c0:c0 + cn], start=True, stop=True),
            wait=[(sls, 16), (ge, ge_c)], inc=pe)
    p.o("vector", lambda e: e.tensor_copy(lsb[:, :], py[0][:, 0:D]), wait=[(pe, pe.count)], inc=ve)
    ve_lsb = ve.count
    pe_lsb = pe.count

    NTC = NST * 128 // 512
    assert NST * 128 % 512 == 0 or NTC * 512 < NST * 128
    if NST * 128 % 512 != 0:
        NTC += 1
    pe_prev_s1 = (pe, 0)
    pe_prev_s2 = (pe, 0)
    se_gel = {0: 0, 1: 0}
    ve_epi = {}
    ve_py = {0: (ve, 0), 1: (ve, 0)}
    ve_ptw = (ve, ve.count)
    sw_g = {}
    for tc in range(NTC):
        tiles = [r for r in range(tc * 4, min(tc * 4 + 4, NST))]
        for i, r in enumerate(tiles):
            slot = (tc % 2) * 4 + i
            w = [(ve, ve_tok_r[r]), (ge, ge_c), (pe, pe_lsb)]
            if slot in ve_epi:
                w.append(ve_epi[slot])
            p.o("gpsimd", lambda e, r=r, slot=slot: e.indirect_dma_start(
                out=toks[slot][:, :], out_offset=None,
                in_=xf_in,
                in_offset=bass.IndirectOffsetOnAxis(toku[:, r:r + 1], axis=0),
                bounds_check=N - 1, oob_is_err=False),
                wait=w, inc=swg[slot], k=16)
            sw_g[slot] = (swg[slot], swg[slot].count)
            for cc in range(DC):
                pw = [sw_g[slot], pe_prev_s1]
                if cc == 0:
                    pw.append(ve_ptw)
                p.o("tensor", lambda e, cc=cc, slot=slot: e.transpose(
                    ptwb[:, cc * 128:(cc + 1) * 128], toks[slot][:, cc * 128:(cc + 1) * 128],
                    idn[:, :]), wait=pw, inc=pe)
            tout = tokst[:, :].rearrange("p (c w) -> p c w", w=512)[:, 0:DC,
                                                                   i * 128:(i + 1) * 128]
            tin = ptwb[:, 0:DC * 128].rearrange("p (c j) -> p c j", j=128)
            p.o("vector", lambda e, tout=tout, tin=tin: e.tensor_copy(tout, tin),
                wait=[(pe, pe.count), (ve, ve.count)], inc=ve)
            ve_ptw = (ve, ve.count)
        ve_tT = ve.count
        ncols = len(tiles) * 128
        se_s1 = {}
        for j in range(HC):
            php = (ph0, ph1)[j % 2]
            for cc in range(DC):
                w2 = [(ve, ve_tT), (swt, swt_all)]
                if cc == 0:
                    if se_gel[j % 2]:
                        w2.append((se, se_gel[j % 2]))
                    w2.append(pe_prev_s2)
                    if tc == 0 and j == 0:
                        w2.append((sstg, 16))
                p.o("tensor", lambda e, j=j, cc=cc, php=php, ncols=ncols: e.matmul(
                    php[:, 0:ncols], fc1s[:, cc * HID + j * 128: cc * HID + (j + 1) * 128],
                    tokst[:, cc * 512: cc * 512 + ncols],
                    start=(cc == 0), stop=(cc == DC - 1)),
                    wait=w2, inc=(pe if cc == DC - 1 else None))
            p.o("scalar", lambda e, j=j, php=php, ncols=ncols: e.activation(
                hts[:, j * 512: j * 512 + ncols], php[:, 0:ncols], AF.Gelu,
                bias=b1s[:, j:j + 1], scale=1.0),
                wait=[(pe, pe.count), (sld, sld_all), pe_prev_s2], inc=se)
            se_s1[j] = se.count
            se_gel[j % 2] = se.count
        pe_prev_s1 = (pe, pe.count)
        for i, r in enumerate(tiles):
            pyp = py[i % 2]
            for ci in range(nlsb):
                c0, cn = ci * 512, min(512, D - ci * 512)
                for j in range(HC):
                    w3 = [(se, se_s1[j])]
                    if j == 0:
                        w3.append(ve_py[i % 2])
                        w3.append((swt2, swt2_all))
                    p.o("tensor", lambda e, j=j, c0=c0, cn=cn, i=i, pyp=pyp: e.matmul(
                        pyp[:, c0:c0 + cn], hts[:, j * 512 + i * 128: j * 512 + (i + 1) * 128],
                        fc2s[:, j * D + c0: j * D + c0 + cn],
                        start=(j == 0), stop=False),
                        wait=w3, inc=None)
                p.o("tensor", lambda e, c0=c0, cn=cn, pyp=pyp: e.matmul(
                    pyp[:, c0:c0 + cn], onesb[:, :], b2r[:, c0:c0 + cn],
                    start=False, stop=True), inc=(pe if ci == nlsb - 1 else None))
            pe_fc2 = pe.count
            yb = ysb[i % 2]
            slot = (tc % 2) * 4 + i
            p.o("vector", lambda e, pyp=pyp, yb=yb: e.tensor_tensor(
                yb[:, :], pyp[:, 0:D], lsb[:, :], OP.mult),
                wait=[(pe, pe_fc2), (ve, ve_lsb), (sy[i % 2], sy[i % 2].count)], inc=ve)
            p.o("vector", lambda e, yb=yb, slot=slot: e.tensor_tensor(
                yb[:, :], yb[:, :], toks[slot][:, :], OP.add),
                wait=[(ve, ve.count)], inc=ve)
            ve_y = ve.count
            ve_epi[slot] = (ve, ve_y)
            ve_py[i % 2] = (ve, ve_y)
            p.o("sync", lambda e, r=r, yb=yb: e.dma_start(
                out=y_out[r * 128:(r + 1) * 128, :], in_=yb[:, :]),
                wait=[(ve, ve_y)], inc=sy[i % 2], k=16)
        pe_prev_s2 = (pe, pe.count)

    p.barrier("sync", (sy[0], sy[0].count), (sy[1], sy[1].count))
    p.emit(nc)
    return nc


def build_l3(cfg: Cfg):
    S, D, NT, DC = cfg.S, cfg.D, cfg.NT, cfg.DC
    N, A, CAP, NST, SLOT = cfg.N, cfg.A, cfg.CAP, cfg.NST, cfg.SLOT
    E, BIG = cfg.E, cfg.BIG
    ESLOT = E * SLOT
    nc = bass.Bass("TRN2", target_bir_lowering=False)

    x2_in = nc.dram_tensor("x2", [S, D], BF16, kind="ExternalInput").ap()
    meta_in = nc.dram_tensor("meta_all", [128, 4 * A], F32, kind="ExternalInput").ap()
    y_in = nc.dram_tensor("y_all", [ESLOT, D], BF16, kind="ExternalInput").ap()
    sel_in = nc.dram_tensor("sel", [128, NT], F32, kind="ExternalInput").ap()
    idn_in = nc.dram_tensor("idn", [128, 128], BF16, kind="ExternalInput").ap()

    out_o = nc.dram_tensor("out", [S, D], F32, kind="ExternalOutput").ap()

    ea = nc.alloc_sbuf_tensor("ea", [128, 4 * A], F32)
    selb = nc.alloc_sbuf_tensor("selb", [128, NT], F32)
    idn = nc.alloc_sbuf_tensor("idn_s", [128, 128], BF16)
    idnf = nc.alloc_sbuf_tensor("idnf", [128, 128], F32)
    rel = nc.alloc_sbuf_tensor("rel", [128, 128], F32)
    utri = nc.alloc_sbuf_tensor("utri", [128, 128], BF16)
    ltri = nc.alloc_sbuf_tensor("ltri", [128, 128], BF16)
    lful = nc.alloc_sbuf_tensor("lful", [128, 128], BF16)
    xb = nc.alloc_sbuf_tensor("xb", [128, NT * D], BF16)
    itg = nc.alloc_sbuf_tensor("itg", [128, A], BF16)
    itab = nc.alloc_sbuf_tensor("itab", [128, 128], BF16)
    scb = nc.alloc_sbuf_tensor("scb", [128, 16], BF16)
    offsl = nc.alloc_sbuf_tensor("offsl", [128, 32], F32)
    posw = nc.alloc_sbuf_tensor("posw", [128, 128], F32)
    kpw = nc.alloc_sbuf_tensor("kpw", [128, 128], F32)
    slotg = [nc.alloc_sbuf_tensor(f"slotg{k}", [128, 128], F32) for k in range(2)]
    slotT = nc.alloc_sbuf_tensor("slotT", [128, 2 * NT], F32)
    slotU = nc.alloc_sbuf_tensor("slotU", [128, 2 * NT], U32)
    keptT = nc.alloc_sbuf_tensor("keptT", [128, 2 * NT], F32)
    wT = nc.alloc_sbuf_tensor("wT", [128, 2 * NT], F32)
    gb = [nc.alloc_sbuf_tensor(f"gb{i}", [128, D], BF16) for i in range(8)]
    tsc = [nc.alloc_sbuf_tensor(f"tsc{i}", [128, D], F32) for i in range(2)]
    osb = [nc.alloc_sbuf_tensor(f"osb{i}", [128, D], F32) for i in range(2)]

    itg16 = nc.alloc_sbuf_tensor("itg16", [128, 16 * A], BF16)
    t2s = nc.alloc_sbuf_tensor("t2s", [128, 16 * 128], F32)
    itab16 = nc.alloc_sbuf_tensor("itab16", [128, 16 * 128], BF16)

    ph0 = nc.alloc_psum_tensor("ph0", [128, 128], F32)
    ph1 = nc.alloc_psum_tensor("ph1", [128, 128], F32)
    ptA = nc.alloc_psum_tensor("ptA", [128, 128], BF16)
    ptB = nc.alloc_psum_tensor("ptB", [128, 128], BF16)

    p = Plan()
    sld = p.sem("sld"); sx = p.sem("sx")
    sg = [p.sem(f"sg{i}") for i in range(8)]
    so = [p.sem("so0"), p.sem("so1")]
    pe = p.sem("pe"); ve = p.sem("ve"); se = p.sem("se"); ge = p.sem("ge")

    p.o("sync", lambda e: e.dma_start(out=ea[:, :], in_=meta_in), inc=sld, k=16)
    p.o("sync", lambda e: e.dma_start(out=selb[:, :], in_=sel_in), inc=sld, k=16)
    p.o("sync", lambda e: e.dma_start(out=idn[:, :], in_=idn_in), inc=sld, k=16)
    sld_all = sld.count
    for t in range(NT):
        p.o("sync", lambda e, t=t: e.dma_start(
            out=xb[:, t * D:(t + 1) * D], in_=x2_in[t * 128:(t + 1) * 128, :]), inc=sx, k=16)
    sx_all = sx.count
    p.o("gpsimd", lambda e: e.iota(rel[:, :], pattern=[[-1, 128]], base=0,
                                   channel_multiplier=1, allow_small_or_imprecise_dtypes=True), inc=ge)
    ge_c = ge.count
    p.o("vector", lambda e: e.tensor_single_scalar(utri[:, :], rel[:, :], 0.0, OP.is_le),
        wait=[(ge, ge_c)], inc=ve)
    p.o("vector", lambda e: e.tensor_single_scalar(ltri[:, :], rel[:, :], 0.0, OP.is_lt), inc=ve)
    p.o("vector", lambda e: e.tensor_single_scalar(idnf[:, :], rel[:, :], 0.0, OP.is_equal), inc=ve)
    p.o("vector", lambda e: e.memset(lful[:, :], 1.0), inc=ve)
    p.o("vector", lambda e: e.memset(scb[:, :], 0.0), inc=ve)
    p.o("vector", lambda e: e.memset(slotg[0][:, :], BIG), inc=ve)
    p.o("vector", lambda e: e.memset(slotg[1][:, :], BIG), inc=ve)
    ve_c = ve.count

    eview = lambda q: ea[:, :].rearrange("p (a q) -> p a q", q=4)[:, :, q:q + 1].squeeze(2)

    ve_cmp = {}
    for i in range(2 * E):
        k, ex = divmod(i, E)
        p.o("vector", lambda e, k=k, ex=ex, i=i: e.tensor_single_scalar(
            itg16[:, i * A:(i + 1) * A], eview(k), float(ex), OP.is_equal),
            wait=[(sld, sld_all), (ve, ve_c)], inc=ve)
        ve_cmp[i] = ve.count
    ve_ev = {}
    for i in range(2 * E):
        php = (ph0, ph1)[i % 2]
        ptX = (ptA, ptB)[i % 2]
        w = [(ve, ve_cmp[i])]
        if i - 2 in ve_ev:
            w.append((ve, ve_ev[i - 2]))
        p.o("tensor", lambda e, i=i, php=php: e.matmul(
            php[0:A, :], itg16[:, i * A:(i + 1) * A], utri[:, :], start=True, stop=True),
            wait=w, inc=pe)
        p.o("tensor", lambda e, i=i, ptX=ptX: e.transpose(
            ptX[0:A, :], itg16[:, i * A:(i + 1) * A], idn[:, :]), inc=pe)
        pe_i = pe.count
        p.o("vector", lambda e, i=i, php=php: e.tensor_copy(
            t2s[0:A, i * 128:(i + 1) * 128], php[0:A, :]), wait=[(pe, pe_i)], inc=ve)
        p.o("vector", lambda e, i=i, ptX=ptX: e.tensor_copy(
            itab16[0:A, i * 128:(i + 1) * 128], ptX[0:A, :]), inc=ve)
        ve_ev[i] = ve.count
    ve_t2 = ve.count
    p.o("vector", lambda e: e.tensor_copy(
        scb[:, 0:16],
        t2s[:, :].rearrange("p (i c) -> p i c", c=128)[:, :, 127:128].squeeze(2)),
        wait=[(ve, ve_t2)], inc=ve)
    ve_scb = ve.count
    p.o("tensor", lambda e: e.matmul(ph1[:, 0:16], ltri[:, :], scb[:, :], start=True, stop=True),
        wait=[(ve, ve_scb)], inc=pe)
    p.o("vector", lambda e: e.tensor_copy(offsl[:, 0:16], ph1[:, 0:16]), wait=[(pe, pe.count)], inc=ve)
    p.o("tensor", lambda e: e.matmul(ph1[:, 0:16], lful[:, :], scb[:, :], start=True, stop=True),
        wait=[(ve, ve.count)], inc=pe)
    p.o("vector", lambda e: e.tensor_copy(offsl[:, 16:32], ph1[:, 0:16]), wait=[(pe, pe.count)], inc=ve)
    ve_off = ve.count

    for k in range(2):
        for ex in range(E):
            i = k * E + ex
            t2v = t2s[:, i * 128:(i + 1) * 128]
            itv = itab16[:, i * 128:(i + 1) * 128]
            if k == 0:
                p.o("vector", lambda e, ex=ex, t2v=t2v: e.tensor_scalar(
                    posw[0:A, :], t2v[0:A, :], offsl[0:A, ex:ex + 1], -1.0, OP.add, OP.add),
                    wait=[(ve, ve_off)], inc=ve)
            else:
                p.o("vector", lambda e, ex=ex, t2v=t2v: e.tensor_scalar(
                    posw[0:A, :], t2v[0:A, :], offsl[0:A, E + ex:E + ex + 1], -1.0, OP.add, OP.add),
                    wait=[(ve, ve_off)], inc=ve)
                p.o("vector", lambda e, ex=ex: e.tensor_scalar(
                    posw[0:A, :], posw[0:A, :], offsl[0:A, 16 + ex:16 + ex + 1], None, OP.add),
                    inc=ve)
            p.o("vector", lambda e: e.tensor_single_scalar(kpw[0:A, :], posw[0:A, :],
                                                           float(CAP), OP.is_lt), inc=ve)
            p.o("vector", lambda e, itv=itv: e.tensor_tensor(kpw[0:A, :], kpw[0:A, :],
                                                             itv[0:A, :], OP.mult), inc=ve)
            p.o("vector", lambda e, ex=ex: e.tensor_scalar(
                posw[0:A, :], posw[0:A, :], float(ex * SLOT - BIG), None, OP.add), inc=ve)
            p.o("vector", lambda e: e.tensor_tensor(posw[0:A, :], posw[0:A, :], kpw[0:A, :],
                                                    OP.mult), inc=ve)
            p.o("vector", lambda e, k=k: e.tensor_tensor(slotg[k][0:A, :], slotg[k][0:A, :],
                                                         posw[0:A, :], OP.add), inc=ve)
    ve_slot = ve.count

    for k in range(2):
        p.o("tensor", lambda e, k=k: e.matmul(ph0[:, 0:NT], slotg[k][0:A, :], selb[0:A, :],
                                              start=True, stop=True),
            wait=[(ve, ve.count), (pe, pe.count)], inc=pe)
        p.o("vector", lambda e, k=k: e.tensor_copy(slotT[:, k * NT:(k + 1) * NT], ph0[:, 0:NT]),
            wait=[(pe, pe.count)], inc=ve)
        p.o("tensor", lambda e, k=k: e.transpose(ph1[0:A, :], eview(2 + k), idnf[:, :]),
            wait=[(ve, ve.count), (sld, sld_all)], inc=pe)
        pe_w = pe.count
        p.o("vector", lambda e: e.tensor_copy(posw[0:A, :], ph1[0:A, 0:128]),
            wait=[(pe, pe_w)], inc=ve)
        p.o("tensor", lambda e, k=k: e.matmul(ph0[:, 0:NT], posw[0:A, :], selb[0:A, :],
                                              start=True, stop=True),
            wait=[(ve, ve.count)], inc=pe)
        p.o("vector", lambda e, k=k: e.tensor_copy(wT[:, k * NT:(k + 1) * NT], ph0[:, 0:NT]),
            wait=[(pe, pe.count)], inc=ve)
    p.o("vector", lambda e: e.tensor_single_scalar(keptT[:, :], slotT[:, :], float(ESLOT), OP.is_lt),
        wait=[(ve, ve.count)], inc=ve)
    p.o("vector", lambda e: e.tensor_tensor(wT[:, :], wT[:, :], keptT[:, :], OP.mult),
        wait=[(ve, ve.count)], inc=ve)
    p.o("vector", lambda e: e.tensor_copy(slotU[:, :], slotT[:, :]), inc=ve)
    ve_wt = ve.count

    for bi in range(8):
        p.o("gpsimd", lambda e, bi=bi: e.memset(gb[bi][:, :], 0.0), inc=ge)
    ge_gb0 = ge.count
    ev_g = {i: None for i in range(8)}
    cons = {i: (ve, 0) for i in range(8)}
    tprev = {0: (ve, 0), 1: (ve, 0)}
    for t in range(NT):
        b1i = (t % 4) * 2
        b2i = b1i + 1
        for k, bi in ((0, b1i), (1, b2i)):
            w = [(ve, ve_wt), (ge, ge_gb0), cons[bi]]
            p.o("gpsimd", lambda e, t=t, k=k, bi=bi: e.indirect_dma_start(
                out=gb[bi][:, :], out_offset=None,
                in_=y_in,
                in_offset=bass.IndirectOffsetOnAxis(slotU[:, k * NT + t:k * NT + t + 1], axis=0),
                bounds_check=ESLOT - 1, oob_is_err=False),
                wait=w, inc=sg[bi], k=16)
            ev_g[bi] = (sg[bi], sg[bi].count)
        par = t % 2
        tb = tsc[par]
        ob = osb[par]
        p.o("scalar", lambda e, t=t, tb=tb, b1i=b1i: e.activation(
            tb[:, :], gb[b1i][:, :], AF.Copy, bias=0.0, scale=wT[:, t:t + 1]),
            wait=[ev_g[b1i], (ve, ve_wt), tprev[par], (se, se.count)], inc=se)
        se_t1 = se.count
        cons[b1i] = (se, se_t1)
        p.o("vector", lambda e, t=t, tb=tb, b2i=b2i: e.scalar_tensor_tensor(
            tb[:, :], gb[b2i][:, :], wT[:, NT + t:NT + t + 1], tb[:, :], OP.mult, OP.add),
            wait=[ev_g[b2i], (se, se_t1), (so[par], so[par].count), (ve, ve.count)], inc=ve)
        cons[b2i] = (ve, ve.count)
        p.o("vector", lambda e, t=t, tb=tb, ob=ob: e.tensor_tensor(
            ob[:, :], tb[:, :], xb[:, t * D:(t + 1) * D], OP.add),
            wait=[(sx, sx_all), (ve, ve.count)], inc=ve)
        ve_o = ve.count
        tprev[par] = (ve, ve_o)
        p.o("sync", lambda e, t=t, ob=ob: e.dma_start(
            out=out_o[t * 128:(t + 1) * 128, :], in_=ob[:, :]),
            wait=[(ve, ve_o)], inc=so[par], k=16)

    p.barrier("sync", (so[0], so[0].count), (so[1], so[1].count))
    p.emit(nc)
    return nc


_IDN = np.eye(128, dtype=np.float32).astype(ml_dtypes.bfloat16)


def kernel(L0_attn_x, L0_attn_W_kqv, L0_attn_m, L0_ffn_router_w, L0_ffn_router_b,
           L0_ffn_fc1_w, L0_ffn_fc1_b, L0_ffn_fc2_w, L0_ffn_fc2_b, L0_ffn_layer_scale):
    cfg = Cfg(S=2048, D=1024, H=16, B=8, HID=4096, E=8)
    B, NT, HC = cfg.B, cfg.NT, cfg.HC
    cores = list(range(8))

    x = np.ascontiguousarray(np.asarray(L0_attn_x, dtype=np.float32))
    wk = np.asarray(L0_attn_W_kqv, dtype=np.float32)
    m = np.ascontiguousarray(np.asarray(L0_attn_m, dtype=np.float32).reshape(cfg.H, 1))
    rw = np.asarray(L0_ffn_router_w, dtype=np.float32)
    rb = np.asarray(L0_ffn_router_b, dtype=np.float32)
    fc1 = np.asarray(L0_ffn_fc1_w, dtype=np.float32)
    fc1b = np.asarray(L0_ffn_fc1_b, dtype=np.float32)
    fc2 = np.asarray(L0_ffn_fc2_w, dtype=np.float32)
    fc2b = np.asarray(L0_ffn_fc2_b, dtype=np.float32)
    ls = np.asarray(L0_ffn_layer_scale, dtype=np.float32)

    wkt = _tile_wkqv(wk)
    rwt = np.ascontiguousarray(rw.T)
    rbr = np.ascontiguousarray(rb[None, :])

    nc1 = build_l1(cfg)
    in1 = [{"x": x[c], "wkqv_t": wkt, "m": m, "rwt": rwt, "rb": rbr, "idn": _IDN}
           for c in cores]
    r1 = run_bass_kernel_spmd(nc1, in1, core_ids=cores)
    metas = [r1.results[c]["meta"] for c in cores]
    xfs = [r1.results[c]["xfo"] for c in cores]
    x2s = [r1.results[c]["x2o"] for c in cores]

    meta_all = np.ascontiguousarray(np.concatenate(metas, axis=1))
    xf_all = np.ascontiguousarray(np.concatenate(xfs, axis=0))

    nc2 = build_l2(cfg)
    in2 = [{
        "meta_all": meta_all, "xf_all": xf_all,
        "fc1t": np.ascontiguousarray(fc1[c].T),
        "fc2t": np.ascontiguousarray(fc2[c].T),
        "b1t": np.ascontiguousarray(fc1b[c].reshape(HC, 128).T),
        "b2r": np.ascontiguousarray(fc2b[c][None, :]),
        "lsr": np.ascontiguousarray(ls[c][None, :]),
        "eid": np.full((128, 1), float(c), np.float32),
        "idn": _IDN,
    } for c in cores]
    r2 = run_bass_kernel_spmd(nc2, in2, core_ids=cores)
    y_all = np.ascontiguousarray(np.concatenate([r2.results[c]["y"] for c in cores], axis=0))

    nc3 = build_l3(cfg)
    in3 = []
    for c in cores:
        sel = np.zeros((128, NT), np.float32)
        for r in range(NT):
            sel[c * NT + r, r] = 1.0
        in3.append({"x2": x2s[c], "meta_all": meta_all, "y_all": y_all,
                    "sel": sel, "idn": _IDN})
    r3 = run_bass_kernel_spmd(nc3, in3, core_ids=cores)
    out = np.stack([r3.results[c]["out"] for c in cores], axis=0)
    return out


# version 25 (final)
# speedup vs baseline: 1.0030x; 1.0001x over previous
import math
import numpy as np
import ml_dtypes
import concourse.bass as bass
import concourse.mybir as mybir
from concourse.bass_utils import run_bass_kernel_spmd

F32 = mybir.dt.float32
BF16 = mybir.dt.bfloat16
I32 = mybir.dt.int32
U32 = mybir.dt.uint32
AF = mybir.ActivationFunctionType
OP = mybir.AluOpType
AX = mybir.AxisListType

ENGINES = ("sync", "tensor", "vector", "scalar", "gpsimd")


class Sem:
    __slots__ = ("name", "count")

    def __init__(self, name):
        self.name = name
        self.count = 0


class Plan:

    def __init__(self):
        self.ops = {e: [] for e in ENGINES}
        self.sems = {}
        self._last_wait = {e: {} for e in ENGINES}
        self.targets = {}

    def sem(self, name) -> Sem:
        if name not in self.sems:
            self.sems[name] = Sem(name)
        return self.sems[name]

    def _resolve_waits(self, eng, wait):
        wl = []
        for w in wait:
            s, v = (w if isinstance(w, tuple) else (w, w.count))
            if v <= 0:
                continue
            if v > self._last_wait[eng].get(s.name, -1):
                self._last_wait[eng][s.name] = v
                wl.append((s.name, v))
                self.targets.setdefault(s.name, set()).add(v)
        return wl

    def o(self, eng, fn, wait=(), inc=None, k=1):
        wl = self._resolve_waits(eng, wait)
        postv = None
        if inc is not None:
            inc.count += k
            postv = inc.count
        self.ops[eng].append((fn, wl, (inc.name if inc else None), k, postv))

    def barrier(self, eng, *sems):
        wl = self._resolve_waits(eng, sems)
        self.ops[eng].append((None, wl, None, 0, None))

    def emit(self, nc):
        handles = {s.name: nc.alloc_semaphore(name=s.name) for s in self.sems.values()}
        tsorted = {n: sorted(v) for n, v in self.targets.items()}
        emitted = {n: 0 for n in self.sems}
        with nc.Block() as blk:
            for eng in ENGINES:
                if not self.ops[eng]:
                    continue
                ops = self.ops[eng]

                def body(e, ops=ops):
                    for fn, wl, incname, k, postv in ops:
                        for sname, v in wl:
                            e.wait_ge(handles[sname], v)
                        if fn is None:
                            continue
                        ins = fn(e)
                        if incname is None:
                            continue
                        if k == 16:
                            ins.then_inc(handles[incname], k)
                            continue
                        tl = self.targets.get(incname)
                        if tl is not None and postv in tl:
                            gap = postv - emitted[incname]
                            emitted[incname] = postv
                            ins.then_inc(handles[incname], gap)

                getattr(blk, eng)(body)


class Cfg:
    def __init__(self, S=2048, D=1024, H=16, B=8, HID=4096, E=8):
        self.S, self.D, self.H, self.B, self.HID, self.E = S, D, H, B, HID, E
        self.hd = 64
        self.NT = S // 128
        self.DC = D // 128
        self.HC = HID // 128
        self.N = B * S
        self.A = self.N // 128
        self.scale = math.sqrt(D)
        self.K = 2
        self.CAP = int(1.2 * (self.N // E + 1))
        self.NST = (self.CAP + 127) // 128
        self.SLOT = self.NST * 128
        self.EPS = 1e-9
        self.LNEPS = 1e-5
        self.BIG = float(1 << 22)
        self.SH = S // 2


L1_UPTO = 99


def build_l1(cfg: Cfg):
    S, D, H, NT, DC, SH = cfg.S, cfg.D, cfg.H, cfg.NT, cfg.DC, cfg.SH
    HP = H // 2
    QW = min(512, S)
    QWP = 512
    QT = QW // 128
    nc = bass.Bass("TRN2", target_bir_lowering=False)

    x_in = nc.dram_tensor("x", [S, D], F32, kind="ExternalInput").ap()
    wkq_in = nc.dram_tensor("wkqv_t", [3 * DC * DC * 128, 128], F32, kind="ExternalInput").ap()
    m_in = nc.dram_tensor("m", [H, 1], F32, kind="ExternalInput").ap()
    rwt_in = nc.dram_tensor("rwt", [DC * 128, 8], F32, kind="ExternalInput").ap()
    rb_in = nc.dram_tensor("rb", [1, 8], F32, kind="ExternalInput").ap()
    idn_in = nc.dram_tensor("idn", [128, 128], BF16, kind="ExternalInput").ap()

    x2_out = nc.dram_tensor("x2o", [S, D], BF16, kind="ExternalOutput").ap()
    xf_out = nc.dram_tensor("xfo", [S, D], BF16, kind="ExternalOutput").ap()
    meta_out = nc.dram_tensor("meta", [128, NT * 4], F32, kind="ExternalOutput").ap()

    xb = nc.alloc_sbuf_tensor("xb", [128, NT * D], BF16)
    arena1 = nc.alloc_sbuf_tensor("arena1", [128, 4 * DC * S], mybir.dt.uint8)
    ab = arena1[:, :].bitcast(BF16)
    ct = ab[:, 0:DC * S]
    kTv = ab[:, DC * S:2 * DC * S]
    kTz = ab
    qT = nc.alloc_sbuf_tensor("qT", [128, DC * S], BF16)
    vb = nc.alloc_sbuf_tensor("vb", [128, NT * H * 65 + 128], BF16)
    wv = nc.alloc_sbuf_tensor("wv", [128, DC * 512], BF16)
    wsl = [nc.alloc_sbuf_tensor(f"wsl{i}", [128, DC * 128], BF16) for i in range(2)]
    idn = nc.alloc_sbuf_tensor("idn_s", [128, 128], BF16)
    xsc = [nc.alloc_sbuf_tensor(f"xsc{i}", [128, D], BF16) for i in range(2)]
    dump8 = [nc.alloc_sbuf_tensor(f"dump8{i}", [128, 8], F32) for i in range(2)]
    st_ = nc.alloc_sbuf_tensor("st_", [128, 20 * NT], F32)
    qgrid = nc.alloc_sbuf_tensor("qgrid", [128, S], BF16)
    mgrid = nc.alloc_sbuf_tensor("mgrid", [128, 128], F32)
    mb = nc.alloc_sbuf_tensor("mb", [128, 32], F32)

    mcol = nc.alloc_sbuf_tensor("mcol", [128, 1], BF16)
    mrow = nc.alloc_sbuf_tensor("mrow", [1, 128], BF16)
    onesb = nc.alloc_sbuf_tensor("onesb", [1, 128], BF16)
    nrlT2 = nc.alloc_sbuf_tensor("nrlT2", [128, H * 128], BF16)
    dump = nc.alloc_sbuf_tensor("dump", [128, D], BF16)
    pstrip = nc.alloc_sbuf_tensor("pstrip", [128, 8 * QW], BF16)
    biasall = nc.alloc_sbuf_tensor("biasall", [128, H * NT], F32)
    kvals = nc.alloc_sbuf_tensor("kvals", [128, NT], F32)
    nrl = nc.alloc_sbuf_tensor("nrl", [128, 32], F32)
    avt = nc.alloc_sbuf_tensor("avt", [128, 2 * QW], BF16)
    rcol = nc.alloc_sbuf_tensor("rcol", [128, 4], F32)
    lgt = nc.alloc_sbuf_tensor("lgt", [128, 8 * NT], F32)
    top8 = nc.alloc_sbuf_tensor("top8", [128, 8 * NT], F32)
    tpi = nc.alloc_sbuf_tensor("tpi", [128, 8 * NT], U32)
    tw = nc.alloc_sbuf_tensor("tw", [128, 13 * NT], F32)
    meta = nc.alloc_sbuf_tensor("meta_s", [128, 4 * NT], F32)
    rwt = nc.alloc_sbuf_tensor("rwt_s", [128, DC * 8], BF16)
    rbr = nc.alloc_sbuf_tensor("rbr", [1, 8], BF16)
    ccol = nc.alloc_sbuf_tensor("ccol", [128, 2], F32)

    pS0 = nc.alloc_psum_tensor("pS0", [128, 2 * QWP], F32)
    pS1 = nc.alloc_psum_tensor("pS1", [128, 2 * QWP], F32)
    pvs = [nc.alloc_psum_tensor(f"pv{i}", [128, QWP], F32) for i in range(4)]

    LNB = [("b0", pS0[:, 0:512]), ("b1", pS0[:, 512:1024]),
           ("b2", pS1[:, 0:512]), ("b3", pS1[:, 512:1024]),
           ("b4", pvs[0][:, 0:512]), ("b5", pvs[1][:, 0:512]),
           ("b6", pvs[2][:, 0:512]), ("b7", pvs[3][:, 0:512])]

    p = Plan()
    sxg = [p.sem(f"sxg{i}") for i in range(4)]
    swv = p.sem("swv"); sld = p.sem("sld")
    swkp = [p.sem("swk0"), p.sem("swk1")]
    sxfp = [p.sem("sxf0"), p.sem("sxf1")]
    scst = p.sem("scst")
    pe = p.sem("pe"); ve = p.sem("ve"); se = p.sem("se"); ge = p.sem("ge")
    sso = p.sem("sso")

    evac = {}

    p.o("sync", lambda e: e.dma_start(out=idn[:, :], in_=idn_in), inc=sld, k=16)
    p.o("gpsimd", lambda e: e.memset(mcol[:, :], 0.0), inc=ge)
    gm = ge.count
    scm = p.sem("scm")
    p.o("gpsimd", lambda e: e.dma_start(out=mcol[0:H, :], in_=m_in), wait=[(ge, gm)], inc=scm, k=16)
    p.o("gpsimd", lambda e: e.iota(mgrid[:, :], pattern=[[-1, 128]], base=0,
                                   channel_multiplier=1,
                                   allow_small_or_imprecise_dtypes=True), inc=ge)
    p.o("gpsimd", lambda e: e.iota(qgrid[:, :], pattern=[[1, S]], base=0,
                                   channel_multiplier=0,
                                   allow_small_or_imprecise_dtypes=True), inc=ge)
    p.o("gpsimd", lambda e: e.iota(kvals[:, :], pattern=[[128, NT]], base=0,
                                   channel_multiplier=1,
                                   allow_small_or_imprecise_dtypes=True), inc=ge)
    p.o("gpsimd", lambda e: e.memset(onesb[:, :], 1.0), inc=ge)
    p.o("gpsimd", lambda e: e.memset(ccol[:, 0:1], cfg.LNEPS), inc=ge)
    p.o("gpsimd", lambda e: e.memset(ccol[:, 1:2], 1.0 / cfg.scale), inc=ge)
    p.o("gpsimd", lambda e: e.memset(nrlT2[:, :], 0.0), inc=ge)
    ge_c = ge.count
    GSZ = (NT + 3) // 4
    slab_cnt = [0, 0]

    def slab_load(jt, slot, after):
        base = jt * DC * 128
        slab_cnt[slot] += 1
        p.o("gpsimd", lambda e, base=base, slot=slot: e.dma_start(
            out=wsl[slot][:, :].rearrange("p (c j) -> p c j", j=128),
            in_=wkq_in[base:base + DC * 128, :].rearrange("(c p) j -> p c j", p=128)),
            wait=[after] if after else [], inc=swkp[slot], k=16)
        return (swkp[slot], slab_cnt[slot] * 16)

    w0 = w1 = None
    for t in range(NT):
        p.o("gpsimd", lambda e, t=t: e.dma_start(
            out=xb[:, t * D:(t + 1) * D], in_=x_in[t * 128:(t + 1) * 128, :]),
            inc=sxg[t // GSZ], k=16)
        if t == GSZ - 1:
            w0 = slab_load(0, 0, None)
            w1 = slab_load(1, 1, None)
    xall = [(s, s.count) for s in sxg if s.count]
    VJ0 = 2 * DC
    swv2 = p.sem("swv2")
    NVC = (D + 511) // 512
    TPH = DC // NVC

    def load_wv_half(jc, sem, after=None, ccs=None):
        for cc in (range(DC) if ccs is None else ccs):
            for jj in range(TPH):
                jt = jc * TPH + jj
                base = ((VJ0 + jt) * DC + cc) * 128
                p.o("gpsimd", lambda e, cc=cc, jj=jj, base=base: e.dma_start(
                    out=wv[:, cc * 512 + jj * 128: cc * 512 + (jj + 1) * 128],
                    in_=wkq_in[base:base + 128, :]),
                    wait=[after] if after else [], inc=sem, k=16)

    p.o("gpsimd", lambda e: e.dma_start(
        out=rwt[:, :].rearrange("p (c e) -> p c e", e=8),
        in_=rwt_in.rearrange("(c p) e -> p c e", p=128)), inc=scst, k=16)
    p.o("gpsimd", lambda e: e.dma_start(out=rbr[:, :], in_=rb_in), inc=scst, k=16)
    p.o("vector", lambda e: e.tensor_single_scalar(mgrid[:, :], mgrid[:, :], 0.0, OP.is_gt),
        wait=[(ge, ge_c)], inc=ve)
    p.o("vector", lambda e: e.tensor_scalar(mgrid[:, :], mgrid[:, :], -1e30, None, OP.mult),
        wait=[(ve, ve.count)], inc=ve)
    p.o("vector", lambda e: e.memset(
        vb[:, 0:NT * H * 65].rearrange("p (t h d) -> p t h d", h=H, d=65)[:, :, :, 64:65], 1.0),
        inc=ve)
    p.o("vector", lambda e: e.memset(vb[:, NT * H * 65:NT * H * 65 + 128], 0.0), inc=ve)
    ve_vones = ve.count
    p.o("tensor", lambda e: e.transpose(pS0[:, 0:512].bitcast(BF16)[0:1, 0:128],
                                        mcol[:, :], idn[:, :]),
        wait=[(sld, 16), (scm, 16)], inc=pe)
    p.o("vector", lambda e: e.tensor_copy(mrow[:, :], pS0[:, 0:512].bitcast(BF16)[0:1, 0:128]),
        wait=[(pe, pe.count)], inc=ve)
    p.o("tensor", lambda e: e.matmul(pS0[:, 0:128], onesb[:, :], mrow[:, :], start=True, stop=True),
        wait=[(ve, ve.count), (ge, ge_c)], inc=pe)
    p.o("vector", lambda e: e.tensor_copy(mb[:, 0:H], pS0[:, 0:H]), wait=[(pe, pe.count)], inc=ve)
    evac["b0"] = (ve, ve.count)
    p.o("vector", lambda e: e.tensor_scalar(nrl[:, 0:H], mb[:, 0:H], 0.0, -1.0, OP.max, OP.mult),
        wait=[(ve, ve.count)], inc=ve)

    SUM, SQ, MEAN, EX2, MSQ, VAR, SD, RSTD, NMR = (i * NT for i in range(9))
    GS = min(8, DC)

    def ln_stats_tile(o, t, sw):
        p.o("vector", lambda e, t=t: e.reduce_sum(
            st_[:, o + SUM + t: o + SUM + t + 1], xb[:, t * D:(t + 1) * D], axis=AX.X),
            wait=[*sw, (ve, ve.count)], inc=ve)
        p.o("scalar", lambda e, t=t: e.activation(
            dump[:, :], xb[:, t * D:(t + 1) * D], AF.Square, bias=0.0, scale=1.0,
            accum_out=st_[:, o + SQ + t: o + SQ + t + 1]),
            wait=[*sw, (se, se.count)], inc=se)

    def ln_chain(o, tlo=0, thi=None):
        thi = NT if thi is None else thi
        p.o("vector", lambda e: e.tensor_scalar(
            st_[:, o + MEAN + tlo:o + MEAN + thi], st_[:, o + SUM + tlo:o + SUM + thi], 1.0 / D, None, OP.mult),
            wait=[(ve, ve.count), (se, se.count)], inc=ve)
        p.o("vector", lambda e: e.tensor_scalar(
            st_[:, o + EX2 + tlo:o + EX2 + thi], st_[:, o + SQ + tlo:o + SQ + thi], 1.0 / D, None, OP.mult), inc=ve)
        p.o("vector", lambda e: e.tensor_tensor(
            st_[:, o + MSQ + tlo:o + MSQ + thi], st_[:, o + MEAN + tlo:o + MEAN + thi],
            st_[:, o + MEAN + tlo:o + MEAN + thi], OP.mult), wait=[(ve, ve.count)], inc=ve)
        p.o("vector", lambda e: e.tensor_tensor(
            st_[:, o + VAR + tlo:o + VAR + thi], st_[:, o + EX2 + tlo:o + EX2 + thi],
            st_[:, o + MSQ + tlo:o + MSQ + thi], OP.subtract), wait=[(ve, ve.count)], inc=ve)
        p.o("vector", lambda e: e.tensor_scalar(
            st_[:, o + VAR + tlo:o + VAR + thi], st_[:, o + VAR + tlo:o + VAR + thi], 0.0, None, OP.max),
            wait=[(ve, ve.count)], inc=ve)
        p.o("scalar", lambda e: e.activation(
            st_[:, o + SD + tlo:o + SD + thi], st_[:, o + VAR + tlo:o + VAR + thi], AF.Sqrt,
            bias=ccol[:, 0:1], scale=1.0), wait=[(ve, ve.count), (ge, ge_c)], inc=se)
        p.o("vector", lambda e: e.reciprocal(
            st_[:, o + RSTD + tlo:o + RSTD + thi], st_[:, o + SD + tlo:o + SD + thi]),
            wait=[(se, se.count)], inc=ve)
        p.o("vector", lambda e: e.scalar_tensor_tensor(
            st_[:, o + NMR + tlo:o + NMR + thi], st_[:, o + MEAN + tlo:o + MEAN + thi], -1.0,
            st_[:, o + RSTD + tlo:o + RSTD + thi], OP.mult, OP.mult), wait=[(ve, ve.count)], inc=ve)
        return ve.count

    pe_xsc = {0: 0, 1: 0}

    def ln_finish(o, dst_ct, ve_stats, out_store=None, extra_wait=(), tiles=None, ci0=0):
        marks = []
        ci = ci0
        for t in (range(NT) if tiles is None else tiles):
            par = t % 2
            buf = xsc[par]
            xw = [(ve, ve_stats), *extra_wait]
            if pe_xsc[par]:
                xw.append((pe, pe_xsc[par]))
            if out_store is not None and sxfp[par].count > 0:
                xw.append((sxfp[par], sxfp[par].count))
            p.o("scalar", lambda e, t=t, buf=buf: e.activation(
                buf[:, :], xb[:, t * D:(t + 1) * D], AF.Identity,
                bias=st_[:, o + NMR + t:o + NMR + t + 1],
                scale=st_[:, o + RSTD + t:o + RSTD + t + 1]),
                wait=xw, inc=se)
            se_t = se.count
            if out_store is not None:
                dram_fn = out_store
                p.o("sync", lambda e, t=t, buf=buf, dram_fn=dram_fn: e.dma_start(
                    out=dram_fn(t), in_=buf[:, :]), wait=[(se, se_t)], inc=sxfp[par], k=16)
            for g0 in range(0, DC, GS):
                bname, bank = LNB[(ci // GS) % 8]
                bankb = bank.bitcast(BF16)
                for j in range(GS):
                    cc = g0 + j
                    w_ = [(se, se_t), (sld, 16), *extra_wait]
                    if j == 0 and bname in evac:
                        w_.append(evac[bname])
                    p.o("tensor", lambda e, bankb=bankb, j=j, buf=buf, cc=cc: e.transpose(
                        bankb[:, j * 128:(j + 1) * 128], buf[:, cc * 128:(cc + 1) * 128],
                        idn[:, :]), wait=w_, inc=pe)
                ctv = dst_ct[:, :].rearrange("p (c s) -> p c s", s=S)[:, g0:g0 + GS,
                                                                     t * 128:(t + 1) * 128]
                bv = bankb[:, 0:GS * 128].rearrange("p (c j) -> p c j", j=128)
                p.o("vector", lambda e, ctv=ctv, bv=bv: e.tensor_copy(ctv, bv),
                    wait=[(pe, pe.count), (ve, ve.count)], inc=ve)
                evac[bname] = (ve, ve.count)
                ci += GS
            pe_xsc[par] = pe.count
            marks.append(ve.count)
        return marks

    def _finish():
        vv = ve.count
        for t in range(NT):
            p.o("sync", lambda e, t=t: e.dma_start(
                out=x2_out[t * 128:(t + 1) * 128, :], in_=xb[:, t * D:(t + 1) * D]),
                wait=[(ve, vv), (se, se.count), (pe, pe.count), *xall], inc=sso, k=16)
        p.barrier("sync", (sso, sso.count))
        p.emit(nc)
        return nc

    if L1_UPTO <= 0:
        return _finish()
    marks1 = []
    for glo in range(0, NT, GSZ):
        ghi = min(glo + GSZ, NT)
        for t in range(glo, ghi):
            ln_stats_tile(0, t, [(sxg[t // GSZ], sxg[t // GSZ].count)])
        vs_g = ln_chain(0, glo, ghi)
        marks1 += ln_finish(0, ct, vs_g, tiles=range(glo, ghi), ci0=glo * (DC // GS))
    ve_xnT = ve.count
    for hx in range(H):
        p.o("vector", lambda e, hx=hx: e.tensor_scalar(
            nrlT2[0:1, hx * 128:(hx + 1) * 128], onesb[:, :], nrl[0:1, hx:hx + 1],
            None, OP.mult), wait=[(ve, ve.count), (ge, ge_c)], inc=ve)
    for h_ in range(H):
        p.o("vector", lambda e, h_=h_: e.tensor_scalar(
            biasall[:, h_ * NT:(h_ + 1) * NT], kvals[:, :], mb[:, h_:h_ + 1], None, OP.mult),
            wait=[(ve, ve.count), (ge, ge_c)], inc=ve)

    if L1_UPTO <= 1:
        return _finish()

    swk_w = {0: w0, 1: w1}
    QCW = min(512, S)
    QC = (S + QCW - 1) // QCW
    TPQ = QCW // 128
    for jt in range(2 * DC):
        slot = jt % 2
        swk_need = swk_w[slot]
        dst = kTv if jt < DC else qT
        dcc = jt % DC
        scl = 1.0 if jt < DC else 1.0 / cfg.scale
        for qc in range(QC):
            ps = (pS0, pS1)[qc % 2]
            bname = ("b0", "b2")[qc % 2]
            for cc in range(DC):
                w = [swk_need, (ve, marks1[min((qc + 1) * TPQ - 1, NT - 1)])]
                if cc == 0 and bname in evac:
                    w.append(evac[bname])
                p.o("tensor", lambda e, qc=qc, cc=cc, ps=ps, slot=slot: e.matmul(
                    ps[:, 0:QCW], wsl[slot][:, cc * 128:(cc + 1) * 128],
                    ct[:, cc * S + qc * QCW: cc * S + (qc + 1) * QCW],
                    start=(cc == 0), stop=(cc == DC - 1)),
                    wait=w, inc=(pe if cc == DC - 1 else None))
            scl_ap = (1.0 if scl == 1.0 else ccol[:, 1:2])
            p.o("scalar", lambda e, dst=dst, dcc=dcc, qc=qc, ps=ps, scl_ap=scl_ap: e.activation(
                dst[:, dcc * S + qc * QCW: dcc * S + (qc + 1) * QCW], ps[:, 0:QCW], AF.Copy,
                bias=0.0, scale=scl_ap), wait=[(pe, pe.count), (ge, ge_c)], inc=se)
            evac[bname] = (se, se.count)
        if jt + 2 < 2 * DC:
            swk_w[slot] = slab_load(jt + 2, slot, (pe, pe.count))
        if jt == 3:
            load_wv_half(0, swv, ccs=range(0, DC // 2))
        if jt == 9:
            load_wv_half(0, swv, ccs=range(DC // 2, DC))
            swv_all = swv.count
    VCW = min(512, D)
    VC = (D + VCW - 1) // VCW
    HPC = VCW // 64
    for jc in range(VC):
        if jc > 0:
            load_wv_half(jc, swv2, after=(pe, pe.count))
        wvsem = (swv, swv_all) if jc == 0 else (swv2, swv2.count)
        for t in range(NT):
            ps = (pS0, pS1)[(jc * NT + t) % 2]
            bname = ("b0", "b2")[(jc * NT + t) % 2]
            for cc in range(DC):
                w = [wvsem, (ve, ve_xnT)]
                if cc == 0 and bname in evac:
                    w.append(evac[bname])
                p.o("tensor", lambda e, t=t, cc=cc, ps=ps: e.matmul(
                    ps[:, 0:VCW], ct[:, cc * S + t * 128:cc * S + (t + 1) * 128],
                    wv[:, cc * 512:cc * 512 + VCW],
                    start=(cc == 0), stop=(cc == DC - 1)),
                    wait=w, inc=(pe if cc == DC - 1 else None))
            p.o("vector", lambda e, t=t, jc=jc, ps=ps: e.tensor_copy(
                vb[:, t * H * 65 + jc * HPC * 65: t * H * 65 + (jc * HPC + HPC) * 65]
                  .rearrange("p (h d) -> p h d", d=65)[:, :, 0:64],
                ps[:, 0:VCW].rearrange("p (h d) -> p h d", d=64)),
                wait=[(pe, pe.count), (ve, ve_vones)], inc=ve)
            evac[bname] = (ve, ve.count)
    se_kqv = se.count
    ve_kqv = ve.count
    pe_kqv = pe.count

    ve_rp = {}
    ge_rp = {}
    ve_cp = {}
    for hp_ in range(HP):
        c0 = hp_ * S
        s0 = (DC + hp_) * S
        p.o("vector", lambda e, c0=c0, s0=s0: e.tensor_copy(
            kTz[0:64, c0:c0 + S], kTz[0:64, s0:s0 + S]),
            wait=[(se, se_kqv), (pe, pe_kqv), (ve, ve.count)], inc=ve)
        ve_cp[hp_] = ve.count
        ve_rp[hp_] = ve.count
    for hp_ in range(HP):
        c0 = hp_ * S
        s0 = (DC + hp_) * S
        p.o("gpsimd", lambda e, c0=c0: e.memset(kTz[64:128, c0:c0 + S], 0.0),
            wait=[(pe, pe_kqv), (ge, ge.count)], inc=ge)
        p.o("gpsimd", lambda e, s0=s0: e.memset(kTz[0:64, s0:s0 + S], 0.0),
            wait=[(se, se_kqv), (ve, ve_cp[hp_]), (ge, ge.count)], inc=ge)
        ge_rp[hp_] = ge.count
    if L1_UPTO <= 2:
        return _finish()

    p.o("vector", lambda e: e.memset(avt[64:128, :], 0.0), inc=ve)
    ve_avt0 = ve.count
    ge_qg = ge_c

    NW = S // QW
    pend = [None, -1, -1]
    pav_free = {}
    avt_tp = {0: 0, 1: 0}
    last_av = {}
    LN2O = 10 * NT

    ln2_pend = []

    def emit_ln2_tile():
        if not ln2_pend:
            return
        t = ln2_pend.pop(0)
        ln_stats_tile(LN2O, t, [(ve, ve.count)])
        p.o("sync", lambda e, t=t: e.dma_start(
            out=x2_out[t * 128:(t + 1) * 128, :], in_=xb[:, t * D:(t + 1) * D]),
            wait=[(ve, ve.count)], inc=sso, k=16)

    def emit_ln2_chunk(wd):
        ln2_pend.extend(range(wd * QT, (wd + 1) * QT))

    inst = 0
    order = []
    for j_ in range(NW // 2):
        for hp_i in range(HP):
            order.append((NW - 1 - j_, hp_i))
            order.append((j_, hp_i))
    if NW % 2:
        for hp_i in range(HP):
            order.append((NW // 2, hp_i))
    wdone = {}
    for (w, hp) in order:
        wlo, whi = w * QW, (w + 1) * QW
        kts = [kt for kt in range(NT) if kt * 128 < whi]
        nk = len(kts)
        h0, h1 = 2 * hp, 2 * hp + 1
        ipar = inst % 2
        se_exp = {}
        av_pe = {}

        def emit_av(ki, X, hx, first, last, w=w, wlo=wlo, whi=whi, kts=kts, ipar=ipar):
            kp = kts[ki]
            q0p = max(wlo, kp * 128)
            nqp = whi - q0p
            slot = X * 4 + (ki % 4)
            pav = pvs[X * 2 + ipar]
            bank = f"b{4 + X * 2 + ipar}"
            avw = [(se, se_exp[(ki, X)])]
            if first:
                avw.append(pav_free.get((X, ipar)) or evac.get(bank, (ve, 0)))
            p.o("tensor", lambda e, kp=kp, hx=hx, q0p=q0p, nqp=nqp, wlo=wlo,
                pav=pav, slot=slot, first=first, last=last: e.matmul(
                pav[0:128, q0p - wlo: q0p - wlo + nqp],
                vb[:, kp * H * 65 + hx * 65: kp * H * 65 + hx * 65 + 128],
                pstrip[:, slot * QW: slot * QW + nqp],
                start=first, stop=last),
                wait=avw, inc=pe)
            av_pe[(ki, X)] = pe.count
            last_av[(ki % 4, X)] = pe.count

        for ki, kt in enumerate(kts):
            q0 = max(wlo, kt * 128)
            nq = whi - q0
            for X, hx in ((0, h0), (1, h1)):
                par = ki % 2
                pst = (pS0, pS1)[par]
                bank = f"b{par * 2 + X}"
                w1_ = [(se, se_kqv), (ve, ve_kqv), (ve, ve_rp[hp]), (ge, ge_rp[hp])]
                if bank in evac:
                    w1_.append(evac[bank])
                reg = hp if X == 0 else DC + hp
                p.o("tensor", lambda e, hp=hp, kt=kt, q0=q0, nq=nq, pst=pst, X=X, reg=reg: e.matmul(
                    pst[:, X * QWP: X * QWP + nq],
                    kTz[:, reg * S + kt * 128: reg * S + (kt + 1) * 128],
                    qT[:, hp * S + q0: hp * S + q0 + nq],
                    start=True, stop=False), wait=w1_, inc=None)
                p.o("tensor", lambda e, pst=pst, X=X, q0=q0, nq=nq, hx=hx: e.matmul(
                    pst[:, X * QWP: X * QWP + nq],
                    nrlT2[:, hx * 128:(hx + 1) * 128],
                    qgrid[:, q0:q0 + nq],
                    start=False, stop=True), inc=pe)
                pe_mm = pe.count
                ve_strip = 0
                if q0 == kt * 128:
                    p.o("vector", lambda e, pst=pst, X=X: e.tensor_tensor(
                        pst[:, X * QWP: X * QWP + 128], pst[:, X * QWP: X * QWP + 128],
                        mgrid[:, :], OP.add),
                        wait=[(pe, pe_mm), (ve, ve.count)], inc=ve)
                    ve_strip = ve.count
                slot = X * 4 + (ki % 4)
                ew = [(pe, pe_mm)]
                if ve_strip:
                    ew.append((ve, ve_strip))
                if (ki % 4, X) in last_av:
                    ew.append((pe, last_av[(ki % 4, X)]))
                p.o("scalar", lambda e, pst=pst, X=X, nq=nq, kt=kt, hx=hx, slot=slot: e.activation(
                    pstrip[:, slot * QW: slot * QW + nq], pst[:, X * QWP: X * QWP + nq],
                    AF.Exp, bias=biasall[:, hx * NT + kt: hx * NT + kt + 1], scale=1.0),
                    wait=ew, inc=se)
                evac[bank] = (se, se.count)
                se_exp[(ki, X)] = se.count
            if ki >= 2:
                for X, hx in ((0, h0), (1, h1)):
                    emit_av(ki - 2, X, hx, first=(ki - 2 == 0), last=False)
            if ki == 1 and pend[0] is not None:
                emit_ln2_tile()
                fw = pend[1]
                pend[0]()
                pend[0] = None
                wdone[fw] = wdone.get(fw, 0) + 1
                if wdone[fw] == HP:
                    emit_ln2_chunk(fw)
        for kl in (nk - 2, nk - 1):
            for X, hx in ((0, h0), (1, h1)):
                emit_av(kl, X, hx, first=(kl == 0), last=(kl == nk - 1))
        pe_av_end = pe.count

        def mk_norm(w=w, h0=h0, h1=h1, pe_av_end=pe_av_end, ipar=ipar):
            def norm():
                ve_avt = {}
                for X in (0, 1):
                    pav = pvs[X * 2 + ipar]
                    aw = [(pe, pe_av_end), (ve, max(ve_avt0, ve.count))]
                    if avt_tp[X]:
                        aw.append((pe, avt_tp[X]))
                    p.o("vector", lambda e, X=X, pav=pav: e.tensor_copy(
                        avt[0:65, X * QW:(X + 1) * QW], pav[0:65, 0:QW]),
                        wait=aw, inc=ve)
                    ve_avt[X] = ve.count
                for X, hx in ((0, h0), (1, h1)):
                    pav = pvs[X * 2 + ipar]
                    pavb = pav[:, :].bitcast(BF16)
                    bank = f"b{4 + X * 2 + ipar}"
                    for qt in range(QT):
                        tw_ = [(ve, ve_avt[X])]
                        if qt == 0 and bank in evac:
                            tw_.append(evac[bank])
                        p.o("tensor", lambda e, X=X, qt=qt, pavb=pavb: e.transpose(
                            pavb[:, qt * 128:(qt + 1) * 128],
                            avt[:, X * QW + qt * 128: X * QW + (qt + 1) * 128],
                            idn[:, :]), wait=tw_, inc=pe)
                    avt_tp[X] = pe.count
                    pe_tp = pe.count
                    for qt in range(QT):
                        p.o("vector", lambda e, X=X, qt=qt, pavb=pavb: e.reciprocal(
                            rcol[:, X:X + 1], pavb[:, qt * 128 + 64: qt * 128 + 65]),
                            wait=[(pe, pe_tp), (ve, ve.count)], inc=ve)
                        qti = w * QT + qt
                        p.o("vector", lambda e, X=X, hx=hx, qt=qt, qti=qti, pavb=pavb: e.scalar_tensor_tensor(
                            xb[:, qti * D + hx * 64: qti * D + hx * 64 + 64],
                            pavb[:, qt * 128: qt * 128 + 64], rcol[:, X:X + 1],
                            xb[:, qti * D + hx * 64: qti * D + hx * 64 + 64],
                            OP.mult, OP.add), wait=[(ve, ve.count)], inc=ve)
                    pav_free[(X, ipar)] = (ve, ve.count)
                    evac[bank] = (ve, ve.count)
            return norm
        pend[0], pend[1], pend[2] = mk_norm(), w, hp
        inst += 1
    if pend[0] is not None:
        fw = pend[1]
        pend[0]()
        pend[0] = None
        wdone[fw] = wdone.get(fw, 0) + 1
        if wdone[fw] == HP:
            emit_ln2_chunk(fw)
    while ln2_pend:
        emit_ln2_tile()
    ve_attn = ve.count
    pe_attn = pe.count
    se_attn = se.count
    if L1_UPTO <= 3:
        return _finish()

    ln2_marks = ln_finish(LN2O, ct, ln_chain(LN2O),
                          out_store=(lambda t: xf_out[t * 128:(t + 1) * 128, :]),
                          extra_wait=[(pe, pe_attn)])
    ve_xfT = ve.count
    if L1_UPTO <= 4:
        p.barrier("sync", (sso, sso.count), (sxfp[0], sxfp[0].count), (sxfp[1], sxfp[1].count))
        p.emit(nc)
        return nc

    for t in range(NT):
        ps = (pS0, pS1)[t % 2]
        bname = ("b0", "b2")[t % 2]
        for cc in range(DC):
            w = [(ve, ln2_marks[t]), (scst, scst.count)]
            if cc == 0 and bname in evac:
                w.append(evac[bname])
            p.o("tensor", lambda e, t=t, cc=cc, ps=ps: e.matmul(
                ps[:, 0:8], ct[:, cc * S + t * 128: cc * S + (t + 1) * 128],
                rwt[:, cc * 8:(cc + 1) * 8], start=(cc == 0), stop=False),
                wait=w, inc=None)
        p.o("tensor", lambda e, ps=ps: e.matmul(
            ps[:, 0:8], onesb[:, :], rbr[:, :], start=False, stop=True), inc=pe)
        p.o("vector", lambda e, t=t, ps=ps: e.tensor_copy(lgt[:, t * 8:(t + 1) * 8], ps[:, 0:8]),
            wait=[(pe, pe.count)], inc=ve)
        evac[bname] = (ve, ve.count)
    ve_lgt = ve.count
    for t in range(NT):
        p.o("vector", lambda e, t=t: e.max(top8[:, t * 8:(t + 1) * 8], lgt[:, t * 8:(t + 1) * 8]),
            wait=[(ve, ve_lgt)], inc=ve)
        p.o("vector", lambda e, t=t: e.max_index(
            tpi[:, t * 8:(t + 1) * 8], top8[:, t * 8:(t + 1) * 8], lgt[:, t * 8:(t + 1) * 8]),
            wait=[(ve, ve.count)], inc=ve)
    ve_top = ve.count
    v1v = lambda: top8[:, :].rearrange("p (t e) -> p t e", e=8)[:, :, 0:1]
    v2v = lambda: top8[:, :].rearrange("p (t e) -> p t e", e=8)[:, :, 1:2]
    NV = NT
    D21, NV1, SSUM, P1, P2, DEN, RDEN, W1, W2, VM1, VM2, E21 = (i * NV for i in range(12))
    p.o("vector", lambda e: e.tensor_tensor(tw[:, D21:D21 + NV], v2v(), v1v(), OP.subtract),
        wait=[(ve, ve_top)], inc=ve)
    p.o("vector", lambda e: e.tensor_scalar(tw[:, NV1:NV1 + NV], v1v(), -1.0, None, OP.mult),
        inc=ve)
    ve_nv1 = ve.count
    for t in range(NT):
        d8 = dump8[t % 2]
        p.o("scalar", lambda e, t=t, d8=d8: e.activation(
            d8[:, :], lgt[:, t * 8:(t + 1) * 8], AF.Exp,
            bias=tw[:, NV1 + t:NV1 + t + 1], scale=1.0,
            accum_out=tw[:, SSUM + t:SSUM + t + 1]),
            wait=[(ve, ve_nv1), (se, se.count)], inc=se)
    p.o("scalar", lambda e: e.activation(tw[:, E21:E21 + NV], tw[:, D21:D21 + NV],
                                         AF.Exp, bias=0.0, scale=1.0),
        wait=[(ve, ve_nv1)], inc=se)
    se_probs = se.count
    p.o("vector", lambda e: e.reciprocal(tw[:, P1:P1 + NV], tw[:, SSUM:SSUM + NV]),
        wait=[(se, se_probs)], inc=ve)
    p.o("vector", lambda e: e.tensor_tensor(tw[:, P2:P2 + NV], tw[:, E21:E21 + NV],
                                            tw[:, P1:P1 + NV], OP.mult),
        wait=[(ve, ve.count)], inc=ve)
    p.o("vector", lambda e: e.tensor_tensor(tw[:, DEN:DEN + NV], tw[:, P1:P1 + NV],
                                            tw[:, P2:P2 + NV], OP.add),
        wait=[(ve, ve.count)], inc=ve)
    p.o("vector", lambda e: e.tensor_scalar(tw[:, DEN:DEN + NV], tw[:, DEN:DEN + NV],
                                            cfg.EPS, None, OP.add),
        wait=[(ve, ve.count)], inc=ve)
    p.o("vector", lambda e: e.reciprocal(tw[:, RDEN:RDEN + NV], tw[:, DEN:DEN + NV]),
        wait=[(ve, ve.count)], inc=ve)
    p.o("vector", lambda e: e.tensor_tensor(tw[:, W1:W1 + NV], tw[:, P1:P1 + NV],
                                            tw[:, RDEN:RDEN + NV], OP.mult),
        wait=[(ve, ve.count)], inc=ve)
    p.o("vector", lambda e: e.tensor_tensor(tw[:, W2:W2 + NV], tw[:, P2:P2 + NV],
                                            tw[:, RDEN:RDEN + NV], OP.mult),
        wait=[(ve, ve.count)], inc=ve)
    p.o("vector", lambda e: e.tensor_single_scalar(tw[:, VM1:VM1 + NV], tw[:, W1:W1 + NV],
                                                   cfg.EPS, OP.is_gt),
        wait=[(ve, ve.count)], inc=ve)
    p.o("vector", lambda e: e.tensor_single_scalar(tw[:, VM2:VM2 + NV], tw[:, W2:W2 + NV],
                                                   cfg.EPS, OP.is_gt),
        wait=[(ve, ve.count)], inc=ve)
    ve_w = ve.count
    mview = lambda q: meta[:, :].rearrange("p (t q) -> p t q", q=4)[:, :, q:q + 1]
    iview = lambda c: tpi[:, :].rearrange("p (t e) -> p t e", e=8)[:, :, c:c + 1]
    p.o("vector", lambda e: e.tensor_copy(tw[:, DEN:DEN + NV], iview(0)), wait=[(ve, ve_w)], inc=ve)
    p.o("vector", lambda e: e.tensor_scalar(tw[:, DEN:DEN + NV], tw[:, DEN:DEN + NV], -8.0, None, OP.add),
        wait=[(ve, ve.count)], inc=ve)
    p.o("vector", lambda e: e.scalar_tensor_tensor(
        mview(0), tw[:, DEN:DEN + NV], 1.0, tw[:, VM1:VM1 + NV], OP.mult, OP.mult),
        wait=[(ve, ve.count)], inc=ve)
    p.o("vector", lambda e: e.tensor_scalar(mview(0), mview(0), 8.0, None, OP.add),
        wait=[(ve, ve.count)], inc=ve)
    p.o("vector", lambda e: e.tensor_copy(tw[:, SSUM:SSUM + NV], iview(1)), inc=ve)
    p.o("vector", lambda e: e.tensor_scalar(tw[:, SSUM:SSUM + NV], tw[:, SSUM:SSUM + NV], -8.0, None, OP.add),
        wait=[(ve, ve.count)], inc=ve)
    p.o("vector", lambda e: e.scalar_tensor_tensor(
        mview(1), tw[:, SSUM:SSUM + NV], 1.0, tw[:, VM2:VM2 + NV], OP.mult, OP.mult),
        wait=[(ve, ve.count)], inc=ve)
    p.o("vector", lambda e: e.tensor_scalar(mview(1), mview(1), 8.0, None, OP.add),
        wait=[(ve, ve.count)], inc=ve)
    p.o("vector", lambda e: e.tensor_copy(mview(2), tw[:, W1:W1 + NV]), inc=ve)
    p.o("vector", lambda e: e.tensor_copy(mview(3), tw[:, W2:W2 + NV]), inc=ve)
    ve_meta = ve.count
    p.o("sync", lambda e: e.dma_start(out=meta_out, in_=meta[:, :]), wait=[(ve, ve_meta)],
        inc=sso, k=16)

    p.barrier("sync", (sso, sso.count), (sxfp[0], sxfp[0].count), (sxfp[1], sxfp[1].count))
    p.emit(nc)
    return nc


def _tile_wkqv(wk):
    D = wk.shape[0]
    J, DCc = wk.shape[1] // 128, D // 128
    t = wk.reshape(DCc, 128, J, 128).transpose(2, 0, 1, 3).reshape(J * DCc * 128, 128)
    return np.ascontiguousarray(t)


def _np_ln(x, eps=1e-5):
    m = x.mean(-1, keepdims=True)
    v = x.var(-1, keepdims=True)
    return (x - m) / np.sqrt(v + eps)


def build_l2(cfg: Cfg):
    S, D, H, NT, DC, HC = cfg.S, cfg.D, cfg.H, cfg.NT, cfg.DC, cfg.HC
    N, A, CAP, NST, SLOT, HID = cfg.N, cfg.A, cfg.CAP, cfg.NST, cfg.SLOT, cfg.HID
    BIG = cfg.BIG
    assert A <= 128
    nc = bass.Bass("TRN2", target_bir_lowering=False)

    meta_in = nc.dram_tensor("meta_all", [128, 4 * A], F32, kind="ExternalInput").ap()
    xf_in = nc.dram_tensor("xf_all", [N, D], BF16, kind="ExternalInput").ap()
    fc1_in = nc.dram_tensor("fc1t", [DC * 128, HID], F32, kind="ExternalInput").ap()
    fc2_in = nc.dram_tensor("fc2t", [HID, D], F32, kind="ExternalInput").ap()
    b1_in = nc.dram_tensor("b1t", [128, HC], F32, kind="ExternalInput").ap()
    b2_in = nc.dram_tensor("b2r", [1, D], F32, kind="ExternalInput").ap()
    ls_in = nc.dram_tensor("lsr", [1, D], F32, kind="ExternalInput").ap()
    eid_in = nc.dram_tensor("eid", [128, 1], F32, kind="ExternalInput").ap()
    idn_in = nc.dram_tensor("idn", [128, 128], BF16, kind="ExternalInput").ap()
    stg_d = nc.dram_tensor("stg_d", [128 * 6 * A, 1], F32).ap()

    y_out = nc.dram_tensor("y", [SLOT, D], BF16, kind="ExternalOutput").ap()

    ea = nc.alloc_sbuf_tensor("ea", [128, 4 * A], F32)
    eid = nc.alloc_sbuf_tensor("eid_s", [128, 1], F32)
    idn = nc.alloc_sbuf_tensor("idn_s", [128, 128], BF16)
    idnf = nc.alloc_sbuf_tensor("idnf", [128, 128], F32)
    rel = nc.alloc_sbuf_tensor("rel", [128, 128], F32)
    qabs = nc.alloc_sbuf_tensor("qabs", [128, 128], F32)
    cbr = nc.alloc_sbuf_tensor("cbr", [128, 4], F32)
    nidx = nc.alloc_sbuf_tensor("nidx", [128, A], F32)
    scolg = nc.alloc_sbuf_tensor("scolg", [128, NST], F32)
    utri = nc.alloc_sbuf_tensor("utri", [128, 128], BF16)
    ltri = nc.alloc_sbuf_tensor("ltri", [128, 128], BF16)
    lful = nc.alloc_sbuf_tensor("lful", [128, 128], BF16)
    shft = nc.alloc_sbuf_tensor("shft", [128, 128], BF16)
    itk = [nc.alloc_sbuf_tensor(f"itk{k}", [128, A], BF16) for k in range(2)]
    itab = [nc.alloc_sbuf_tensor(f"itab{k}", [128, 128], BF16) for k in range(2)]
    scb = nc.alloc_sbuf_tensor("scb", [128, 2], BF16)
    offs = nc.alloc_sbuf_tensor("offs", [128, 8], F32)
    dcb = nc.alloc_sbuf_tensor("dcb", [128, 4], BF16)
    lpg = [nc.alloc_sbuf_tensor(f"lpg{k}", [128, 128], F32) for k in range(2)]
    posg = [nc.alloc_sbuf_tensor(f"posg{k}", [128, 128], F32) for k in range(2)]
    lpt = [nc.alloc_sbuf_tensor(f"lpt{k}", [128, A], F32) for k in range(2)]
    rhs3 = [nc.alloc_sbuf_tensor(f"rhs3{k}", [128, 3 * A], F32) for k in range(2)]
    mscr = [nc.alloc_sbuf_tensor(f"mscr{i}", [128, 128], F32) for i in range(3)]
    arena = nc.alloc_sbuf_tensor("arena", [128, 32768], mybir.dt.uint8)
    stg = arena[:, 0:6 * A * 4].bitcast(F32)
    hts = arena.bitcast(BF16)
    cnt = nc.alloc_sbuf_tensor("cnt", [128, 4 * NST], F32)
    cwork = nc.alloc_sbuf_tensor("cwork", [128, 10 * NST], F32)
    tokf = nc.alloc_sbuf_tensor("tokf", [128, NST], F32)
    toku = nc.alloc_sbuf_tensor("toku", [128, NST], U32)
    gcolu = nc.alloc_sbuf_tensor("gcolu", [128, NST], U32)
    toks = [nc.alloc_sbuf_tensor(f"toks{i}", [128, D], BF16) for i in range(8)]
    tokst = nc.alloc_sbuf_tensor("tokst", [128, DC * 512], BF16)
    ysb = [nc.alloc_sbuf_tensor(f"ysb{i}", [128, D], BF16) for i in range(2)]
    fc1s = nc.alloc_sbuf_tensor("fc1s", [128, DC * HID], BF16)
    fc2s = nc.alloc_sbuf_tensor("fc2s", [128, HC * D], BF16)
    b1s = nc.alloc_sbuf_tensor("b1s", [128, HC], F32)
    b2r = nc.alloc_sbuf_tensor("b2r_s", [1, D], BF16)
    lsb = nc.alloc_sbuf_tensor("lsb", [128, D], BF16)
    onesb = nc.alloc_sbuf_tensor("onesb", [1, 128], BF16)
    onecol = nc.alloc_sbuf_tensor("onecol", [128, 1], F32)

    ph0 = nc.alloc_psum_tensor("ph0", [128, 512], F32)
    ph1 = nc.alloc_psum_tensor("ph1", [128, 512], F32)
    proute = nc.alloc_psum_tensor("proute", [128, 512], F32)
    ptw = nc.alloc_psum_tensor("ptw", [128, 512], F32)
    ptwb = ptw[:, :].bitcast(BF16)
    py = [nc.alloc_psum_tensor(f"py{i}", [128, D], F32) for i in range(2)]

    p = Plan()
    sld = p.sem("sld"); swt = p.sem("swt")
    sng = p.sem("sng")
    swg = [p.sem(f"swg{i}") for i in range(8)]
    sy = [p.sem("sy0"), p.sem("sy1")]
    sstg = p.sem("sstg")
    pe = p.sem("pe"); ve = p.sem("ve"); se = p.sem("se"); ge = p.sem("ge")

    p.o("sync", lambda e: e.dma_start(out=ea[:, :], in_=meta_in), inc=sld, k=16)
    p.o("sync", lambda e: e.dma_start(out=eid[:, :], in_=eid_in), inc=sld, k=16)
    p.o("sync", lambda e: e.dma_start(out=idn[:, :], in_=idn_in), inc=sld, k=16)
    p.o("sync", lambda e: e.dma_start(out=b1s[:, :], in_=b1_in), inc=sld, k=16)
    sld_all = sld.count
    swt2 = p.sem("swt2")
    sls = p.sem("sls")
    p.o("gpsimd", lambda e: e.dma_start(out=toks[0][0:1, 0:D], in_=ls_in), inc=sls, k=16)
    p.o("gpsimd", lambda e: e.dma_start(out=b2r[:, :], in_=b2_in), inc=swt2, k=16)
    for cc in range(DC):
        p.o("gpsimd", lambda e, cc=cc: e.dma_start(
            out=fc1s[:, cc * HID:(cc + 1) * HID], in_=fc1_in[cc * 128:(cc + 1) * 128, :]),
            inc=swt, k=16)
    for hc in range(HC):
        p.o("gpsimd", lambda e, hc=hc: e.dma_start(
            out=fc2s[:, hc * D:(hc + 1) * D], in_=fc2_in[hc * 128:(hc + 1) * 128, :]),
            inc=swt2, k=16)
    swt_all = swt.count
    swt2_all = swt2.count

    p.o("gpsimd", lambda e: e.iota(rel[:, :], pattern=[[-1, 128]], base=0,
                                   channel_multiplier=1, allow_small_or_imprecise_dtypes=True), inc=ge)
    ge_rel = ge.count
    p.o("gpsimd", lambda e: e.iota(qabs[:, :], pattern=[[1, 128]], base=0,
                                   channel_multiplier=0, allow_small_or_imprecise_dtypes=True), inc=ge)
    p.o("gpsimd", lambda e: e.iota(nidx[:, :], pattern=[[128, A]], base=0,
                                   channel_multiplier=1, allow_small_or_imprecise_dtypes=True), inc=ge)
    p.o("gpsimd", lambda e: e.iota(scolg[:, :], pattern=[[128, NST]], base=0,
                                   channel_multiplier=1, allow_small_or_imprecise_dtypes=True), inc=ge)
    p.o("gpsimd", lambda e: e.memset(onesb[:, :], 1.0), inc=ge)
    p.o("gpsimd", lambda e: e.memset(onecol[:, :], 1.0), inc=ge)
    p.o("gpsimd", lambda e: e.memset(tokf[:, :], 0.0), inc=ge)
    ge_c = ge.count
    p.o("vector", lambda e: e.tensor_single_scalar(utri[:, :], rel[:, :], 0.0, OP.is_le),
        wait=[(ge, ge_rel)], inc=ve)
    p.o("vector", lambda e: e.tensor_single_scalar(ltri[:, :], rel[:, :], 0.0, OP.is_lt), inc=ve)
    p.o("vector", lambda e: e.tensor_single_scalar(shft[:, :], rel[:, :], -1.0, OP.is_equal), inc=ve)
    p.o("vector", lambda e: e.tensor_single_scalar(idnf[:, :], rel[:, :], 0.0, OP.is_equal), inc=ve)
    p.o("vector", lambda e: e.memset(lful[:, :], 1.0), inc=ve)
    ve_c = ve.count

    eview = lambda q: ea[:, :].rearrange("p (a q) -> p a q", q=4)[:, :, q:q + 1].squeeze(2)
    for k in range(2):
        p.o("vector", lambda e, k=k: e.tensor_scalar(
            itk[k][:, :], eview(k), eid[:, 0:1], None, OP.is_equal),
            wait=[(sld, sld_all), (ve, ve_c)], inc=ve)
    ve_it = ve.count
    p.o("vector", lambda e: e.memset(scb[:, :], 0.0), inc=ve)
    ve_scb0 = ve.count
    t2ps = {0: ph0, 1: ph1}
    for k in range(2):
        p.o("tensor", lambda e, k=k: e.matmul(
            t2ps[k][0:A, 0:128], itk[k][:, :], utri[:, :], start=True, stop=True),
            wait=[(ve, ve_it)], inc=pe)
    pe_t2 = pe.count
    for k in range(2):
        p.o("vector", lambda e, k=k: e.tensor_copy(scb[0:A, k:k + 1], t2ps[k][0:A, 127:128]),
            wait=[(pe, pe_t2), (ve, ve_scb0)], inc=ve)
    ve_scb = ve.count
    p.o("tensor", lambda e: e.matmul(proute[:, 0:2], ltri[:, :], scb[:, :], start=True, stop=True),
        wait=[(ve, ve_scb)], inc=pe)
    p.o("vector", lambda e: e.tensor_copy(offs[:, 0:2], proute[:, 0:2]), wait=[(pe, pe.count)], inc=ve)
    p.o("tensor", lambda e: e.matmul(proute[:, 2:4], lful[:, :], scb[:, :], start=True, stop=True),
        wait=[(ve, ve.count)], inc=pe)
    p.o("vector", lambda e: e.tensor_copy(offs[:, 2:4], proute[:, 2:4]), wait=[(pe, pe.count)], inc=ve)
    p.o("tensor", lambda e: e.matmul(proute[:, 4:6], shft[:, :], scb[:, :], start=True, stop=True),
        wait=[(ve, ve.count)], inc=pe)
    p.o("vector", lambda e: e.tensor_copy(dcb[:, 1:2], proute[:, 4:5]), wait=[(pe, pe.count)], inc=ve)
    p.o("vector", lambda e: e.tensor_copy(dcb[:, 3:4], proute[:, 5:6]), wait=[(ve, ve.count)], inc=ve)
    p.o("vector", lambda e: e.memset(dcb[:, 0:1], 1.0), inc=ve)
    p.o("vector", lambda e: e.memset(dcb[:, 2:3], 1.0), inc=ve)
    p.o("vector", lambda e: e.tensor_tensor(offs[:, 4:5], offs[:, 1:2], offs[:, 2:3], OP.add),
        wait=[(ve, ve.count)], inc=ve)
    ve_offs = ve.count

    for k in range(2):
        cbcol = (offs[:, 0:1], offs[:, 4:5])[k]
        p.o("vector", lambda e, k=k, cbcol=cbcol: e.tensor_scalar(
            posg[k][0:A, :], t2ps[k][0:A, 0:128], cbcol[0:A, :], -1.0, OP.add, OP.add),
            wait=[(ve, ve_offs)], inc=ve)
        p.o("vector", lambda e, k=k: e.tensor_scalar(
            lpg[k][0:A, :], t2ps[k][0:A, 0:128], -1.0, None, OP.add),
            wait=[(ve, ve.count)], inc=ve)
    ve_pos = ve.count
    for k in range(2):
        p.o("tensor", lambda e, k=k: e.transpose(ptwb[0:A, k * 128:(k + 1) * 128], itk[k][:, :], idn[:, :]),
            wait=[(ve, ve_pos), (pe, pe.count)], inc=pe)
        p.o("vector", lambda e, k=k: e.tensor_copy(itab[k][0:A, :], ptwb[0:A, k * 128:(k + 1) * 128]),
            wait=[(pe, pe.count)], inc=ve)
        p.o("vector", lambda e, k=k: e.scalar_tensor_tensor(
            lpg[k][0:A, :], itab[k][0:A, :], 1e6, lpg[k][0:A, :], OP.mult, OP.add),
            wait=[(ve, ve.count)], inc=ve)
        p.o("vector", lambda e, k=k: e.tensor_scalar(
            lpg[k][0:A, :], lpg[k][0:A, :], -1e6, None, OP.add),
            wait=[(ve, ve.count)], inc=ve)
        ve_lpm = ve.count
        p.o("tensor", lambda e, k=k: e.transpose(
            proute[:, 384:384 + A], lpg[k][0:A, :], idnf[0:A, 0:A]),
            wait=[(ve, ve_lpm), (pe, pe.count)], inc=pe)
        p.o("vector", lambda e, k=k: e.tensor_copy(lpt[k][:, 0:A], proute[:, 384:384 + A]),
            wait=[(pe, pe.count)], inc=ve)
        p.o("tensor", lambda e, k=k: e.transpose(
            proute[:, 384:384 + A], posg[k][0:A, :], idnf[0:A, 0:A]),
            wait=[(ve, ve.count), (pe, pe.count)], inc=pe)
        p.o("vector", lambda e, k=k: e.tensor_copy(
            rhs3[k][:, :].rearrange("p (a q) -> p a q", q=3)[:, :, 1:2].squeeze(2),
            proute[:, 384:384 + A]),
            wait=[(pe, pe.count)], inc=ve)
    ve_tr = ve.count
    for k in range(2):
        p.o("vector", lambda e, k=k: e.tensor_copy(
            rhs3[k][:, :].rearrange("p (a q) -> p a q", q=3)[:, :, 0:1].squeeze(2), nidx[:, :]),
            wait=[(ve, ve_tr), (ge, ge_c)], inc=ve)
        p.o("vector", lambda e, k=k: e.memset(
            rhs3[k][:, :].rearrange("p (a q) -> p a q", q=3)[:, :, 2:3].squeeze(2), 1.0), inc=ve)
    ve_rhs = ve.count

    C1, O1, C2, O2 = 0, NST, 2 * NST, 3 * NST
    CNTC, OFFSEL, T1T, GG = 4 * NST, 5 * NST, 6 * NST, 7 * NST
    pe_cnt0 = pe.count
    mscr6 = mscr + [lpg[0], lpg[1], posg[0]]
    cnt_pe = {}
    for r in range(NST):
        for k in range(2):
            idx = 2 * r + k
            cbcol = (offs[:, 0:1], offs[:, 4:5])[k]
            ms = mscr6[idx % 6]
            cw_ = [(ve, ve.count)]
            if idx - 6 in cnt_pe:
                cw_.append((pe, cnt_pe[idx - 6]))
            elif idx < 6:
                cw_.append((pe, pe_cnt0))
            p.o("vector", lambda e, r=r, k=k, idx=idx, cbcol=cbcol: e.tensor_scalar(
                cbr[:, idx % 4:idx % 4 + 1], cbcol, float(-128 * r), None, OP.add),
                wait=cw_, inc=ve)
            p.o("vector", lambda e, idx=idx, ms=ms: e.tensor_scalar(
                ms[:, :].bitcast(BF16)[:, 0:128], qabs[:, 0:128],
                cbr[:, idx % 4:idx % 4 + 1], None, OP.is_ge),
                wait=[(ve, ve.count)], inc=ve)
            p.o("tensor", lambda e, r=r, k=k, ms=ms: e.matmul(
                ph0[:, 4 * (r % 8) + 2 * k: 4 * (r % 8) + 2 * k + 2],
                ms[:, :].bitcast(BF16)[0:A, 0:128], dcb[0:A, 2 * k:2 * k + 2],
                start=True, stop=True),
                wait=[(ve, ve.count)], inc=pe)
            cnt_pe[idx] = pe.count
        if r % 8 == 7 or r == NST - 1:
            lo = (r // 8) * 8
            p.o("vector", lambda e, r=r, lo=lo: e.tensor_copy(
                cnt[:, 4 * lo:4 * (r + 1)], ph0[:, 0:4 * (r + 1 - lo)]),
                wait=[(pe, pe.count)], inc=ve)
    ve_cnt = ve.count
    cview = lambda q: cnt[:, :].rearrange("p (r q) -> p r q", q=4)[:, :, q:q + 1].squeeze(2)
    p.o("vector", lambda e: e.tensor_tensor(cwork[:, CNTC:CNTC + NST], cview(0), cview(2), OP.add),
        wait=[(ve, ve_cnt)], inc=ve)
    TMPA = 8 * NST
    p.o("vector", lambda e: e.tensor_scalar(cwork[:, TMPA:TMPA + NST], cview(3), offs[:, 2:3],
                                            None, OP.add), inc=ve)
    p.o("vector", lambda e: e.tensor_tensor(cwork[:, TMPA:TMPA + NST], cwork[:, TMPA:TMPA + NST],
                                            cview(1), OP.subtract),
        wait=[(ve, ve.count)], inc=ve)
    p.o("vector", lambda e: e.tensor_scalar(cwork[:, T1T:T1T + NST], scolg[:, :], offs[:, 2:3],
                                            None, OP.is_ge), inc=ve)
    p.o("vector", lambda e: e.tensor_tensor(cwork[:, TMPA:TMPA + NST], cwork[:, TMPA:TMPA + NST],
                                            cwork[:, T1T:T1T + NST], OP.mult),
        wait=[(ve, ve.count)], inc=ve)
    p.o("vector", lambda e: e.tensor_tensor(cwork[:, OFFSEL:OFFSEL + NST], cview(1),
                                            cwork[:, TMPA:TMPA + NST], OP.add),
        wait=[(ve, ve.count)], inc=ve)
    p.o("vector", lambda e: e.tensor_tensor(cwork[:, GG:GG + NST], scolg[:, :],
                                            cwork[:, OFFSEL:OFFSEL + NST], OP.subtract),
        wait=[(ve, ve.count)], inc=ve)
    p.o("vector", lambda e: e.tensor_scalar(cwork[:, GG:GG + NST], cwork[:, GG:GG + NST],
                                            float(6 * A), None, OP.mult),
        wait=[(ve, ve.count)], inc=ve)
    p.o("vector", lambda e: e.tensor_scalar(cwork[:, T1T:T1T + NST], cwork[:, CNTC:CNTC + NST],
                                            3.0, -3.0, OP.mult, OP.add),
        wait=[(ve, ve.count)], inc=ve)
    p.o("vector", lambda e: e.tensor_tensor(cwork[:, GG:GG + NST], cwork[:, GG:GG + NST],
                                            cwork[:, T1T:T1T + NST], OP.add),
        wait=[(ve, ve.count)], inc=ve)
    p.o("vector", lambda e: e.tensor_copy(gcolu[:, :], cwork[:, GG:GG + NST]),
        wait=[(ve, ve.count)], inc=ve)
    ve_g = ve.count
    pe_cnt_end = pe.count

    nch = 2 * A
    ev_m = (ve, ve_rhs)
    pe_m = {}
    for batch in range(0, nch, 128):
        bn = min(128, nch - batch)
        for i in range(bn):
            ch = batch + i
            k, a = ch // A, ch % A
            ms = mscr6[ch % 6]
            bw = [(ve, ve_rhs)]
            if ch - 6 in pe_m:
                bw.append((pe, pe_m[ch - 6]))
            else:
                bw.append((pe, pe_cnt_end))
            p.o("vector", lambda e, k=k, a=a, ms=ms: e.tensor_scalar(
                ms[:, :], qabs[:, 0:128], lpt[k][:, a:a + 1], None, OP.is_equal),
                wait=bw, inc=ve)
            p.o("tensor", lambda e, k=k, a=a, i=i, ms=ms: e.matmul(
                proute[:, 3 * i:3 * i + 3], ms[:, :], rhs3[k][:, 3 * a:3 * a + 3],
                start=True, stop=True),
                wait=[(ve, ve.count), ev_m], inc=pe)
            pe_m[ch] = pe.count
        p.o("vector", lambda e, batch=batch, bn=bn: e.tensor_copy(
            stg[:, 3 * batch:3 * (batch + bn)], proute[:, 0:3 * bn]),
            wait=[(pe, pe.count)], inc=ve)
        ev_m = (ve, ve.count)
    ve_stg = ve.count
    p.o("scalar", lambda e: e.dma_start(
        out=stg_d.rearrange("(p f) o -> p (f o)", p=128), in_=stg[:, :]),
        wait=[(ve, ve_stg)], inc=sstg, k=16)

    for r in range(NST):
        p.o("gpsimd", lambda e, r=r: e.indirect_dma_start(
            out=tokf[:, r:r + 1], out_offset=None,
            in_=stg_d,
            in_offset=bass.IndirectOffsetOnAxis(gcolu[:, r:r + 1], axis=0),
            bounds_check=128 * 6 * A - 1, oob_is_err=False),
            wait=[(ve, ve_g), (sstg, 16), (ge, ge_c)], inc=sng, k=16)
    p.o("vector", lambda e: e.tensor_copy(toku[:, :], tokf[:, :]),
        wait=[(sng, 16 * NST)], inc=ve)
    ve_tok = ve.count

    nlsb = (D + 511) // 512
    for i in range(nlsb):
        c0, cn = i * 512, min(512, D - i * 512)
        p.o("tensor", lambda e, c0=c0, cn=cn: e.matmul(
            py[0][:, c0:c0 + cn], onesb[:, :], toks[0][0:1, c0:c0 + cn], start=True, stop=True),
            wait=[(sls, 16), (ge, ge_c)], inc=pe)
    p.o("vector", lambda e: e.tensor_copy(lsb[:, :], py[0][:, 0:D]), wait=[(pe, pe.count)], inc=ve)
    ve_lsb = ve.count
    pe_lsb = pe.count

    NTC = NST * 128 // 512
    assert NST * 128 % 512 == 0 or NTC * 512 < NST * 128
    if NST * 128 % 512 != 0:
        NTC += 1
    pe_prev_s1 = (pe, 0)
    pe_prev_s2 = (pe, 0)
    se_gel = {0: 0, 1: 0}
    ve_epi = {}
    ve_py = {0: (ve, 0), 1: (ve, 0)}
    ve_ptw = (ve, ve.count)
    sw_g = {}
    for tc in range(NTC):
        tiles = [r for r in range(tc * 4, min(tc * 4 + 4, NST))]
        for i, r in enumerate(tiles):
            slot = (tc % 2) * 4 + i
            w = [(ve, ve_tok), (ge, ge_c), (pe, pe_lsb)]
            if slot in ve_epi:
                w.append(ve_epi[slot])
            p.o("gpsimd", lambda e, r=r, slot=slot: e.indirect_dma_start(
                out=toks[slot][:, :], out_offset=None,
                in_=xf_in,
                in_offset=bass.IndirectOffsetOnAxis(toku[:, r:r + 1], axis=0),
                bounds_check=N - 1, oob_is_err=False),
                wait=w, inc=swg[slot], k=16)
            sw_g[slot] = (swg[slot], swg[slot].count)
            for cc in range(DC):
                pw = [sw_g[slot], pe_prev_s1]
                if cc == 0:
                    pw.append(ve_ptw)
                p.o("tensor", lambda e, cc=cc, slot=slot: e.transpose(
                    ptwb[:, cc * 128:(cc + 1) * 128], toks[slot][:, cc * 128:(cc + 1) * 128],
                    idn[:, :]), wait=pw, inc=pe)
            tout = tokst[:, :].rearrange("p (c w) -> p c w", w=512)[:, 0:DC,
                                                                   i * 128:(i + 1) * 128]
            tin = ptwb[:, 0:DC * 128].rearrange("p (c j) -> p c j", j=128)
            p.o("vector", lambda e, tout=tout, tin=tin: e.tensor_copy(tout, tin),
                wait=[(pe, pe.count), (ve, ve.count)], inc=ve)
            ve_ptw = (ve, ve.count)
        ve_tT = ve.count
        ncols = len(tiles) * 128
        se_s1 = {}
        for j in range(HC):
            php = (ph0, ph1)[j % 2]
            for cc in range(DC):
                w2 = [(ve, ve_tT), (swt, swt_all)]
                if cc == 0:
                    if se_gel[j % 2]:
                        w2.append((se, se_gel[j % 2]))
                    w2.append(pe_prev_s2)
                    if tc == 0 and j == 0:
                        w2.append((sstg, 16))
                p.o("tensor", lambda e, j=j, cc=cc, php=php, ncols=ncols: e.matmul(
                    php[:, 0:ncols], fc1s[:, cc * HID + j * 128: cc * HID + (j + 1) * 128],
                    tokst[:, cc * 512: cc * 512 + ncols],
                    start=(cc == 0), stop=(cc == DC - 1)),
                    wait=w2, inc=(pe if cc == DC - 1 else None))
            p.o("scalar", lambda e, j=j, php=php, ncols=ncols: e.activation(
                hts[:, j * 512: j * 512 + ncols], php[:, 0:ncols], AF.Gelu,
                bias=b1s[:, j:j + 1], scale=1.0),
                wait=[(pe, pe.count), (sld, sld_all), pe_prev_s2], inc=se)
            se_s1[j] = se.count
            se_gel[j % 2] = se.count
        pe_prev_s1 = (pe, pe.count)
        for i, r in enumerate(tiles):
            pyp = py[i % 2]
            for ci in range(nlsb):
                c0, cn = ci * 512, min(512, D - ci * 512)
                for j in range(HC):
                    w3 = [(se, se_s1[j])]
                    if j == 0:
                        w3.append(ve_py[i % 2])
                        w3.append((swt2, swt2_all))
                    p.o("tensor", lambda e, j=j, c0=c0, cn=cn, i=i, pyp=pyp: e.matmul(
                        pyp[:, c0:c0 + cn], hts[:, j * 512 + i * 128: j * 512 + (i + 1) * 128],
                        fc2s[:, j * D + c0: j * D + c0 + cn],
                        start=(j == 0), stop=False),
                        wait=w3, inc=None)
                p.o("tensor", lambda e, c0=c0, cn=cn, pyp=pyp: e.matmul(
                    pyp[:, c0:c0 + cn], onesb[:, :], b2r[:, c0:c0 + cn],
                    start=False, stop=True), inc=(pe if ci == nlsb - 1 else None))
            pe_fc2 = pe.count
            yb = ysb[i % 2]
            slot = (tc % 2) * 4 + i
            p.o("vector", lambda e, pyp=pyp, yb=yb: e.tensor_tensor(
                yb[:, :], pyp[:, 0:D], lsb[:, :], OP.mult),
                wait=[(pe, pe_fc2), (ve, ve_lsb), (sy[i % 2], sy[i % 2].count)], inc=ve)
            p.o("vector", lambda e, yb=yb, slot=slot: e.tensor_tensor(
                yb[:, :], yb[:, :], toks[slot][:, :], OP.add),
                wait=[(ve, ve.count)], inc=ve)
            ve_y = ve.count
            ve_epi[slot] = (ve, ve_y)
            ve_py[i % 2] = (ve, ve_y)
            p.o("sync", lambda e, r=r, yb=yb: e.dma_start(
                out=y_out[r * 128:(r + 1) * 128, :], in_=yb[:, :]),
                wait=[(ve, ve_y)], inc=sy[i % 2], k=16)
        pe_prev_s2 = (pe, pe.count)

    p.barrier("sync", (sy[0], sy[0].count), (sy[1], sy[1].count))
    p.emit(nc)
    return nc


def build_l3(cfg: Cfg):
    S, D, NT, DC = cfg.S, cfg.D, cfg.NT, cfg.DC
    N, A, CAP, NST, SLOT = cfg.N, cfg.A, cfg.CAP, cfg.NST, cfg.SLOT
    E, BIG = cfg.E, cfg.BIG
    ESLOT = E * SLOT
    nc = bass.Bass("TRN2", target_bir_lowering=False)

    x2_in = nc.dram_tensor("x2", [S, D], BF16, kind="ExternalInput").ap()
    meta_in = nc.dram_tensor("meta_all", [128, 4 * A], F32, kind="ExternalInput").ap()
    y_in = nc.dram_tensor("y_all", [ESLOT, D], BF16, kind="ExternalInput").ap()
    sel_in = nc.dram_tensor("sel", [128, NT], F32, kind="ExternalInput").ap()
    idn_in = nc.dram_tensor("idn", [128, 128], BF16, kind="ExternalInput").ap()

    out_o = nc.dram_tensor("out", [S, D], F32, kind="ExternalOutput").ap()

    ea = nc.alloc_sbuf_tensor("ea", [128, 4 * A], F32)
    selb = nc.alloc_sbuf_tensor("selb", [128, NT], F32)
    idn = nc.alloc_sbuf_tensor("idn_s", [128, 128], BF16)
    idnf = nc.alloc_sbuf_tensor("idnf", [128, 128], F32)
    rel = nc.alloc_sbuf_tensor("rel", [128, 128], F32)
    utri = nc.alloc_sbuf_tensor("utri", [128, 128], BF16)
    ltri = nc.alloc_sbuf_tensor("ltri", [128, 128], BF16)
    lful = nc.alloc_sbuf_tensor("lful", [128, 128], BF16)
    xb = nc.alloc_sbuf_tensor("xb", [128, NT * D], BF16)
    itg = nc.alloc_sbuf_tensor("itg", [128, A], BF16)
    itab = nc.alloc_sbuf_tensor("itab", [128, 128], BF16)
    scb = nc.alloc_sbuf_tensor("scb", [128, 16], BF16)
    offsl = nc.alloc_sbuf_tensor("offsl", [128, 32], F32)
    posw = nc.alloc_sbuf_tensor("posw", [128, 128], F32)
    kpw = nc.alloc_sbuf_tensor("kpw", [128, 128], F32)
    slotg = [nc.alloc_sbuf_tensor(f"slotg{k}", [128, 128], F32) for k in range(2)]
    slotT = nc.alloc_sbuf_tensor("slotT", [128, 2 * NT], F32)
    slotU = nc.alloc_sbuf_tensor("slotU", [128, 2 * NT], U32)
    keptT = nc.alloc_sbuf_tensor("keptT", [128, 2 * NT], F32)
    wT = nc.alloc_sbuf_tensor("wT", [128, 2 * NT], F32)
    gb = [nc.alloc_sbuf_tensor(f"gb{i}", [128, D], BF16) for i in range(8)]
    tsc = [nc.alloc_sbuf_tensor(f"tsc{i}", [128, D], F32) for i in range(2)]
    osb = [nc.alloc_sbuf_tensor(f"osb{i}", [128, D], F32) for i in range(2)]

    itg16 = nc.alloc_sbuf_tensor("itg16", [128, 16 * A], BF16)
    t2s = nc.alloc_sbuf_tensor("t2s", [128, 16 * 128], F32)
    itab16 = nc.alloc_sbuf_tensor("itab16", [128, 16 * 128], BF16)

    ph0 = nc.alloc_psum_tensor("ph0", [128, 128], F32)
    ph1 = nc.alloc_psum_tensor("ph1", [128, 128], F32)
    ptA = nc.alloc_psum_tensor("ptA", [128, 128], BF16)
    ptB = nc.alloc_psum_tensor("ptB", [128, 128], BF16)

    p = Plan()
    sld = p.sem("sld"); sx = p.sem("sx")
    sg = [p.sem(f"sg{i}") for i in range(8)]
    so = [p.sem("so0"), p.sem("so1")]
    pe = p.sem("pe"); ve = p.sem("ve"); se = p.sem("se"); ge = p.sem("ge")

    p.o("sync", lambda e: e.dma_start(out=ea[:, :], in_=meta_in), inc=sld, k=16)
    p.o("sync", lambda e: e.dma_start(out=selb[:, :], in_=sel_in), inc=sld, k=16)
    p.o("sync", lambda e: e.dma_start(out=idn[:, :], in_=idn_in), inc=sld, k=16)
    sld_all = sld.count
    for t in range(NT):
        p.o("sync", lambda e, t=t: e.dma_start(
            out=xb[:, t * D:(t + 1) * D], in_=x2_in[t * 128:(t + 1) * 128, :]), inc=sx, k=16)
    sx_all = sx.count
    p.o("gpsimd", lambda e: e.iota(rel[:, :], pattern=[[-1, 128]], base=0,
                                   channel_multiplier=1, allow_small_or_imprecise_dtypes=True), inc=ge)
    ge_c = ge.count
    p.o("vector", lambda e: e.tensor_single_scalar(utri[:, :], rel[:, :], 0.0, OP.is_le),
        wait=[(ge, ge_c)], inc=ve)
    p.o("vector", lambda e: e.tensor_single_scalar(ltri[:, :], rel[:, :], 0.0, OP.is_lt), inc=ve)
    p.o("vector", lambda e: e.tensor_single_scalar(idnf[:, :], rel[:, :], 0.0, OP.is_equal), inc=ve)
    p.o("vector", lambda e: e.memset(lful[:, :], 1.0), inc=ve)
    p.o("vector", lambda e: e.memset(scb[:, :], 0.0), inc=ve)
    p.o("vector", lambda e: e.memset(slotg[0][:, :], BIG), inc=ve)
    p.o("vector", lambda e: e.memset(slotg[1][:, :], BIG), inc=ve)
    ve_c = ve.count

    eview = lambda q: ea[:, :].rearrange("p (a q) -> p a q", q=4)[:, :, q:q + 1].squeeze(2)

    ve_cmp = {}
    for i in range(2 * E):
        k, ex = divmod(i, E)
        p.o("vector", lambda e, k=k, ex=ex, i=i: e.tensor_single_scalar(
            itg16[:, i * A:(i + 1) * A], eview(k), float(ex), OP.is_equal),
            wait=[(sld, sld_all), (ve, ve_c)], inc=ve)
        ve_cmp[i] = ve.count
    ve_ev = {}
    for i in range(2 * E):
        php = (ph0, ph1)[i % 2]
        ptX = (ptA, ptB)[i % 2]
        w = [(ve, ve_cmp[i])]
        if i - 2 in ve_ev:
            w.append((ve, ve_ev[i - 2]))
        p.o("tensor", lambda e, i=i, php=php: e.matmul(
            php[0:A, :], itg16[:, i * A:(i + 1) * A], utri[:, :], start=True, stop=True),
            wait=w, inc=pe)
        p.o("tensor", lambda e, i=i, ptX=ptX: e.transpose(
            ptX[0:A, :], itg16[:, i * A:(i + 1) * A], idn[:, :]), inc=pe)
        pe_i = pe.count
        p.o("vector", lambda e, i=i, php=php: e.tensor_copy(
            t2s[0:A, i * 128:(i + 1) * 128], php[0:A, :]), wait=[(pe, pe_i)], inc=ve)
        p.o("vector", lambda e, i=i, ptX=ptX: e.tensor_copy(
            itab16[0:A, i * 128:(i + 1) * 128], ptX[0:A, :]), inc=ve)
        ve_ev[i] = ve.count
    ve_t2 = ve.count
    p.o("vector", lambda e: e.tensor_copy(
        scb[:, 0:16],
        t2s[:, :].rearrange("p (i c) -> p i c", c=128)[:, :, 127:128].squeeze(2)),
        wait=[(ve, ve_t2)], inc=ve)
    ve_scb = ve.count
    p.o("tensor", lambda e: e.matmul(ph1[:, 0:16], ltri[:, :], scb[:, :], start=True, stop=True),
        wait=[(ve, ve_scb)], inc=pe)
    p.o("vector", lambda e: e.tensor_copy(offsl[:, 0:16], ph1[:, 0:16]), wait=[(pe, pe.count)], inc=ve)
    p.o("tensor", lambda e: e.matmul(ph1[:, 0:16], lful[:, :], scb[:, :], start=True, stop=True),
        wait=[(ve, ve.count)], inc=pe)
    p.o("vector", lambda e: e.tensor_copy(offsl[:, 16:32], ph1[:, 0:16]), wait=[(pe, pe.count)], inc=ve)
    ve_off = ve.count

    for k in range(2):
        for ex in range(E):
            i = k * E + ex
            t2v = t2s[:, i * 128:(i + 1) * 128]
            itv = itab16[:, i * 128:(i + 1) * 128]
            if k == 0:
                p.o("vector", lambda e, ex=ex, t2v=t2v: e.tensor_scalar(
                    posw[0:A, :], t2v[0:A, :], offsl[0:A, ex:ex + 1], -1.0, OP.add, OP.add),
                    wait=[(ve, ve_off)], inc=ve)
            else:
                p.o("vector", lambda e, ex=ex, t2v=t2v: e.tensor_scalar(
                    posw[0:A, :], t2v[0:A, :], offsl[0:A, E + ex:E + ex + 1], -1.0, OP.add, OP.add),
                    wait=[(ve, ve_off)], inc=ve)
                p.o("vector", lambda e, ex=ex: e.tensor_scalar(
                    posw[0:A, :], posw[0:A, :], offsl[0:A, 16 + ex:16 + ex + 1], None, OP.add),
                    inc=ve)
            p.o("vector", lambda e: e.tensor_single_scalar(kpw[0:A, :], posw[0:A, :],
                                                           float(CAP), OP.is_lt), inc=ve)
            p.o("vector", lambda e, itv=itv: e.tensor_tensor(kpw[0:A, :], kpw[0:A, :],
                                                             itv[0:A, :], OP.mult), inc=ve)
            p.o("vector", lambda e, ex=ex: e.tensor_scalar(
                posw[0:A, :], posw[0:A, :], float(ex * SLOT - BIG), None, OP.add), inc=ve)
            p.o("vector", lambda e: e.tensor_tensor(posw[0:A, :], posw[0:A, :], kpw[0:A, :],
                                                    OP.mult), inc=ve)
            p.o("vector", lambda e, k=k: e.tensor_tensor(slotg[k][0:A, :], slotg[k][0:A, :],
                                                         posw[0:A, :], OP.add), inc=ve)
    ve_slot = ve.count

    for k in range(2):
        p.o("tensor", lambda e, k=k: e.matmul(ph0[:, 0:NT], slotg[k][0:A, :], selb[0:A, :],
                                              start=True, stop=True),
            wait=[(ve, ve.count), (pe, pe.count)], inc=pe)
        p.o("vector", lambda e, k=k: e.tensor_copy(slotT[:, k * NT:(k + 1) * NT], ph0[:, 0:NT]),
            wait=[(pe, pe.count)], inc=ve)
        p.o("tensor", lambda e, k=k: e.transpose(ph1[0:A, :], eview(2 + k), idnf[:, :]),
            wait=[(ve, ve.count), (sld, sld_all)], inc=pe)
        pe_w = pe.count
        p.o("vector", lambda e: e.tensor_copy(posw[0:A, :], ph1[0:A, 0:128]),
            wait=[(pe, pe_w)], inc=ve)
        p.o("tensor", lambda e, k=k: e.matmul(ph0[:, 0:NT], posw[0:A, :], selb[0:A, :],
                                              start=True, stop=True),
            wait=[(ve, ve.count)], inc=pe)
        p.o("vector", lambda e, k=k: e.tensor_copy(wT[:, k * NT:(k + 1) * NT], ph0[:, 0:NT]),
            wait=[(pe, pe.count)], inc=ve)
    p.o("vector", lambda e: e.tensor_single_scalar(keptT[:, :], slotT[:, :], float(ESLOT), OP.is_lt),
        wait=[(ve, ve.count)], inc=ve)
    p.o("vector", lambda e: e.tensor_tensor(wT[:, :], wT[:, :], keptT[:, :], OP.mult),
        wait=[(ve, ve.count)], inc=ve)
    p.o("vector", lambda e: e.tensor_copy(slotU[:, :], slotT[:, :]), inc=ve)
    ve_wt = ve.count

    for bi in range(8):
        p.o("gpsimd", lambda e, bi=bi: e.memset(gb[bi][:, :], 0.0), inc=ge)
    ge_gb0 = ge.count
    ev_g = {i: None for i in range(8)}
    cons = {i: (ve, 0) for i in range(8)}
    tprev = {0: (ve, 0), 1: (ve, 0)}
    for t in range(NT):
        b1i = (t % 4) * 2
        b2i = b1i + 1
        for k, bi in ((0, b1i), (1, b2i)):
            w = [(ve, ve_wt), (ge, ge_gb0), cons[bi]]
            p.o("gpsimd", lambda e, t=t, k=k, bi=bi: e.indirect_dma_start(
                out=gb[bi][:, :], out_offset=None,
                in_=y_in,
                in_offset=bass.IndirectOffsetOnAxis(slotU[:, k * NT + t:k * NT + t + 1], axis=0),
                bounds_check=ESLOT - 1, oob_is_err=False),
                wait=w, inc=sg[bi], k=16)
            ev_g[bi] = (sg[bi], sg[bi].count)
        par = t % 2
        tb = tsc[par]
        ob = osb[par]
        p.o("scalar", lambda e, t=t, tb=tb, b1i=b1i: e.activation(
            tb[:, :], gb[b1i][:, :], AF.Copy, bias=0.0, scale=wT[:, t:t + 1]),
            wait=[ev_g[b1i], (ve, ve_wt), tprev[par], (se, se.count)], inc=se)
        se_t1 = se.count
        cons[b1i] = (se, se_t1)
        p.o("vector", lambda e, t=t, tb=tb, b2i=b2i: e.scalar_tensor_tensor(
            tb[:, :], gb[b2i][:, :], wT[:, NT + t:NT + t + 1], tb[:, :], OP.mult, OP.add),
            wait=[ev_g[b2i], (se, se_t1), (so[par], so[par].count), (ve, ve.count)], inc=ve)
        cons[b2i] = (ve, ve.count)
        p.o("vector", lambda e, t=t, tb=tb, ob=ob: e.tensor_tensor(
            ob[:, :], tb[:, :], xb[:, t * D:(t + 1) * D], OP.add),
            wait=[(sx, sx_all), (ve, ve.count)], inc=ve)
        ve_o = ve.count
        tprev[par] = (ve, ve_o)
        p.o("sync", lambda e, t=t, ob=ob: e.dma_start(
            out=out_o[t * 128:(t + 1) * 128, :], in_=ob[:, :]),
            wait=[(ve, ve_o)], inc=so[par], k=16)

    p.barrier("sync", (so[0], so[0].count), (so[1], so[1].count))
    p.emit(nc)
    return nc


_IDN = np.eye(128, dtype=np.float32).astype(ml_dtypes.bfloat16)


def kernel(L0_attn_x, L0_attn_W_kqv, L0_attn_m, L0_ffn_router_w, L0_ffn_router_b,
           L0_ffn_fc1_w, L0_ffn_fc1_b, L0_ffn_fc2_w, L0_ffn_fc2_b, L0_ffn_layer_scale):
    cfg = Cfg(S=2048, D=1024, H=16, B=8, HID=4096, E=8)
    B, NT, HC = cfg.B, cfg.NT, cfg.HC
    cores = list(range(8))

    x = np.ascontiguousarray(np.asarray(L0_attn_x, dtype=np.float32))
    wk = np.asarray(L0_attn_W_kqv, dtype=np.float32)
    m = np.ascontiguousarray(np.asarray(L0_attn_m, dtype=np.float32).reshape(cfg.H, 1))
    rw = np.asarray(L0_ffn_router_w, dtype=np.float32)
    rb = np.asarray(L0_ffn_router_b, dtype=np.float32)
    fc1 = np.asarray(L0_ffn_fc1_w, dtype=np.float32)
    fc1b = np.asarray(L0_ffn_fc1_b, dtype=np.float32)
    fc2 = np.asarray(L0_ffn_fc2_w, dtype=np.float32)
    fc2b = np.asarray(L0_ffn_fc2_b, dtype=np.float32)
    ls = np.asarray(L0_ffn_layer_scale, dtype=np.float32)

    wkt = _tile_wkqv(wk)
    rwt = np.ascontiguousarray(rw.T)
    rbr = np.ascontiguousarray(rb[None, :])

    nc1 = build_l1(cfg)
    in1 = [{"x": x[c], "wkqv_t": wkt, "m": m, "rwt": rwt, "rb": rbr, "idn": _IDN}
           for c in cores]
    r1 = run_bass_kernel_spmd(nc1, in1, core_ids=cores)
    metas = [r1.results[c]["meta"] for c in cores]
    xfs = [r1.results[c]["xfo"] for c in cores]
    x2s = [r1.results[c]["x2o"] for c in cores]

    meta_all = np.ascontiguousarray(np.concatenate(metas, axis=1))
    xf_all = np.ascontiguousarray(np.concatenate(xfs, axis=0))

    nc2 = build_l2(cfg)
    in2 = [{
        "meta_all": meta_all, "xf_all": xf_all,
        "fc1t": np.ascontiguousarray(fc1[c].T),
        "fc2t": np.ascontiguousarray(fc2[c].T),
        "b1t": np.ascontiguousarray(fc1b[c].reshape(HC, 128).T),
        "b2r": np.ascontiguousarray(fc2b[c][None, :]),
        "lsr": np.ascontiguousarray(ls[c][None, :]),
        "eid": np.full((128, 1), float(c), np.float32),
        "idn": _IDN,
    } for c in cores]
    r2 = run_bass_kernel_spmd(nc2, in2, core_ids=cores)
    y_all = np.ascontiguousarray(np.concatenate([r2.results[c]["y"] for c in cores], axis=0))

    nc3 = build_l3(cfg)
    in3 = []
    for c in cores:
        sel = np.zeros((128, NT), np.float32)
        for r in range(NT):
            sel[c * NT + r, r] = 1.0
        in3.append({"x2": x2s[c], "meta_all": meta_all, "y_all": y_all,
                    "sel": sel, "idn": _IDN})
    r3 = run_bass_kernel_spmd(nc3, in3, core_ids=cores)
    out = np.stack([r3.results[c]["out"] for c in cores], axis=0)
    return out
